# version 09
# baseline (speedup 1.0000x reference)
import os as _os
import numpy as np
import ml_dtypes
import concourse.bass as bass
import concourse.mybir as mybir
from concourse.bass_utils import run_bass_kernel_spmd
from concourse import library_config as _libs
from concourse.library_overlay import lower_extended_insts

F32 = mybir.dt.float32
BF16 = mybir.dt.bfloat16
I16 = mybir.dt.int16
U32 = mybir.dt.uint32
AF = mybir.ActivationFunctionType
ALU = mybir.AluOpType
AX = mybir.AxisListType

T = 2048
NT = 16
C = 1024
CCH = 8
H = 16
R = 64
E = 8
IFF = 4096
CCAP = 56
REGS = CCAP * 16
NT_E = REGS // 128
NCOL = E * CCAP
QB = 512
NQB = T // QB
EPS = 1e-5
NG = 16
GI = 2
NH = 2


def _consts():
    p = np.arange(128)
    idn = np.eye(128, dtype=np.float32)
    MB = (np.arange(896)[None, :] >= p[:, None] + 384).astype(np.float32)
    LM = np.zeros((128, 128), np.float32)
    for e in range(E):
        for l in range(16):
            LM[:, e * 16 + l] = (p % 16 == l)
    JB = np.zeros((128, 248), np.float32)
    JB[p, (p // 16) + 120] = 1.0
    EC = (CCAP * (p // 16)).astype(np.float32).reshape(128, 1)
    DC = (16 * np.arange(128)[None, :] + (p % 16)[:, None] + 1).astype(np.int16)
    S16 = np.zeros((128, 16), np.float32)
    S16[p, p % 16] = 1.0
    S2 = np.zeros((16, 128), np.float32)
    for l in range(16):
        for g in range(8):
            S2[l, g * 16 + l] = 1.0
    return idn, MB, LM, JB, EC, DC, S16, S2


def build_nc():
    PH = int(_os.environ.get('KPH', '9'))
    KE = int(_os.environ.get('KE', '8'))
    KSUB = int(_os.environ.get('KSUB', '9'))
    has_ld = KSUB in (3, 4, 45, 5, 6, 9)
    has_gateup = KSUB in (4, 45, 5, 6, 9)
    has_down = KSUB in (45, 5, 6, 9)
    has_ye = KSUB in (5, 6, 9)
    has_scat = KSUB in (6, 9)
    from contextlib import ExitStack
    nc = bass.Bass("TRN2", target_bir_lowering=False, detect_race_conditions=False)
    dr = lambda n, s: nc.dram_tensor(n, s, F32, kind="ExternalInput").ap()
    x_h = dr("x", [T, C])
    bw_h = dr("basis_w", [R, C])
    core_h = dr("core", [R, R])
    hr_h = dr("head_residual", [H, R, R])
    vw_h = dr("v_w", [C, C])
    ow_h = dr("o_w", [C, C])
    rg_h = dr("router_gate", [C, E])
    gw_h = dr("gate_w", [E, NG, 128, CCH, 256])
    uw_h = dr("up_w", [E, NG, 128, CCH, 256])
    dw_h = dr("down_w", [E, NH, 128, 16, C])
    gwb_h = nc.dram_tensor("gwb", [E, NG, 128, CCH, 256], BF16, kind="Internal").ap()
    uwb_h = nc.dram_tensor("uwb", [E, NG, 128, CCH, 256], BF16, kind="Internal").ap()
    dwb_h = nc.dram_tensor("dwb", [E, NH, 128, 16, C], BF16, kind="Internal").ap()
    out_h = nc.dram_tensor("out", [T + 1, C], F32, kind="ExternalOutput").ap()

    idn_c, MB_c, LM_c, JB_c, EC_c, DC_c, S16_c, S2_c = _consts()
    il = lambda a, n: nc.inline_tensor(np.ascontiguousarray(a), n).ap()
    idn_h = il(idn_c, "c_idn")
    MB_h = il(MB_c, "c_mb")
    LM_h = il(LM_c, "c_lm")
    JB_h = il(JB_c, "c_jb")
    EC_h = il(EC_c, "c_ec")
    DC_h = il(DC_c, "c_dc")
    S16_h = il(S16_c, "c_s16")
    S2_h = il(S2_c, "c_s2")

    es = ExitStack()
    sbA = lambda n, s, dt=F32: es.enter_context(nc.sbuf_tensor(n, s, dt))
    idn_sb = sbA("idn", [128, 128])
    MB_sb = sbA("mbk", [128, 896], BF16)
    LM_sb = sbA("lmk", [128, 128], BF16)
    JB_sb = sbA("jbk", [128, 248], BF16)
    EC_sb = sbA("eck", [128, 1])
    DC_sb = sbA("dck", [128, 128], I16)
    S16_sb = sbA("s16k", [128, 16])
    S2_sb = sbA("s2k", [16, 128])
    bwT_sb = sbA("bwT", [128, CCH, R], BF16)
    M_sb = sbA("msbk", [R, H, R], BF16)
    rg_sb = sbA("rgk", [128, CCH, E], BF16)
    ones_sb = sbA("onesk", [1, 64])
    onesb_sb = sbA("onesbk", [128, 1], BF16)
    stat_sb = sbA("stat", [128, NT, 6])
    x_sb = sbA("xk", [128, NT, C], BF16)
    rows_sb = sbA("rows", [128, 16, 1024], BF16)
    grows_sb = sbA("grows", [128, 17, 128], BF16)
    logit_sb = sbA("logit", [128, NT, E])
    tw_sb = sbA("twk", [128, NT, 2])
    ti_sb = sbA("tik", [128, NT, 2])
    gd_sb = sbA("gdk", [128, NT, E])
    gidx_sb = sbA("gidxk", [128, NCOL], I16)
    gcol_sb = sbA("gcolk", [128, NT_E])
    wstage_sb = sbA("wstage", [128, CCH, 1024])
    V_sb = sbA("vsbk", [128, NT, H, 65], BF16)
    fm_sb = sbA("fmk", [128, CCH, T], BF16)

    fm_flat = fm_sb[:].rearrange("p a b -> p (a b)")
    x_flat = x_sb[:].rearrange("p a b -> p (a b)")
    rows_flat = rows_sb[:].rearrange("p a b -> p (a b)")
    V_flat = V_sb[:].rearrange("p a b c -> p (a b c)")
    ws_flat = wstage_sb[:].rearrange("p a b -> p (a b)")

    _off = [0]
    def fmr(n_bf16):
        o = _off[0]
        _off[0] += n_bf16
        assert _off[0] <= 16384
        return fm_flat[:, o:o + n_bf16]
    maxv_ap = fmr(2 * NT * E).bitcast(F32).rearrange("p (a b) -> p a b", a=NT)
    Eall_ap = fmr(2 * NT * E).bitcast(F32).rearrange("p (a b) -> p a b", a=NT)
    idxu_ap = fmr(2 * NT * E).bitcast(U32).rearrange("p (a b) -> p a b", a=NT)
    zrow_ap = fmr(2 * NT).bitcast(F32).rearrange("p (a b) -> p a b", b=1)
    w12_ap = fmr(2 * NT * 4).bitcast(F32).rearrange("p (a b) -> p a b", a=NT)
    indall_ap = fmr(2 * NT * E).bitcast(F32).rearrange("p (a b) -> p a b", a=NT)
    tmp16_ap = fmr(2 * NT).bitcast(F32)
    m1_ap = [fmr(128) for _ in range(2)]
    fmst_ap = [fmr(128) for _ in range(2)]
    cum_ap = [fmr(256).bitcast(F32) for _ in range(2)]
    bsb_ap = fmr(256).bitcast(F32)
    cex_ap = fmr(256).bitcast(F32)
    idxf_ap = fmr(256).bitcast(F32)
    lsf_ap = fmr(2 * NCOL).bitcast(F32)
    c16_full = fmr(2 * NCOL).bitcast(F32)
    c16_ap = c16_full[0:16, :]
    idxi_ap = fmr(128).bitcast(I16)
    lsd_ap = fmr(NCOL).bitcast(I16)
    mski_ap = fmr(NCOL).bitcast(I16)
    sq2_view = fm_flat
    silu_ap = fm_flat[:, 10240:10240 + REGS]
    gw_ap = [fm_flat[:, 11264 + 2048 * i:11264 + 2048 * (i + 1)]
             .rearrange("p (a b) -> p a b", a=CCH) for i in range(2)]
    _moe0 = (_off[0] + 1) // 2 * 2
    silu_ap = fm_flat[:, _moe0:_moe0 + REGS]
    _g0 = _moe0 + REGS
    gw_ap = [fm_flat[:, _g0 + 2048 * i:_g0 + 2048 * (i + 1)]
             .rearrange("p (a b) -> p a b", a=CCH) for i in range(2)]
    _u0 = _g0 + 4096
    uw_ap = [fm_flat[:, _u0 + 2048 * i:_u0 + 2048 * (i + 1)]
             .rearrange("p (a b) -> p a b", a=CCH) for i in range(2)]
    assert _u0 + 4096 <= 16384, (_off[0], _moe0, _u0)
    xe_ap = [x_flat[:, 7168 * i:7168 * (i + 1)].rearrange("p (a b) -> p a b", a=CCH)
             for i in range(2)]
    ge_ap = x_flat[:, 14336:14336 + REGS].rearrange("p (a b) -> p a b", a=1)
    shuf_ap = x_flat[0:32, 15232:15232 + REGS]
    h_ap = V_flat[:, 0:16 * REGS].rearrange("p (a b) -> p a b", a=16)
    dw_ap = ws_flat.bitcast(BF16).rearrange("p (a b) -> p a b", a=16)
    sq_view = ws_flat.bitcast(BF16)
    ye_sb = sbA("yek", [128, NT_E, 1024])
    lnt_sb = [ye_sb[:, i, :] for i in range(2)]
    rec_ap = ye_sb[0:1, 2, 0:QB]
    hrV_ap = ye_sb[0:64, 2, :]
    coreV_ap = ye_sb[64:128, 2, 0:64]
    coreT_ap = ye_sb[64:128, 2, 64:128]
    wvo_ap = rows_flat[:, 4608:4608 + CCH * 1024].rearrange("p (a b) -> p a b", a=CCH)
    L_ap = rows_flat[0:64, 0:T]
    lt_ap = [rows_flat[0:64, T + QB * i:T + QB * (i + 1)] for i in range(2)]
    pt_ap = [rows_flat[:, T + 1024 + QB * i:T + 1024 + QB * (i + 1)] for i in range(6)]
    rb_ap = rows_flat.bitcast(F32)[0:64, 6400:6400 + QB]


    psA = es.enter_context(nc.psum_tensor("psA", [128, 1024], F32))
    psB = es.enter_context(nc.psum_tensor("psB", [128, 1024], F32))
    psC = es.enter_context(nc.psum_tensor("psC", [128, 512], F32))
    psD = es.enter_context(nc.psum_tensor("psD", [128, 512], F32))
    psE = es.enter_context(nc.psum_tensor("psE", [128, 512], F32))
    psF = es.enter_context(nc.psum_tensor("psF", [128, 512], F32))
    PS4 = [psA[:, 0:512], psA[:, 512:1024], psB[:, 0:512], psB[:, 512:1024]]
    AVB = [psE, psD]
    LTB = [psF, psC]

    sem = lambda n: es.enter_context(nc.semaphore(n))
    s_ld = sem("s_ld"); s_ow = sem("s_ow"); s_gx = sem("s_gx")
    s_mw = sem("s_mw"); s_md = sem("s_md"); s_gth = sem("s_gth")
    s_sct = sem("s_sct"); s_xo = sem("s_xo")
    s_pp = sem("s_pp"); s_vp = sem("s_vp"); s_vs = sem("s_vs"); s_as = sem("s_as")
    s_sq = sem("s_sq"); s_n1 = sem("s_n1"); s_t1 = sem("s_t1"); s_f1 = sem("s_f1")
    s_vl = sem("s_vl"); s_ve = sem("s_ve")
    s_sc = sem("s_sc"); s_ex = sem("s_ex"); s_mk = sem("s_mk"); s_av = sem("s_av")
    s_lt = sem("s_lt"); s_rc = sem("s_rc"); s_rb = sem("s_rb"); s_on = sem("s_on")
    s_op = sem("s_op"); s_ox = sem("s_ox")
    s_n2 = sem("s_n2"); s_t2 = sem("s_t2"); s_f2 = sem("s_f2")
    s_lg = sem("s_lg"); s_le = sem("s_le")
    s_rt = sem("s_rt"); s_rp = sem("s_rp"); s_rw = sem("s_rw"); s_ls = sem("s_ls")
    s_rg = sem("s_rg"); s_rd = sem("s_rd")
    s_ga = sem("s_ga"); s_ub = sem("s_ub"); s_si = sem("s_si"); s_hm = sem("s_hm")
    s_dn = sem("s_dn"); s_ya = sem("s_ya"); s_gc = sem("s_gc"); s_gv = sem("s_gv")
    s_sh = sem("s_sh"); s_a2 = sem("s_a2"); s_pc = sem("s_pc")

    attn_seq = []
    hq_list = []
    for h in range(H):
        for qb in range(NQB):
            hq_list.append((h, qb))
            for kt in range(4 * qb + 4):
                attn_seq.append((h, qb, kt, kt >= 4 * qb))
    N_KT = len(attn_seq)
    N_HQ = len(hq_list)
    SCX = {}
    MKX = {}
    nmk = 0
    for i, (h, qb, kt, dg) in enumerate(attn_seq):
        SCX[(h, qb, kt)] = i + 1
        if dg:
            nmk += 1
        MKX[(h, qb, kt)] = nmk
    LDN_CONSTS = 8
    LDN = 13
    NPP_PRE = 8 + 1 + 64
    NPP_OW = 64
    NVE_ALL = 32 + NQB

    block = es.enter_context(nc.Block())

    @block.sync
    def _(s):
        s.dma_start(out=idn_sb[:], in_=idn_h).then_inc(s_ld, 16)
        s.dma_start(out=lnt_sb[0][:, 0:896], in_=MB_h).then_inc(s_ld, 16)
        s.dma_start(out=lnt_sb[1][:, 0:128], in_=LM_h).then_inc(s_ld, 16)
        s.dma_start(out=lnt_sb[1][:, 128:376], in_=JB_h).then_inc(s_ld, 16)
        s.dma_start(out=EC_sb[:], in_=EC_h).then_inc(s_ld, 16)
        s.dma_start(out=DC_sb[:], in_=DC_h).then_inc(s_ld, 16)
        s.dma_start(out=S16_sb[:], in_=S16_h).then_inc(s_ld, 16)
        s.dma_start(out=S2_sb[:], in_=S2_h).then_inc(s_ld, 16)
        s.wait_ge(s_rt, 1)
        s.dma_start(out=lnt_sb[0][0:64, :], in_=bw_h).then_inc(s_ld, 16)
        s.dma_start(out=coreV_ap, in_=core_h).then_inc(s_ld, 16)
        s.dma_start(out=hrV_ap.rearrange("p (a b) -> p a b", a=16), in_=hr_h.rearrange("h r s -> r h s")).then_inc(s_ld, 16)
        s.dma_start(out=wstage_sb[:], in_=vw_h.rearrange("(a b) c -> b a c", b=128)).then_inc(s_ld, 16)
        s.dma_start(out=lnt_sb[1][:, 384:448].rearrange("p (a b) -> p a b", a=8),
                    in_=rg_h.rearrange("(a b) c -> b a c", b=128)).then_inc(s_ld, 16)
        s.wait_ge(s_rt, 2)
        s.dma_start(out=wstage_sb[:], in_=ow_h.rearrange("(a b) c -> b a c", b=128)).then_inc(s_ow, 16)
        if PH >= 3 and has_ld:
            s.wait_ge(s_rg, 1)
            s.wait_ge(s_rd, 1)
            s.wait_ge(s_vp, NPP_PRE + NPP_OW)
            for e in range(KE):
                for grp in range(NG):
                    k = e * NG + grp
                    if grp == 0:
                        s.wait_ge(s_pc, (3 * e + 2) * 16)
                    if k >= 2 and has_gateup:
                        s.wait_ge(s_ub, GI * (k - 1))
                    s.dma_start(out=gw_ap[k % 2], in_=gwb_h[e, grp]).then_inc(s_mw, 16)
                    s.dma_start(out=uw_ap[k % 2], in_=uwb_h[e, grp]).then_inc(s_mw, 16)
                    if grp == NG // 2 - 1 or grp == NG - 1:
                        hf = 0 if grp == NG // 2 - 1 else 1
                        kd = e * NH + hf
                        if grp == NG // 2 - 1:
                            s.wait_ge(s_pc, (3 * e + 3) * 16)
                        if kd >= 1 and has_down:
                            s.wait_ge(s_dn, 14 * kd)
                        s.dma_start(out=dw_ap, in_=dwb_h[e, hf]).then_inc(s_md, 16)

    @block.gpsimd
    def _(g):
        g.load_library(_libs.local_scatter)
        g.dma_start(out=x_sb[:], in_=x_h.rearrange("(a b) c -> b a c", b=128)).then_inc(s_gx, 16)
        if PH >= 3 and has_ld:
            for e in range(KE):
                g.dma_start(out=gwb_h[e], in_=gw_h[e]).then_inc(s_pc, 16)
                g.dma_start(out=uwb_h[e], in_=uw_h[e]).then_inc(s_pc, 16)
                g.dma_start(out=dwb_h[e], in_=dw_h[e]).then_inc(s_pc, 16)
        for tt in range(NT):
            g.wait_ge(s_ox, 2 * (tt + 1))
            g.dma_start(out=out_h[128 * tt:128 * (tt + 1), :], in_=x_sb[:, tt, :]).then_inc(s_xo, 16)
        if PH < 2:
            g.wait_ge(s_xo, NT * 16)
            return
        g.wait_ge(s_rt, 6)
        g.local_scatter(lsd_ap, DC_sb[:], idxi_ap, 128, NCOL, 128).then_inc(s_ls, 1)
        g.load_library(_libs.mlp)
        g.wait_ge(s_rg, 1)
        g.wait_ge(s_rd, 1)
        if PH < 3:
            return
        g.wait_ge(s_xo, NT * 16)
        def scat(e, hf):
            kd = e * NH + hf
            g.wait_ge(s_ya, 14 * (kd + 1))
            g.wait_ge(s_sct, 16 * kd)
            if kd == 0:
                g.wait_ge(s_xo, NT * 16)
            g.dma_scatter_add(out_h, ye_sb[:], gidx_sb[:, e * CCAP:(e + 1) * CCAP],
                              REGS, REGS, C, elem_step=C).then_inc(s_sct, 16)
        for e in range(KE):
            epar = e % 2
            if e >= 2 and has_gateup:
                g.wait_ge(s_ub, GI * NG * (e - 1))
            g.dma_gather(xe_ap[epar], rows_flat, mski_ap[:, e * CCAP:(e + 1) * CCAP],
                         REGS, REGS, 1024, transpose=True,
                         sbuf_tokens_per_rank=128, sbuf_free_dim_per_rank=2048,
                         sbuf_free_dim_pad_per_rank=0, sbuf_byte_offset=0).then_inc(s_gth, 16)
            if e >= 1:
                g.wait_ge(s_sh, e)
            g.dma_gather(ge_ap, grows_sb[:].rearrange("p a b -> p (a b)"),
                         gidx_sb[:, e * CCAP:(e + 1) * CCAP],
                         REGS, REGS, 128, transpose=True,
                         sbuf_tokens_per_rank=128, sbuf_free_dim_per_rank=256,
                         sbuf_free_dim_pad_per_rank=0, sbuf_byte_offset=0).then_inc(s_gth, 16)
            if has_scat and e >= 1:
                scat(e - 1, 1)
            if has_scat and has_ye:
                scat(e, 0)
        if has_scat:
            scat(KE - 1, 1)
            g.wait_ge(s_sct, KE * NH * 16)
        else:
            g.wait_ge(s_gth, KE * 32)

    @block.tensor
    def _(tn):
        tn.wait_ge(s_ld, LDN * 16)
        npp = 0
        for j in range(CCH):
            if npp >= 2:
                tn.wait_ge(s_vp, npp - 1)
            ps = psC if npp % 2 == 0 else psD
            tn.transpose(ps[:, 0:64], lnt_sb[0][0:64, 128 * j:128 * (j + 1)], idn_sb[0:64, 0:64]).then_inc(s_pp, 1)
            npp += 1
        tn.wait_ge(s_vp, npp - 1)
        ps = psC if npp % 2 == 0 else psD
        tn.transpose(ps[0:64, 0:64], coreV_ap, idn_sb[64:128, 64:128]).then_inc(s_pp, 1)
        npp += 1
        for dch in range(CCH):
            for cchi in range(CCH):
                tn.wait_ge(s_vp, npp - 1)
                ps = psC if npp % 2 == 0 else psD
                tn.transpose(ps[:, 0:128], wstage_sb[:, dch, 128 * cchi:128 * (cchi + 1)],
                             idn_sb[:, :]).then_inc(s_pp, 1)
                npp += 1
        nt1 = 0
        tn.wait_ge(s_vp, NPP_PRE)
        for tt in range(NT):
            tn.wait_ge(s_n1, tt + 1)
            for j in range(CCH):
                if nt1 >= 4:
                    tn.wait_ge(s_f1, nt1 - 3)
                ps = PS4[nt1 % 4]
                tn.transpose(ps[:, 0:128], lnt_sb[tt % 2][:, 128 * j:128 * (j + 1)],
                             idn_sb[:, :]).then_inc(s_t1, 1)
                nt1 += 1
        nvl = 0
        tn.wait_ge(s_f1, 128)
        for tt in range(NT):
            for hf in range(2):
                if nvl >= 2:
                    tn.wait_ge(s_ve, nvl - 1)
                ps = psC if nvl % 2 == 0 else psD
                for cch in range(CCH):
                    mmi = tn.matmul(ps[:, :], fm_sb[:, cch, 128 * tt:128 * (tt + 1)],
                                    wvo_ap[:, cch, 512 * hf:512 * (hf + 1)],
                                    start=(cch == 0), stop=(cch == 7))
                mmi.then_inc(s_vl, 1)
                nvl += 1
        for qb in range(NQB):
            if qb >= 1:
                tn.wait_ge(s_ve, 32 + qb)
            for cch in range(CCH):
                mmi = tn.matmul(psE[0:64, :], bwT_sb[:, cch, :],
                                fm_sb[:, cch, QB * qb:QB * (qb + 1)],
                                start=(cch == 0), stop=(cch == 7))
            mmi.then_inc(s_vl, 1)
        tn.wait_ge(s_ve, NVE_ALL)
        def emit_recb(ih2):
            tn.wait_ge(s_rc, ih2 + 1)
            tn.matmul(LTB[ih2 % 2][0:64, :], ones_sb[0:1, 0:64], rec_ap,
                      start=True, stop=True).then_inc(s_rb, 1)
        for ih, (h, qb) in enumerate(hq_list):
            if ih >= 2:
                tn.wait_ge(s_on, ih - 1)
            mmi = tn.matmul(LTB[ih % 2][0:64, :], M_sb[:, h, :], L_ap[:, QB * qb:QB * (qb + 1)],
                            start=True, stop=True)
            mmi.then_inc(s_vl, 1)
            tn.wait_ge(s_lt, ih + 1)
            nkt = 4 * qb + 4
            def emit_S(kt, h=h, qb=qb, ih=ih):
                i = SCX[(h, qb, kt)]
                if i >= 5:
                    tn.wait_ge(s_ex, i - 4)
                tn.matmul(PS4[(i - 1) % 4], L_ap[:, 128 * kt:128 * (kt + 1)],
                          lt_ap[ih % 2], start=True, stop=True).then_inc(s_sc, 1)
            def emit_AV(kt, h=h, qb=qb, ih=ih, nkt=nkt):
                i = SCX[(h, qb, kt)]
                if kt >= 4 * qb:
                    tn.wait_ge(s_mk, MKX[(h, qb, kt)])
                else:
                    tn.wait_ge(s_ex, i)
                if kt == 0 and ih >= 2:
                    tn.wait_ge(s_on, ih - 1)
                tn.matmul(AVB[ih % 2][0:65, :], V_sb[:, kt, h, :], pt_ap[(i - 1) % 6],
                          start=(kt == 0), stop=(kt == nkt - 1)).then_inc(s_av, 1)
            LA = 3
            for kt in range(min(LA, nkt)):
                emit_S(kt)
            if ih >= 1 and qb == 0:
                emit_recb(ih - 1)
            for kt in range(nkt):
                if kt + LA < nkt:
                    emit_S(kt + LA)
                emit_AV(kt)
                if kt == 0 and ih >= 1 and qb >= 1:
                    emit_recb(ih - 1)
        emit_recb(N_HQ - 1)
        tn.wait_ge(s_ow, 16)
        tn.wait_ge(s_ex, N_KT)
        tn.wait_ge(s_on, N_HQ)
        for kk in range(NPP_OW):
            if kk >= 2:
                tn.wait_ge(s_vp, NPP_PRE + kk - 1)
            ps = psC if (NPP_PRE + kk) % 2 == 0 else psD
            cchi, dch = kk // 8, kk % 8
            tn.transpose(ps[:, 0:128], wstage_sb[:, cchi, 128 * dch:128 * (dch + 1)],
                         idn_sb[:, :]).then_inc(s_pp, 1)
        tn.wait_ge(s_on, N_HQ)
        tn.wait_ge(s_vp, NPP_PRE + NPP_OW)
        nop = 0
        for tt in range(NT):
            for hf in range(2):
                if nop >= 2:
                    tn.wait_ge(s_ox, nop - 1)
                ps = psC if nop % 2 == 0 else psD
                for dch in range(CCH):
                    mmi = tn.matmul(ps[:, :], fm_sb[:, dch, 128 * tt:128 * (tt + 1)],
                                    wvo_ap[:, dch, 512 * hf:512 * (hf + 1)],
                                    start=(dch == 0), stop=(dch == 7))
                mmi.then_inc(s_op, 1)
                nop += 1
        if PH < 2:
            return
        nt2 = 0
        for tt in range(NT):
            tn.wait_ge(s_n2, tt + 1)
            if tt >= 1:
                tn.wait_ge(s_le, tt)
            for j in range(CCH):
                if nt2 >= 4:
                    tn.wait_ge(s_f2, nt2 - 3)
                ps = PS4[nt2 % 4]
                tn.transpose(ps[:, 0:128], lnt_sb[tt % 2][:, 128 * j:128 * (j + 1)],
                             idn_sb[:, :]).then_inc(s_t2, 1)
                tn.wait_ge(s_f2, nt2 + 1)
                mmi = tn.matmul(psE[:, 0:E], fmst_ap[nt2 % 2], rg_sb[:, j, :],
                                start=(j == 0), stop=(j == 7))
                mmi.then_inc(s_lg, 1)
                nt2 += 1
        tn.wait_ge(s_f2, 128)
        for cc in range(16):
            tn.wait_ge(s_rw, (cc + 1) * 8)
            tn.matmul(psC[:, 0:128], m1_ap[cc % 2], JB_sb[:, 120 - 8 * cc:248 - 8 * cc],
                      start=(cc == 0), stop=(cc == 15)).then_inc(s_rp, 1)
        tn.wait_ge(s_ls, 2)
        tn.matmul(psD[0:16, 0:NCOL], S16_sb[:, :], lsf_ap, start=True, stop=True).then_inc(s_rp, 1)
        tn.wait_ge(s_rt, 7)
        tn.matmul(psE[:, 0:NCOL], S2_sb[:, :], c16_ap, start=True, stop=True).then_inc(s_rp, 1)
        if PH < 3:
            return
        for e in range(KE):
            if KSUB < 2:
                break
            epar = e % 2
            tn.wait_ge(s_sh, e + 1)
            if e >= 1:
                tn.wait_ge(s_gv, e)
            for st in range(NT_E):
                mmg = tn.matmul(psF[:, st:st + 1], shuf_ap[0:1, 128 * st:128 * (st + 1)],
                                onesb_sb[0:1, 0:1], start=True, stop=True)
            mmg.then_inc(s_gc, 1)
            if not has_gateup:
                continue
            tn.wait_ge(s_gth, (2 * e + 1) * 16)
            for half in range(NH):
                for grp in range(NG // NH * half, NG // NH * (half + 1)):
                    k = e * NG + grp
                    tn.wait_ge(s_mw, (k + 1) * 32)
                    for il_ in range(GI):
                        ich = GI * grp + il_
                        gich = e * 32 + ich
                        if gich >= 1:
                            tn.wait_ge(s_si, gich)
                        for cch in range(CCH):
                            for sp in range(2):
                                w_ = 512 if sp == 0 else 384
                                mmi = tn.matmul(psA[:, 512 * sp:512 * sp + w_],
                                                gw_ap[k % 2][:, cch, 128 * il_:128 * (il_ + 1)],
                                                xe_ap[epar][:, cch, 512 * sp:512 * sp + w_],
                                                start=(cch == 0), stop=(cch == 7))
                        mmi.then_inc(s_ga, 1)
                        if gich >= 1:
                            tn.wait_ge(s_hm, gich)
                        for cch in range(CCH):
                            for sp in range(2):
                                w_ = 512 if sp == 0 else 384
                                mmi = tn.matmul(psB[:, 512 * sp:512 * sp + w_],
                                                uw_ap[k % 2][:, cch, 128 * il_:128 * (il_ + 1)],
                                                xe_ap[epar][:, cch, 512 * sp:512 * sp + w_],
                                                start=(cch == 0), stop=(cch == 7))
                        mmi.then_inc(s_ub, 1)
                kd = e * NH + half
                if not has_down:
                    continue
                tn.wait_ge(s_md, (kd + 1) * 16)
                tn.wait_ge(s_hm, e * 32 + 16 * (half + 1))
                if half == 0:
                    tn.wait_ge(s_gv, e + 1)
                for st in range(NT_E):
                    prev = kd * NT_E + st
                    if prev >= 2 and has_ye:
                        tn.wait_ge(s_ya, 2 * (prev - 1))
                    pA, pB = (psC, psD) if prev % 2 == 0 else (psE, psF)
                    for hf2 in range(2):
                        ps = pA if hf2 == 0 else pB
                        for ichl in range(16):
                            mmi = tn.matmul(ps[:, :], h_ap[:, ichl, 128 * st:128 * (st + 1)],
                                            dw_ap[:, ichl, 512 * hf2:512 * (hf2 + 1)],
                                            start=(ichl == 0), stop=(ichl == 15))
                        if hf2 == 0 and half == NH - 1 and st == NT_E - 1:
                            mmi.then_inc(s_a2, 1)
                    mmi.then_inc(s_dn, 2)

    @block.vector
    def _(v):
        nvs = [0]
        def seal(ins):
            nvs[0] += 1
            ins.then_inc(s_vs, 1)
            v.wait_ge(s_vs, nvs[0])
        v.memset(ones_sb[:], 1.0)
        v.memset(onesb_sb[:], 1.0)
        v.wait_ge(s_ld, LDN_CONSTS * 16)
        v.tensor_copy(MB_sb[:], lnt_sb[0][:, 0:896])
        v.tensor_copy(LM_sb[:], lnt_sb[1][:, 0:128])
        seal(v.tensor_copy(JB_sb[:], lnt_sb[1][:, 128:376]))
        v.engine_nop().then_inc(s_rt, 1)
        v.wait_ge(s_ld, LDN * 16)
        seal(v.tensor_copy(rg_sb[:], lnt_sb[1][:, 384:448].rearrange("p (a b) -> p a b", a=8)))
        nvp = 0
        for j in range(CCH):
            v.wait_ge(s_pp, nvp + 1)
            ps = psC if nvp % 2 == 0 else psD
            v.tensor_copy(bwT_sb[:, j, :], ps[:, 0:64]).then_inc(s_vp, 1)
            nvp += 1
        v.wait_ge(s_pp, nvp + 1)
        ps = psC if nvp % 2 == 0 else psD
        seal(v.tensor_copy(coreT_ap, ps[0:64, 0:64]))
        v.engine_nop().then_inc(s_vp, 1)
        nvp += 1
        hrv = hrV_ap.rearrange("p (a b) -> p a b", a=16)
        seal(v.tensor_add(lnt_sb[0][0:64, 0:64], coreV_ap, coreT_ap))
        seal(v.tensor_scalar_mul(lnt_sb[0][0:64, 0:64], lnt_sb[0][0:64, 0:64], 0.5 / (16.0 * 8.0)))
        seal(v.reduce_sum(lnt_sb[0][0:64, 64:128].rearrange("p (a b) -> p a b", b=1),
                          hrv.rearrange("p a b -> p b a"), axis=AX.X))
        seal(v.tensor_scalar_mul(lnt_sb[0][0:64, 64:128], lnt_sb[0][0:64, 64:128], 1.0 / 16.0))
        for hh in range(H):
            seal(v.tensor_sub(lnt_sb[0][0:64, 128:192], hrv[:, hh, :], lnt_sb[0][0:64, 64:128]))
            seal(v.scalar_tensor_tensor(M_sb[:, hh, :], lnt_sb[0][0:64, 128:192], 1.0 / 8.0,
                                        lnt_sb[0][0:64, 0:64], ALU.mult, ALU.add))
        for kk in range(64):
            v.wait_ge(s_pp, 9 + kk + 1)
            ps = psC if (9 + kk) % 2 == 0 else psD
            dch, cchi = kk // 8, kk % 8
            v.tensor_copy(wvo_ap[:, cchi, 128 * dch:128 * (dch + 1)], ps[:, 0:128]).then_inc(s_vp, 1)
        v.memset(V_sb[:, :, :, 64:65], 1.0)
        v.wait_ge(s_gx, 16)
        v.wait_ge(s_sq, 1)
        seal(v.reduce_sum(stat_sb[:, :, 0:1], x_sb[:], axis=AX.X))
        seal(v.reduce_sum(stat_sb[:, :, 1:2],
                          sq_view.rearrange("p (a b) -> p a b", a=NT), axis=AX.X))
        v.engine_nop().then_inc(s_rt, 1)
        seal(v.tensor_scalar_mul(stat_sb[:, :, 0:1], stat_sb[:, :, 0:1], 1.0 / float(C)))
        seal(v.tensor_scalar_mul(stat_sb[:, :, 1:2], stat_sb[:, :, 1:2], 1.0 / float(C)))
        seal(v.tensor_tensor(stat_sb[:, :, 2:3], stat_sb[:, :, 0:1], stat_sb[:, :, 0:1], ALU.mult))
        seal(v.tensor_sub(stat_sb[:, :, 2:3], stat_sb[:, :, 1:2], stat_sb[:, :, 2:3]))
        seal(v.tensor_scalar_add(stat_sb[:, :, 2:3], stat_sb[:, :, 2:3], EPS))
        seal(v.reciprocal(stat_sb[:, :, 3:4], stat_sb[:, :, 2:3]))
        v.engine_nop().then_inc(s_rt, 1)
        v.wait_ge(s_as, 1)
        nf1 = 0
        for tt in range(NT):
            if tt >= 2:
                v.wait_ge(s_t1, 8 * (tt - 1))
            seal(v.tensor_scalar(lnt_sb[tt % 2], x_sb[:, tt, :],
                                 stat_sb[:, tt, 0:1], stat_sb[:, tt, 4:5],
                                 ALU.subtract, ALU.mult))
            v.engine_nop().then_inc(s_n1, 1)
            for j in range(CCH):
                v.wait_ge(s_t1, nf1 + 1)
                ps = PS4[nf1 % 4]
                v.tensor_copy(fm_sb[:, j, 128 * tt:128 * (tt + 1)], ps[:, 0:128]).then_inc(s_f1, 1)
                nf1 += 1
        nve = 0
        for tt in range(NT):
            for hf in range(2):
                v.wait_ge(s_vl, nve + 1)
                ps = psC if nve % 2 == 0 else psD
                v.tensor_copy(V_sb[:, tt, 8 * hf:8 * (hf + 1), 0:64],
                              ps[:, :].rearrange("p (a b) -> p a b", b=64)).then_inc(s_ve, 1)
                nve += 1
        for qb in range(NQB):
            v.wait_ge(s_vl, nve + 1)
            v.tensor_copy(L_ap[:, QB * qb:QB * (qb + 1)], psE[0:64, :]).then_inc(s_ve, 1)
            nve += 1
        def vchain(j):
            h2, qb2 = hq_list[j]
            v.wait_ge(s_av, SCX[(h2, qb2, 4 * qb2 + 3)])
            seal(v.tensor_copy(rec_ap, AVB[j % 2][64:65, :]))
            v.engine_nop().then_inc(s_rc, 1)
            v.wait_ge(s_rb, j + 1)
            hcol, hpar = h2 // 2, h2 % 2
            seal(v.reciprocal(rb_ap, LTB[j % 2][0:64, :]))
            v.tensor_tensor(fm_sb[64 * hpar:64 * hpar + 64, hcol, QB * qb2:QB * (qb2 + 1)],
                            AVB[j % 2][0:64, :], rb_ap, ALU.mult).then_inc(s_on, 1)
        for ih, (h, qb) in enumerate(hq_list):
            v.wait_ge(s_vl, NVE_ALL + ih + 1)
            v.tensor_copy(lt_ap[ih % 2], LTB[ih % 2][0:64, :]).then_inc(s_lt, 1)
            if ih >= 1:
                vchain(ih - 1)
            for kt in range(4 * qb, 4 * qb + 4):
                i = SCX[(h, qb, kt)]
                d = kt - 4 * qb
                v.wait_ge(s_ex, i)
                v.tensor_tensor(pt_ap[(i - 1) % 6], pt_ap[(i - 1) % 6],
                                MB_sb[:, 384 - 128 * d:384 - 128 * d + 512],
                                ALU.mult).then_inc(s_mk, 1)
        vchain(N_HQ - 1)
        for kk in range(64):
            v.wait_ge(s_pp, NPP_PRE + kk + 1)
            ps = psC if (NPP_PRE + kk) % 2 == 0 else psD
            cchi, dch = kk // 8, kk % 8
            v.tensor_copy(wvo_ap[:, dch, 128 * cchi:128 * (cchi + 1)], ps[:, 0:128]).then_inc(s_vp, 1)
        nox = 0
        for tt in range(NT):
            for hf in range(2):
                v.wait_ge(s_op, nox + 1)
                ps = psC if nox % 2 == 0 else psD
                v.tensor_tensor(x_sb[:, tt, 512 * hf:512 * (hf + 1)], ps[:, :],
                                x_sb[:, tt, 512 * hf:512 * (hf + 1)], ALU.add).then_inc(s_ox, 1)
                nox += 1
        if PH < 2:
            return
        v.wait_ge(s_sq, 2)
        seal(v.reduce_sum(stat_sb[:, :, 0:1], x_sb[:], axis=AX.X))
        seal(v.reduce_sum(stat_sb[:, :, 1:2],
                          sq2_view.rearrange("p (a b) -> p a b", a=NT), axis=AX.X))
        seal(v.tensor_scalar_mul(stat_sb[:, :, 0:1], stat_sb[:, :, 0:1], 1.0 / float(C)))
        seal(v.tensor_scalar_mul(stat_sb[:, :, 1:2], stat_sb[:, :, 1:2], 1.0 / float(C)))
        seal(v.tensor_tensor(stat_sb[:, :, 2:3], stat_sb[:, :, 0:1], stat_sb[:, :, 0:1], ALU.mult))
        seal(v.tensor_sub(stat_sb[:, :, 2:3], stat_sb[:, :, 1:2], stat_sb[:, :, 2:3]))
        seal(v.tensor_scalar_add(stat_sb[:, :, 2:3], stat_sb[:, :, 2:3], EPS))
        seal(v.reciprocal(stat_sb[:, :, 3:4], stat_sb[:, :, 2:3]))
        v.engine_nop().then_inc(s_rt, 1)
        v.wait_ge(s_as, 2)
        nf2 = 0
        for tt in range(NT):
            if tt >= 2:
                v.wait_ge(s_t2, 8 * (tt - 1))
            seal(v.tensor_scalar(lnt_sb[tt % 2], x_sb[:, tt, :],
                                 stat_sb[:, tt, 0:1], stat_sb[:, tt, 4:5],
                                 ALU.subtract, ALU.mult))
            v.tensor_copy(rows_sb[:, tt, :], lnt_sb[tt % 2])
            v.engine_nop().then_inc(s_n2, 1)
            for j in range(CCH):
                v.wait_ge(s_t2, nf2 + 1)
                if nf2 >= 2:
                    v.wait_ge(s_lg, nf2 - 1)
                ps = PS4[nf2 % 4]
                v.tensor_copy(fmst_ap[nf2 % 2], ps[:, 0:128]).then_inc(s_f2, 1)
                nf2 += 1
            v.wait_ge(s_lg, 8 * (tt + 1))
            v.tensor_copy(logit_sb[:, tt, :], psE[:, 0:E]).then_inc(s_le, 1)
        for tt in range(NT):
            seal(v.max(out=maxv_ap[:, tt, :], in_=logit_sb[:, tt, :]))
            seal(v.max_index(idxu_ap[:, tt, :], maxv_ap[:, tt, :], logit_sb[:, tt, :]))
        seal(v.tensor_copy(ti_sb[:], idxu_ap[:, :, 0:2]))
        v.engine_nop().then_inc(s_rt, 1)
        v.wait_ge(s_as, 3)
        seal(v.reduce_sum(zrow_ap, Eall_ap, axis=AX.X))
        seal(v.tensor_tensor(w12_ap[:, :, 2:3], Eall_ap[:, :, 0:1], Eall_ap[:, :, 1:2], ALU.add))
        seal(v.scalar_tensor_tensor(w12_ap[:, :, 2:3], zrow_ap, 1e-9,
                                    w12_ap[:, :, 2:3], ALU.mult, ALU.add))
        seal(v.reciprocal(w12_ap[:, :, 3:4], w12_ap[:, :, 2:3]))
        seal(v.tensor_tensor(tw_sb[:, :, 0:1], Eall_ap[:, :, 0:1], w12_ap[:, :, 3:4], ALU.mult))
        seal(v.tensor_tensor(tw_sb[:, :, 1:2], Eall_ap[:, :, 1:2], w12_ap[:, :, 3:4], ALU.mult))
        for e in range(E):
            seal(v.tensor_single_scalar(gd_sb[:, :, e], ti_sb[:, :, 0], float(e), ALU.is_equal))
            seal(v.tensor_tensor(gd_sb[:, :, e], gd_sb[:, :, e], tw_sb[:, :, 0], ALU.mult))
            seal(v.tensor_single_scalar(tmp16_ap, ti_sb[:, :, 1], float(e), ALU.is_equal))
            seal(v.tensor_tensor(tmp16_ap, tmp16_ap, tw_sb[:, :, 1], ALU.mult))
            seal(v.tensor_tensor(gd_sb[:, :, e], gd_sb[:, :, e], tmp16_ap, ALU.add))
        seal(v.tensor_single_scalar(indall_ap, gd_sb[:], 0.0, ALU.is_gt))
        for r in range(NT):
            v.tensor_copy(grows_sb[:, r, 0:8], gd_sb[:, r, :])
        v.memset(grows_sb[:, :, 8:128], 0.0)
        v.memset(grows_sb[:, 16, :], 0.0)
        v.engine_nop().then_inc(s_rd, 1)
        for cc in range(16):
            if cc >= 2:
                v.wait_ge(s_rp, cc - 1)
            for e in range(E):
                v.tensor_scalar(m1_ap[cc % 2][:, e * 16:(e + 1) * 16],
                                LM_sb[:, e * 16:(e + 1) * 16],
                                indall_ap[:, cc, e:e + 1], None, ALU.mult).then_inc(s_rw, 1)
        v.wait_ge(s_rp, 16)
        seal(v.tensor_copy(cum_ap[0], psC[:, 0:128]))
        seal(v.tensor_copy(bsb_ap, psC[:, 0:128]))
        par = 0
        k = 1
        while k < 128:
            nxt = 1 - par
            seal(v.tensor_copy(cum_ap[nxt][:, 0:k], cum_ap[par][:, 0:k]))
            seal(v.tensor_tensor(cum_ap[nxt][:, k:128], cum_ap[par][:, k:128],
                                 cum_ap[par][:, 0:128 - k], ALU.add))
            par = nxt
            k *= 2
        seal(v.tensor_sub(cex_ap, cum_ap[par], bsb_ap))
        seal(v.tensor_single_scalar(idxf_ap, cex_ap, float(CCAP), ALU.is_lt))
        seal(v.tensor_tensor(idxf_ap, idxf_ap, bsb_ap, ALU.mult))
        seal(v.tensor_scalar(cex_ap, cex_ap, EC_sb[:, 0:1], None, ALU.add))
        seal(v.tensor_scalar_add(cex_ap, cex_ap, 1.0))
        seal(v.tensor_tensor(idxf_ap, idxf_ap, cex_ap, ALU.mult))
        seal(v.tensor_scalar_add(idxf_ap, idxf_ap, -1.0))
        seal(v.tensor_copy(idxi_ap, idxf_ap))
        v.engine_nop().then_inc(s_rt, 1)
        v.wait_ge(s_ls, 1)
        seal(v.tensor_copy(lsf_ap, lsd_ap))
        v.engine_nop().then_inc(s_ls, 1)
        v.wait_ge(s_rp, 17)
        seal(v.tensor_copy(c16_ap, psD[0:16, 0:NCOL]))
        v.engine_nop().then_inc(s_rt, 1)
        v.wait_ge(s_rp, 18)
        seal(v.tensor_scalar_add(lsf_ap, psE[:, 0:NCOL], -1.0))
        seal(v.tensor_copy(gidx_sb[:], lsf_ap))
        seal(v.tensor_single_scalar(mski_ap, gidx_sb[:], 0, ALU.is_lt))
        seal(v.scalar_tensor_tensor(gidx_sb[:], mski_ap, 2049.0, gidx_sb[:], ALU.mult, ALU.add))
        seal(v.tensor_scalar_min(mski_ap, gidx_sb[:], 2047))
        v.engine_nop().then_inc(s_rg, 1)
        if PH < 3:
            return
        for e in range(KE):
            if KSUB < 2:
                break
            v.wait_ge(s_gth, (2 * e + 2) * 16)
            if e >= 1:
                v.wait_ge(s_gc, e)
            mask = list(range(32))
            mask[0] = e
            v.stream_shuffle(shuf_ap, ge_ap[0:32, 0, :], mask).then_inc(s_sh, 1)
            v.wait_ge(s_gc, e + 1)
            seal(v.tensor_copy(gcol_sb[:, :], psF[:, 0:NT_E]))
            v.engine_nop().then_inc(s_gv, 1)
            if not has_gateup:
                continue
            for half in range(NH):
                for ichl in range(16):
                    ich = 16 * half + ichl
                    gich = e * 32 + ich
                    v.wait_ge(s_si, gich + 1)
                    v.wait_ge(s_ub, gich + 1)
                    v.tensor_tensor(h_ap[:, ichl, :], silu_ap, psB[:, 0:REGS],
                                    ALU.mult).then_inc(s_hm, 1)
                kd2 = e * NH + half
                if not has_ye:
                    continue
                for st in range(NT_E):
                    idxg = kd2 * NT_E + st
                    v.wait_ge(s_dn, 2 * (idxg + 1))
                    if st == 0 and kd2 >= 1 and has_scat:
                        v.wait_ge(s_sct, 16 * kd2)
                    pA, pB = (psC, psD) if idxg % 2 == 0 else (psE, psF)
                    v.tensor_scalar(ye_sb[:, st, 0:512], pA[:, :],
                                    gcol_sb[:, st:st + 1], None, ALU.mult).then_inc(s_ya, 1)
                    v.tensor_scalar(ye_sb[:, st, 512:1024], pB[:, :],
                                    gcol_sb[:, st:st + 1], None, ALU.mult).then_inc(s_ya, 1)

    @block.scalar
    def _(sc):
        sc.wait_ge(s_gx, 16)
        sc.wait_ge(s_vp, NPP_PRE)
        sc.activation(sq_view, x_sb[:].rearrange("p a b -> p (a b)"), AF.Square).then_inc(s_sq, 1)
        sc.wait_ge(s_rt, 3)
        sc.activation(stat_sb[:, :, 4:5], stat_sb[:, :, 3:4], AF.Sqrt).then_inc(s_as, 1)
        for i0, (h, qb, kt, dg) in enumerate(attn_seq):
            i = i0 + 1
            sc.wait_ge(s_sc, i)
            if i >= 7:
                sc.wait_ge(s_av, i - 6)
            sc.activation(pt_ap[(i - 1) % 6], PS4[(i - 1) % 4], AF.Exp).then_inc(s_ex, 1)
        if PH < 2:
            return
        sc.wait_ge(s_ox, 32)
        sc.wait_ge(s_on, N_HQ)
        sc.activation(sq2_view, x_sb[:].rearrange("p a b -> p (a b)"), AF.Square).then_inc(s_sq, 1)
        sc.wait_ge(s_rt, 4)
        sc.activation(stat_sb[:, :, 4:5], stat_sb[:, :, 3:4], AF.Sqrt).then_inc(s_as, 1)
        sc.wait_ge(s_rt, 5)
        sc.activation(Eall_ap, maxv_ap, AF.Exp).then_inc(s_as, 1)
        if PH < 3:
            return
        for e in range(KE):
            if not has_gateup:
                break
            for ich in range(32):
                gich = e * 32 + ich
                sc.wait_ge(s_ga, gich + 1)
                if gich >= 1:
                    sc.wait_ge(s_hm, gich)
                sc.activation(silu_ap, psA[:, 0:REGS], AF.Silu).then_inc(s_si, 1)

    es.close()
    lower_extended_insts(nc)
    return nc


_NC_CACHE = [None]


def kernel(L0_attn_hidden_states, L0_attn_basis_w, L0_attn_core, L0_attn_head_residual,
           L0_attn_v_w, L0_attn_v_b, L0_attn_o_w, L0_attn_o_b,
           L0_ffn_router_gate, L0_ffn_gate_w, L0_ffn_up_w, L0_ffn_down_w):
    x = np.ascontiguousarray(np.asarray(L0_attn_hidden_states, dtype=np.float32))
    if _NC_CACHE[0] is None:
        _NC_CACHE[0] = build_nc()
    nc = _NC_CACHE[0]
    w = dict(
        basis_w=np.ascontiguousarray(np.asarray(L0_attn_basis_w, dtype=np.float32)),
        core=np.ascontiguousarray(np.asarray(L0_attn_core, dtype=np.float32)),
        head_residual=np.ascontiguousarray(np.asarray(L0_attn_head_residual, dtype=np.float32)),
        v_w=np.ascontiguousarray(np.asarray(L0_attn_v_w, dtype=np.float32)),
        o_w=np.ascontiguousarray(np.asarray(L0_attn_o_w, dtype=np.float32)),
        router_gate=np.ascontiguousarray(np.asarray(L0_ffn_router_gate, dtype=np.float32)),
        gate_w=np.ascontiguousarray(np.asarray(L0_ffn_gate_w, dtype=np.float32)
                                    .reshape(8, 8, 128, 16, 256).transpose(0, 3, 2, 1, 4)),
        up_w=np.ascontiguousarray(np.asarray(L0_ffn_up_w, dtype=np.float32)
                                  .reshape(8, 8, 128, 16, 256).transpose(0, 3, 2, 1, 4)),
        down_w=np.ascontiguousarray(np.asarray(L0_ffn_down_w, dtype=np.float32)
                                    .reshape(8, 2, 16, 128, 1024).transpose(0, 1, 3, 2, 4)),
    )
    in_maps = [dict(w, x=np.ascontiguousarray(x[b])) for b in range(8)]
    res = run_bass_kernel_spmd(nc, in_maps, core_ids=list(range(8)))
    out = np.stack([np.asarray(r["out"])[:T] for r in res.results], axis=0)
    return out.astype(np.float32)


# version 10 (final)
# speedup vs baseline: 1.0099x; 1.0099x over previous
import os as _os
import numpy as np
import ml_dtypes
import concourse.bass as bass
import concourse.mybir as mybir
from concourse.bass_utils import run_bass_kernel_spmd
from concourse import library_config as _libs
from concourse.library_overlay import lower_extended_insts

F32 = mybir.dt.float32
BF16 = mybir.dt.bfloat16
I16 = mybir.dt.int16
U32 = mybir.dt.uint32
AF = mybir.ActivationFunctionType
ALU = mybir.AluOpType
AX = mybir.AxisListType

T = 2048
NT = 16
C = 1024
CCH = 8
H = 16
R = 64
E = 8
IFF = 4096
CCAP = 56
REGS = CCAP * 16
NT_E = REGS // 128
NCOL = E * CCAP
QB = 512
NQB = T // QB
EPS = 1e-5
NG = 16
GI = 2
NH = 2


def _consts():
    p = np.arange(128)
    idn = np.eye(128, dtype=np.float32)
    MB = (np.arange(896)[None, :] >= p[:, None] + 384).astype(np.float32)
    LM = np.zeros((128, 128), np.float32)
    for e in range(E):
        for l in range(16):
            LM[:, e * 16 + l] = (p % 16 == l)
    JB = np.zeros((128, 248), np.float32)
    JB[p, (p // 16) + 120] = 1.0
    EC = (CCAP * (p // 16)).astype(np.float32).reshape(128, 1)
    DC = (16 * np.arange(128)[None, :] + (p % 16)[:, None] + 1).astype(np.int16)
    S16 = np.zeros((128, 16), np.float32)
    S16[p, p % 16] = 1.0
    S2 = np.zeros((16, 128), np.float32)
    for l in range(16):
        for g in range(8):
            S2[l, g * 16 + l] = 1.0
    return idn, MB, LM, JB, EC, DC, S16, S2


def build_nc():
    PH = int(_os.environ.get('KPH', '9'))
    KE = int(_os.environ.get('KE', '8'))
    KSUB = int(_os.environ.get('KSUB', '9'))
    has_ld = KSUB in (3, 4, 45, 5, 6, 9)
    has_gateup = KSUB in (4, 45, 5, 6, 9)
    has_down = KSUB in (45, 5, 6, 9)
    has_ye = KSUB in (5, 6, 9)
    has_scat = KSUB in (6, 9)
    from contextlib import ExitStack
    nc = bass.Bass("TRN2", target_bir_lowering=False, detect_race_conditions=False)
    dr = lambda n, s: nc.dram_tensor(n, s, F32, kind="ExternalInput").ap()
    x_h = dr("x", [T, C])
    bw_h = dr("basis_w", [R, C])
    core_h = dr("core", [R, R])
    hr_h = dr("head_residual", [H, R, R])
    vw_h = dr("v_w", [C, C])
    ow_h = dr("o_w", [C, C])
    rg_h = dr("router_gate", [C, E])
    gw_h = dr("gate_w", [E, NG, 128, CCH, 256])
    uw_h = dr("up_w", [E, NG, 128, CCH, 256])
    dw_h = dr("down_w", [E, NH, 128, 16, C])
    gwb_h = nc.dram_tensor("gwb", [E, NG, 128, CCH, 256], BF16, kind="Internal").ap()
    uwb_h = nc.dram_tensor("uwb", [E, NG, 128, CCH, 256], BF16, kind="Internal").ap()
    dwb_h = nc.dram_tensor("dwb", [E, NH, 128, 16, C], BF16, kind="Internal").ap()
    out_h = nc.dram_tensor("out", [T + 1, C], F32, kind="ExternalOutput").ap()

    idn_c, MB_c, LM_c, JB_c, EC_c, DC_c, S16_c, S2_c = _consts()
    il = lambda a, n: nc.inline_tensor(np.ascontiguousarray(a), n).ap()
    idn_h = il(idn_c, "c_idn")
    MB_h = il(MB_c, "c_mb")
    LM_h = il(LM_c, "c_lm")
    JB_h = il(JB_c, "c_jb")
    EC_h = il(EC_c, "c_ec")
    DC_h = il(DC_c, "c_dc")
    S16_h = il(S16_c, "c_s16")
    S2_h = il(S2_c, "c_s2")

    es = ExitStack()
    sbA = lambda n, s, dt=F32: es.enter_context(nc.sbuf_tensor(n, s, dt))
    idn_sb = sbA("idn", [128, 128])
    MB_sb = sbA("mbk", [128, 896], BF16)
    LM_sb = sbA("lmk", [128, 128], BF16)
    JB_sb = sbA("jbk", [128, 248], BF16)
    EC_sb = sbA("eck", [128, 1])
    DC_sb = sbA("dck", [128, 128], I16)
    S16_sb = sbA("s16k", [128, 16])
    S2_sb = sbA("s2k", [16, 128])
    bwT_sb = sbA("bwT", [128, CCH, R], BF16)
    M_sb = sbA("msbk", [R, H, R], BF16)
    rg_sb = sbA("rgk", [128, CCH, E], BF16)
    ones_sb = sbA("onesk", [1, 64])
    onesb_sb = sbA("onesbk", [128, 1], BF16)
    stat_sb = sbA("stat", [128, NT, 6])
    x_sb = sbA("xk", [128, NT, C], BF16)
    rows_sb = sbA("rows", [128, 16, 1024], BF16)
    grows_sb = sbA("grows", [128, 17, 128], BF16)
    logit_sb = sbA("logit", [128, NT, E])
    tw_sb = sbA("twk", [128, NT, 2])
    ti_sb = sbA("tik", [128, NT, 2])
    gd_sb = sbA("gdk", [128, NT, E])
    gidx_sb = sbA("gidxk", [128, NCOL], I16)
    gcol_sb = sbA("gcolk", [128, NT_E])
    wstage_sb = sbA("wstage", [128, CCH, 1024])
    V_sb = sbA("vsbk", [128, NT, H, 65], BF16)
    fm_sb = sbA("fmk", [128, CCH, T], BF16)

    fm_flat = fm_sb[:].rearrange("p a b -> p (a b)")
    x_flat = x_sb[:].rearrange("p a b -> p (a b)")
    rows_flat = rows_sb[:].rearrange("p a b -> p (a b)")
    V_flat = V_sb[:].rearrange("p a b c -> p (a b c)")
    ws_flat = wstage_sb[:].rearrange("p a b -> p (a b)")

    _off = [0]
    def fmr(n_bf16):
        o = _off[0]
        _off[0] += n_bf16
        assert _off[0] <= 16384
        return fm_flat[:, o:o + n_bf16]
    maxv_ap = fmr(2 * NT * E).bitcast(F32).rearrange("p (a b) -> p a b", a=NT)
    Eall_ap = fmr(2 * NT * E).bitcast(F32).rearrange("p (a b) -> p a b", a=NT)
    idxu_ap = fmr(2 * NT * E).bitcast(U32).rearrange("p (a b) -> p a b", a=NT)
    zrow_ap = fmr(2 * NT).bitcast(F32).rearrange("p (a b) -> p a b", b=1)
    w12_ap = fmr(2 * NT * 4).bitcast(F32).rearrange("p (a b) -> p a b", a=NT)
    indall_ap = fmr(2 * NT * E).bitcast(F32).rearrange("p (a b) -> p a b", a=NT)
    tmp16_ap = fmr(2 * NT).bitcast(F32)
    m1_ap = [fmr(128) for _ in range(2)]
    fmst_ap = [fmr(128) for _ in range(2)]
    cum_ap = [fmr(256).bitcast(F32) for _ in range(2)]
    bsb_ap = fmr(256).bitcast(F32)
    cex_ap = fmr(256).bitcast(F32)
    idxf_ap = fmr(256).bitcast(F32)
    lsf_ap = fmr(2 * NCOL).bitcast(F32)
    c16_full = fmr(2 * NCOL).bitcast(F32)
    c16_ap = c16_full[0:16, :]
    idxi_ap = fmr(128).bitcast(I16)
    lsd_ap = fmr(NCOL).bitcast(I16)
    mski_ap = fmr(NCOL).bitcast(I16)
    sq2_view = fm_flat
    silu_ap = fm_flat[:, 10240:10240 + REGS]
    gw_ap = [fm_flat[:, 11264 + 2048 * i:11264 + 2048 * (i + 1)]
             .rearrange("p (a b) -> p a b", a=CCH) for i in range(2)]
    _moe0 = (_off[0] + 1) // 2 * 2
    silu_ap = fm_flat[:, _moe0:_moe0 + REGS]
    _g0 = _moe0 + REGS
    gw_ap = [fm_flat[:, _g0 + 2048 * i:_g0 + 2048 * (i + 1)]
             .rearrange("p (a b) -> p a b", a=CCH) for i in range(2)]
    _u0 = _g0 + 4096
    uw_ap = [fm_flat[:, _u0 + 2048 * i:_u0 + 2048 * (i + 1)]
             .rearrange("p (a b) -> p a b", a=CCH) for i in range(2)]
    assert _u0 + 4096 <= 16384, (_off[0], _moe0, _u0)
    xe_ap = [x_flat[:, 7168 * i:7168 * (i + 1)].rearrange("p (a b) -> p a b", a=CCH)
             for i in range(2)]
    ge_ap = x_flat[:, 14336:14336 + REGS].rearrange("p (a b) -> p a b", a=1)
    shuf_ap = x_flat[0:32, 15232:15232 + REGS]
    h_ap = V_flat[:, 0:16 * REGS].rearrange("p (a b) -> p a b", a=16)
    dw_ap = ws_flat.bitcast(BF16).rearrange("p (a b) -> p a b", a=16)
    sq_view = ws_flat.bitcast(BF16)
    ye_sb = sbA("yek", [128, NT_E, 1024])
    lnt_sb = [ye_sb[:, i, :] for i in range(2)]
    rec_ap = ye_sb[0:1, 2, 0:QB]
    hrV_ap = ye_sb[0:64, 2, :]
    coreV_ap = ye_sb[64:128, 2, 0:64]
    coreT_ap = ye_sb[64:128, 2, 64:128]
    wvo_ap = rows_flat[:, 4608:4608 + CCH * 1024].rearrange("p (a b) -> p a b", a=CCH)
    L_ap = rows_flat[0:64, 0:T]
    lt_ap = [rows_flat[0:64, T + QB * i:T + QB * (i + 1)] for i in range(2)]
    pt_ap = [rows_flat[:, T + 1024 + QB * i:T + 1024 + QB * (i + 1)] for i in range(6)]
    rb_ap = rows_flat.bitcast(F32)[0:64, 6400:6400 + QB]


    psA = es.enter_context(nc.psum_tensor("psA", [128, 1024], F32))
    psB = es.enter_context(nc.psum_tensor("psB", [128, 1024], F32))
    psC = es.enter_context(nc.psum_tensor("psC", [128, 512], F32))
    psD = es.enter_context(nc.psum_tensor("psD", [128, 512], F32))
    psE = es.enter_context(nc.psum_tensor("psE", [128, 512], F32))
    psF = es.enter_context(nc.psum_tensor("psF", [128, 512], F32))
    PS4 = [psA[:, 0:512], psA[:, 512:1024], psB[:, 0:512], psB[:, 512:1024]]
    AVB = [psE, psD]
    LTB = [psF, psC]

    sem = lambda n: es.enter_context(nc.semaphore(n))
    s_ld = sem("s_ld"); s_ow = sem("s_ow"); s_gx = sem("s_gx")
    s_mw = sem("s_mw"); s_md = sem("s_md"); s_gth = sem("s_gth")
    s_sct = sem("s_sct"); s_xo = sem("s_xo")
    s_pp = sem("s_pp"); s_vp = sem("s_vp"); s_vs = sem("s_vs"); s_as = sem("s_as")
    s_sq = sem("s_sq"); s_n1 = sem("s_n1"); s_t1 = sem("s_t1"); s_f1 = sem("s_f1")
    s_vl = sem("s_vl"); s_ve = sem("s_ve")
    s_sc = sem("s_sc"); s_ex = sem("s_ex"); s_mk = sem("s_mk"); s_av = sem("s_av")
    s_lt = sem("s_lt"); s_rc = sem("s_rc"); s_rb = sem("s_rb"); s_on = sem("s_on")
    s_op = sem("s_op"); s_ox = sem("s_ox")
    s_n2 = sem("s_n2"); s_t2 = sem("s_t2"); s_f2 = sem("s_f2")
    s_lg = sem("s_lg"); s_le = sem("s_le")
    s_rt = sem("s_rt"); s_rp = sem("s_rp"); s_rw = sem("s_rw"); s_ls = sem("s_ls")
    s_rg = sem("s_rg"); s_rd = sem("s_rd")
    s_ga = sem("s_ga"); s_ub = sem("s_ub"); s_si = sem("s_si"); s_hm = sem("s_hm")
    s_dn = sem("s_dn"); s_ya = sem("s_ya"); s_gc = sem("s_gc"); s_gv = sem("s_gv")
    s_sh = sem("s_sh"); s_a2 = sem("s_a2"); s_pc = sem("s_pc")

    attn_seq = []
    hq_list = []
    for h in range(H):
        for qb in range(NQB):
            hq_list.append((h, qb))
            for kt in range(4 * qb + 4):
                attn_seq.append((h, qb, kt, kt >= 4 * qb))
    N_KT = len(attn_seq)
    N_HQ = len(hq_list)
    SCX = {}
    MKX = {}
    nmk = 0
    for i, (h, qb, kt, dg) in enumerate(attn_seq):
        SCX[(h, qb, kt)] = i + 1
        if dg:
            nmk += 1
        MKX[(h, qb, kt)] = nmk
    LDN_CONSTS = 8
    LDN = 13
    NPP_PRE = 8 + 1 + 64
    NPP_OW = 64
    NVE_ALL = 32 + NQB

    block = es.enter_context(nc.Block())

    @block.sync
    def _(s):
        s.dma_start(out=idn_sb[:], in_=idn_h).then_inc(s_ld, 16)
        s.dma_start(out=lnt_sb[0][:, 0:896], in_=MB_h).then_inc(s_ld, 16)
        s.dma_start(out=lnt_sb[1][:, 0:128], in_=LM_h).then_inc(s_ld, 16)
        s.dma_start(out=lnt_sb[1][:, 128:376], in_=JB_h).then_inc(s_ld, 16)
        s.dma_start(out=EC_sb[:], in_=EC_h).then_inc(s_ld, 16)
        s.dma_start(out=DC_sb[:], in_=DC_h).then_inc(s_ld, 16)
        s.dma_start(out=S16_sb[:], in_=S16_h).then_inc(s_ld, 16)
        s.dma_start(out=S2_sb[:], in_=S2_h).then_inc(s_ld, 16)
        s.wait_ge(s_rt, 1)
        s.dma_start(out=lnt_sb[0][0:64, :], in_=bw_h).then_inc(s_ld, 16)
        s.dma_start(out=coreV_ap, in_=core_h).then_inc(s_ld, 16)
        s.dma_start(out=hrV_ap.rearrange("p (a b) -> p a b", a=16), in_=hr_h.rearrange("h r s -> r h s")).then_inc(s_ld, 16)
        s.dma_start(out=wstage_sb[:], in_=vw_h.rearrange("(a b) c -> b a c", b=128)).then_inc(s_ld, 16)
        s.dma_start(out=lnt_sb[1][:, 384:448].rearrange("p (a b) -> p a b", a=8),
                    in_=rg_h.rearrange("(a b) c -> b a c", b=128)).then_inc(s_ld, 16)
        s.wait_ge(s_rt, 2)
        s.dma_start(out=wstage_sb[:], in_=ow_h.rearrange("(a b) c -> b a c", b=128)).then_inc(s_ow, 16)
        if PH >= 3 and has_ld:
            s.wait_ge(s_rg, 1)
            s.wait_ge(s_rd, 1)
            s.wait_ge(s_vp, NPP_PRE + NPP_OW)
            for e in range(KE):
                for grp in range(NG):
                    k = e * NG + grp
                    if grp == 0:
                        s.wait_ge(s_pc, (3 * e + 2) * 16)
                    if k >= 2 and has_gateup:
                        s.wait_ge(s_ub, GI * (k - 1))
                    s.dma_start(out=gw_ap[k % 2], in_=gwb_h[e, grp]).then_inc(s_mw, 16)
                    s.dma_start(out=uw_ap[k % 2], in_=uwb_h[e, grp]).then_inc(s_mw, 16)
                    if grp == NG // 2 - 1 or grp == NG - 1:
                        hf = 0 if grp == NG // 2 - 1 else 1
                        kd = e * NH + hf
                        if grp == NG // 2 - 1:
                            s.wait_ge(s_pc, (3 * e + 3) * 16)
                        if kd >= 1 and has_down:
                            s.wait_ge(s_dn, 14 * kd)
                        s.dma_start(out=dw_ap, in_=dwb_h[e, hf]).then_inc(s_md, 16)

    @block.gpsimd
    def _(g):
        g.load_library(_libs.local_scatter)
        g.dma_start(out=x_sb[:], in_=x_h.rearrange("(a b) c -> b a c", b=128)).then_inc(s_gx, 16)
        if PH >= 3 and has_ld:
            g.wait_ge(s_ow, 16)
            for e in range(KE):
                g.dma_start(out=gwb_h[e], in_=gw_h[e]).then_inc(s_pc, 16)
                g.dma_start(out=uwb_h[e], in_=uw_h[e]).then_inc(s_pc, 16)
                g.dma_start(out=dwb_h[e], in_=dw_h[e]).then_inc(s_pc, 16)
        for tt in range(NT):
            g.wait_ge(s_ox, 2 * (tt + 1))
            g.dma_start(out=out_h[128 * tt:128 * (tt + 1), :], in_=x_sb[:, tt, :]).then_inc(s_xo, 16)
        if PH < 2:
            g.wait_ge(s_xo, NT * 16)
            return
        g.wait_ge(s_rt, 6)
        g.local_scatter(lsd_ap, DC_sb[:], idxi_ap, 128, NCOL, 128).then_inc(s_ls, 1)
        g.load_library(_libs.mlp)
        g.wait_ge(s_rg, 1)
        g.wait_ge(s_rd, 1)
        if PH < 3:
            return
        g.wait_ge(s_xo, NT * 16)
        def scat(e, hf):
            kd = e * NH + hf
            g.wait_ge(s_ya, 14 * (kd + 1))
            g.wait_ge(s_sct, 16 * kd)
            if kd == 0:
                g.wait_ge(s_xo, NT * 16)
            g.dma_scatter_add(out_h, ye_sb[:], gidx_sb[:, e * CCAP:(e + 1) * CCAP],
                              REGS, REGS, C, elem_step=C).then_inc(s_sct, 16)
        for e in range(KE):
            epar = e % 2
            if e >= 2 and has_gateup:
                g.wait_ge(s_ub, GI * NG * (e - 1))
            g.dma_gather(xe_ap[epar], rows_flat, mski_ap[:, e * CCAP:(e + 1) * CCAP],
                         REGS, REGS, 1024, transpose=True,
                         sbuf_tokens_per_rank=128, sbuf_free_dim_per_rank=2048,
                         sbuf_free_dim_pad_per_rank=0, sbuf_byte_offset=0).then_inc(s_gth, 16)
            if e >= 1:
                g.wait_ge(s_sh, e)
            g.dma_gather(ge_ap, grows_sb[:].rearrange("p a b -> p (a b)"),
                         gidx_sb[:, e * CCAP:(e + 1) * CCAP],
                         REGS, REGS, 128, transpose=True,
                         sbuf_tokens_per_rank=128, sbuf_free_dim_per_rank=256,
                         sbuf_free_dim_pad_per_rank=0, sbuf_byte_offset=0).then_inc(s_gth, 16)
            if has_scat and e >= 1:
                scat(e - 1, 1)
            if has_scat and has_ye:
                scat(e, 0)
        if has_scat:
            scat(KE - 1, 1)
            g.wait_ge(s_sct, KE * NH * 16)
        else:
            g.wait_ge(s_gth, KE * 32)

    @block.tensor
    def _(tn):
        tn.wait_ge(s_ld, LDN * 16)
        npp = 0
        for j in range(CCH):
            if npp >= 2:
                tn.wait_ge(s_vp, npp - 1)
            ps = psC if npp % 2 == 0 else psD
            tn.transpose(ps[:, 0:64], lnt_sb[0][0:64, 128 * j:128 * (j + 1)], idn_sb[0:64, 0:64]).then_inc(s_pp, 1)
            npp += 1
        tn.wait_ge(s_vp, npp - 1)
        ps = psC if npp % 2 == 0 else psD
        tn.transpose(ps[0:64, 0:64], coreV_ap, idn_sb[64:128, 64:128]).then_inc(s_pp, 1)
        npp += 1
        for dch in range(CCH):
            for cchi in range(CCH):
                tn.wait_ge(s_vp, npp - 1)
                ps = psC if npp % 2 == 0 else psD
                tn.transpose(ps[:, 0:128], wstage_sb[:, dch, 128 * cchi:128 * (cchi + 1)],
                             idn_sb[:, :]).then_inc(s_pp, 1)
                npp += 1
        nt1 = 0
        tn.wait_ge(s_vp, NPP_PRE)
        for tt in range(NT):
            tn.wait_ge(s_n1, tt + 1)
            for j in range(CCH):
                if nt1 >= 4:
                    tn.wait_ge(s_f1, nt1 - 3)
                ps = PS4[nt1 % 4]
                tn.transpose(ps[:, 0:128], lnt_sb[tt % 2][:, 128 * j:128 * (j + 1)],
                             idn_sb[:, :]).then_inc(s_t1, 1)
                nt1 += 1
        nvl = 0
        tn.wait_ge(s_f1, 128)
        for tt in range(NT):
            for hf in range(2):
                if nvl >= 2:
                    tn.wait_ge(s_ve, nvl - 1)
                ps = psC if nvl % 2 == 0 else psD
                for cch in range(CCH):
                    mmi = tn.matmul(ps[:, :], fm_sb[:, cch, 128 * tt:128 * (tt + 1)],
                                    wvo_ap[:, cch, 512 * hf:512 * (hf + 1)],
                                    start=(cch == 0), stop=(cch == 7))
                mmi.then_inc(s_vl, 1)
                nvl += 1
        for qb in range(NQB):
            if qb >= 1:
                tn.wait_ge(s_ve, 32 + qb)
            for cch in range(CCH):
                mmi = tn.matmul(psE[0:64, :], bwT_sb[:, cch, :],
                                fm_sb[:, cch, QB * qb:QB * (qb + 1)],
                                start=(cch == 0), stop=(cch == 7))
            mmi.then_inc(s_vl, 1)
        tn.wait_ge(s_ve, NVE_ALL)
        def emit_recb(ih2):
            tn.wait_ge(s_rc, ih2 + 1)
            tn.matmul(LTB[ih2 % 2][0:64, :], ones_sb[0:1, 0:64], rec_ap,
                      start=True, stop=True).then_inc(s_rb, 1)
        for ih, (h, qb) in enumerate(hq_list):
            if ih >= 2:
                tn.wait_ge(s_on, ih - 1)
            mmi = tn.matmul(LTB[ih % 2][0:64, :], M_sb[:, h, :], L_ap[:, QB * qb:QB * (qb + 1)],
                            start=True, stop=True)
            mmi.then_inc(s_vl, 1)
            tn.wait_ge(s_lt, ih + 1)
            nkt = 4 * qb + 4
            def emit_S(kt, h=h, qb=qb, ih=ih):
                i = SCX[(h, qb, kt)]
                if i >= 5:
                    tn.wait_ge(s_ex, i - 4)
                tn.matmul(PS4[(i - 1) % 4], L_ap[:, 128 * kt:128 * (kt + 1)],
                          lt_ap[ih % 2], start=True, stop=True).then_inc(s_sc, 1)
            def emit_AV(kt, h=h, qb=qb, ih=ih, nkt=nkt):
                i = SCX[(h, qb, kt)]
                if kt >= 4 * qb:
                    tn.wait_ge(s_mk, MKX[(h, qb, kt)])
                else:
                    tn.wait_ge(s_ex, i)
                if kt == 0 and ih >= 2:
                    tn.wait_ge(s_on, ih - 1)
                tn.matmul(AVB[ih % 2][0:65, :], V_sb[:, kt, h, :], pt_ap[(i - 1) % 6],
                          start=(kt == 0), stop=(kt == nkt - 1)).then_inc(s_av, 1)
            LA = 3
            for kt in range(min(LA, nkt)):
                emit_S(kt)
            if ih >= 1 and qb == 0:
                emit_recb(ih - 1)
            for kt in range(nkt):
                if kt + LA < nkt:
                    emit_S(kt + LA)
                emit_AV(kt)
                if kt == 0 and ih >= 1 and qb >= 1:
                    emit_recb(ih - 1)
        emit_recb(N_HQ - 1)
        tn.wait_ge(s_ow, 16)
        tn.wait_ge(s_ex, N_KT)
        tn.wait_ge(s_on, N_HQ)
        for kk in range(NPP_OW):
            if kk >= 2:
                tn.wait_ge(s_vp, NPP_PRE + kk - 1)
            ps = psC if (NPP_PRE + kk) % 2 == 0 else psD
            cchi, dch = kk // 8, kk % 8
            tn.transpose(ps[:, 0:128], wstage_sb[:, cchi, 128 * dch:128 * (dch + 1)],
                         idn_sb[:, :]).then_inc(s_pp, 1)
        tn.wait_ge(s_on, N_HQ)
        tn.wait_ge(s_vp, NPP_PRE + NPP_OW)
        nop = 0
        for tt in range(NT):
            for hf in range(2):
                if nop >= 2:
                    tn.wait_ge(s_ox, nop - 1)
                ps = psC if nop % 2 == 0 else psD
                for dch in range(CCH):
                    mmi = tn.matmul(ps[:, :], fm_sb[:, dch, 128 * tt:128 * (tt + 1)],
                                    wvo_ap[:, dch, 512 * hf:512 * (hf + 1)],
                                    start=(dch == 0), stop=(dch == 7))
                mmi.then_inc(s_op, 1)
                nop += 1
        if PH < 2:
            return
        nt2 = 0
        for tt in range(NT):
            tn.wait_ge(s_n2, tt + 1)
            if tt >= 1:
                tn.wait_ge(s_le, tt)
            for j in range(CCH):
                if nt2 >= 4:
                    tn.wait_ge(s_f2, nt2 - 3)
                ps = PS4[nt2 % 4]
                tn.transpose(ps[:, 0:128], lnt_sb[tt % 2][:, 128 * j:128 * (j + 1)],
                             idn_sb[:, :]).then_inc(s_t2, 1)
                tn.wait_ge(s_f2, nt2 + 1)
                mmi = tn.matmul(psE[:, 0:E], fmst_ap[nt2 % 2], rg_sb[:, j, :],
                                start=(j == 0), stop=(j == 7))
                mmi.then_inc(s_lg, 1)
                nt2 += 1
        tn.wait_ge(s_f2, 128)
        for cc in range(16):
            tn.wait_ge(s_rw, (cc + 1) * 8)
            tn.matmul(psC[:, 0:128], m1_ap[cc % 2], JB_sb[:, 120 - 8 * cc:248 - 8 * cc],
                      start=(cc == 0), stop=(cc == 15)).then_inc(s_rp, 1)
        tn.wait_ge(s_ls, 2)
        tn.matmul(psD[0:16, 0:NCOL], S16_sb[:, :], lsf_ap, start=True, stop=True).then_inc(s_rp, 1)
        tn.wait_ge(s_rt, 7)
        tn.matmul(psE[:, 0:NCOL], S2_sb[:, :], c16_ap, start=True, stop=True).then_inc(s_rp, 1)
        if PH < 3:
            return
        for e in range(KE):
            if KSUB < 2:
                break
            epar = e % 2
            tn.wait_ge(s_sh, e + 1)
            if e >= 1:
                tn.wait_ge(s_gv, e)
            for st in range(NT_E):
                mmg = tn.matmul(psF[:, st:st + 1], shuf_ap[0:1, 128 * st:128 * (st + 1)],
                                onesb_sb[0:1, 0:1], start=True, stop=True)
            mmg.then_inc(s_gc, 1)
            if not has_gateup:
                continue
            tn.wait_ge(s_gth, (2 * e + 1) * 16)
            for half in range(NH):
                for grp in range(NG // NH * half, NG // NH * (half + 1)):
                    k = e * NG + grp
                    tn.wait_ge(s_mw, (k + 1) * 32)
                    for il_ in range(GI):
                        ich = GI * grp + il_
                        gich = e * 32 + ich
                        if gich >= 1:
                            tn.wait_ge(s_si, gich)
                        for cch in range(CCH):
                            for sp in range(2):
                                w_ = 512 if sp == 0 else 384
                                mmi = tn.matmul(psA[:, 512 * sp:512 * sp + w_],
                                                gw_ap[k % 2][:, cch, 128 * il_:128 * (il_ + 1)],
                                                xe_ap[epar][:, cch, 512 * sp:512 * sp + w_],
                                                start=(cch == 0), stop=(cch == 7))
                        mmi.then_inc(s_ga, 1)
                        if gich >= 1:
                            tn.wait_ge(s_hm, gich)
                        for cch in range(CCH):
                            for sp in range(2):
                                w_ = 512 if sp == 0 else 384
                                mmi = tn.matmul(psB[:, 512 * sp:512 * sp + w_],
                                                uw_ap[k % 2][:, cch, 128 * il_:128 * (il_ + 1)],
                                                xe_ap[epar][:, cch, 512 * sp:512 * sp + w_],
                                                start=(cch == 0), stop=(cch == 7))
                        mmi.then_inc(s_ub, 1)
                kd = e * NH + half
                if not has_down:
                    continue
                tn.wait_ge(s_md, (kd + 1) * 16)
                tn.wait_ge(s_hm, e * 32 + 16 * (half + 1))
                if half == 0:
                    tn.wait_ge(s_gv, e + 1)
                for st in range(NT_E):
                    prev = kd * NT_E + st
                    if prev >= 2 and has_ye:
                        tn.wait_ge(s_ya, 2 * (prev - 1))
                    pA, pB = (psC, psD) if prev % 2 == 0 else (psE, psF)
                    for hf2 in range(2):
                        ps = pA if hf2 == 0 else pB
                        for ichl in range(16):
                            mmi = tn.matmul(ps[:, :], h_ap[:, ichl, 128 * st:128 * (st + 1)],
                                            dw_ap[:, ichl, 512 * hf2:512 * (hf2 + 1)],
                                            start=(ichl == 0), stop=(ichl == 15))
                        if hf2 == 0 and half == NH - 1 and st == NT_E - 1:
                            mmi.then_inc(s_a2, 1)
                    mmi.then_inc(s_dn, 2)

    @block.vector
    def _(v):
        nvs = [0]
        def seal(ins):
            nvs[0] += 1
            ins.then_inc(s_vs, 1)
            v.wait_ge(s_vs, nvs[0])
        v.memset(ones_sb[:], 1.0)
        v.memset(onesb_sb[:], 1.0)
        v.wait_ge(s_ld, LDN_CONSTS * 16)
        v.tensor_copy(MB_sb[:], lnt_sb[0][:, 0:896])
        v.tensor_copy(LM_sb[:], lnt_sb[1][:, 0:128])
        seal(v.tensor_copy(JB_sb[:], lnt_sb[1][:, 128:376]))
        v.engine_nop().then_inc(s_rt, 1)
        v.wait_ge(s_ld, LDN * 16)
        seal(v.tensor_copy(rg_sb[:], lnt_sb[1][:, 384:448].rearrange("p (a b) -> p a b", a=8)))
        nvp = 0
        for j in range(CCH):
            v.wait_ge(s_pp, nvp + 1)
            ps = psC if nvp % 2 == 0 else psD
            v.tensor_copy(bwT_sb[:, j, :], ps[:, 0:64]).then_inc(s_vp, 1)
            nvp += 1
        v.wait_ge(s_pp, nvp + 1)
        ps = psC if nvp % 2 == 0 else psD
        seal(v.tensor_copy(coreT_ap, ps[0:64, 0:64]))
        v.engine_nop().then_inc(s_vp, 1)
        nvp += 1
        hrv = hrV_ap.rearrange("p (a b) -> p a b", a=16)
        seal(v.tensor_add(lnt_sb[0][0:64, 0:64], coreV_ap, coreT_ap))
        seal(v.tensor_scalar_mul(lnt_sb[0][0:64, 0:64], lnt_sb[0][0:64, 0:64], 0.5 / (16.0 * 8.0)))
        seal(v.reduce_sum(lnt_sb[0][0:64, 64:128].rearrange("p (a b) -> p a b", b=1),
                          hrv.rearrange("p a b -> p b a"), axis=AX.X))
        seal(v.tensor_scalar_mul(lnt_sb[0][0:64, 64:128], lnt_sb[0][0:64, 64:128], 1.0 / 16.0))
        for hh in range(H):
            seal(v.tensor_sub(lnt_sb[0][0:64, 128:192], hrv[:, hh, :], lnt_sb[0][0:64, 64:128]))
            seal(v.scalar_tensor_tensor(M_sb[:, hh, :], lnt_sb[0][0:64, 128:192], 1.0 / 8.0,
                                        lnt_sb[0][0:64, 0:64], ALU.mult, ALU.add))
        for kk in range(64):
            v.wait_ge(s_pp, 9 + kk + 1)
            ps = psC if (9 + kk) % 2 == 0 else psD
            dch, cchi = kk // 8, kk % 8
            v.tensor_copy(wvo_ap[:, cchi, 128 * dch:128 * (dch + 1)], ps[:, 0:128]).then_inc(s_vp, 1)
        v.memset(V_sb[:, :, :, 64:65], 1.0)
        v.wait_ge(s_gx, 16)
        v.wait_ge(s_sq, 1)
        seal(v.reduce_sum(stat_sb[:, :, 0:1], x_sb[:], axis=AX.X))
        seal(v.reduce_sum(stat_sb[:, :, 1:2],
                          sq_view.rearrange("p (a b) -> p a b", a=NT), axis=AX.X))
        v.engine_nop().then_inc(s_rt, 1)
        seal(v.tensor_scalar_mul(stat_sb[:, :, 0:1], stat_sb[:, :, 0:1], 1.0 / float(C)))
        seal(v.tensor_scalar_mul(stat_sb[:, :, 1:2], stat_sb[:, :, 1:2], 1.0 / float(C)))
        seal(v.tensor_tensor(stat_sb[:, :, 2:3], stat_sb[:, :, 0:1], stat_sb[:, :, 0:1], ALU.mult))
        seal(v.tensor_sub(stat_sb[:, :, 2:3], stat_sb[:, :, 1:2], stat_sb[:, :, 2:3]))
        seal(v.tensor_scalar_add(stat_sb[:, :, 2:3], stat_sb[:, :, 2:3], EPS))
        seal(v.reciprocal(stat_sb[:, :, 3:4], stat_sb[:, :, 2:3]))
        v.engine_nop().then_inc(s_rt, 1)
        v.wait_ge(s_as, 1)
        nf1 = 0
        for tt in range(NT):
            if tt >= 2:
                v.wait_ge(s_t1, 8 * (tt - 1))
            seal(v.tensor_scalar(lnt_sb[tt % 2], x_sb[:, tt, :],
                                 stat_sb[:, tt, 0:1], stat_sb[:, tt, 4:5],
                                 ALU.subtract, ALU.mult))
            v.engine_nop().then_inc(s_n1, 1)
            for j in range(CCH):
                v.wait_ge(s_t1, nf1 + 1)
                ps = PS4[nf1 % 4]
                v.tensor_copy(fm_sb[:, j, 128 * tt:128 * (tt + 1)], ps[:, 0:128]).then_inc(s_f1, 1)
                nf1 += 1
        nve = 0
        for tt in range(NT):
            for hf in range(2):
                v.wait_ge(s_vl, nve + 1)
                ps = psC if nve % 2 == 0 else psD
                v.tensor_copy(V_sb[:, tt, 8 * hf:8 * (hf + 1), 0:64],
                              ps[:, :].rearrange("p (a b) -> p a b", b=64)).then_inc(s_ve, 1)
                nve += 1
        for qb in range(NQB):
            v.wait_ge(s_vl, nve + 1)
            v.tensor_copy(L_ap[:, QB * qb:QB * (qb + 1)], psE[0:64, :]).then_inc(s_ve, 1)
            nve += 1
        def vchain(j):
            h2, qb2 = hq_list[j]
            v.wait_ge(s_av, SCX[(h2, qb2, 4 * qb2 + 3)])
            seal(v.tensor_copy(rec_ap, AVB[j % 2][64:65, :]))
            v.engine_nop().then_inc(s_rc, 1)
            v.wait_ge(s_rb, j + 1)
            hcol, hpar = h2 // 2, h2 % 2
            seal(v.reciprocal(rb_ap, LTB[j % 2][0:64, :]))
            v.tensor_tensor(fm_sb[64 * hpar:64 * hpar + 64, hcol, QB * qb2:QB * (qb2 + 1)],
                            AVB[j % 2][0:64, :], rb_ap, ALU.mult).then_inc(s_on, 1)
        for ih, (h, qb) in enumerate(hq_list):
            v.wait_ge(s_vl, NVE_ALL + ih + 1)
            v.tensor_copy(lt_ap[ih % 2], LTB[ih % 2][0:64, :]).then_inc(s_lt, 1)
            if ih >= 1:
                vchain(ih - 1)
            for kt in range(4 * qb, 4 * qb + 4):
                i = SCX[(h, qb, kt)]
                d = kt - 4 * qb
                v.wait_ge(s_ex, i)
                v.tensor_tensor(pt_ap[(i - 1) % 6], pt_ap[(i - 1) % 6],
                                MB_sb[:, 384 - 128 * d:384 - 128 * d + 512],
                                ALU.mult).then_inc(s_mk, 1)
        vchain(N_HQ - 1)
        for kk in range(64):
            v.wait_ge(s_pp, NPP_PRE + kk + 1)
            ps = psC if (NPP_PRE + kk) % 2 == 0 else psD
            cchi, dch = kk // 8, kk % 8
            v.tensor_copy(wvo_ap[:, dch, 128 * cchi:128 * (cchi + 1)], ps[:, 0:128]).then_inc(s_vp, 1)
        nox = 0
        for tt in range(NT):
            for hf in range(2):
                v.wait_ge(s_op, nox + 1)
                ps = psC if nox % 2 == 0 else psD
                v.tensor_tensor(x_sb[:, tt, 512 * hf:512 * (hf + 1)], ps[:, :],
                                x_sb[:, tt, 512 * hf:512 * (hf + 1)], ALU.add).then_inc(s_ox, 1)
                nox += 1
        if PH < 2:
            return
        v.wait_ge(s_sq, 2)
        seal(v.reduce_sum(stat_sb[:, :, 0:1], x_sb[:], axis=AX.X))
        seal(v.reduce_sum(stat_sb[:, :, 1:2],
                          sq2_view.rearrange("p (a b) -> p a b", a=NT), axis=AX.X))
        seal(v.tensor_scalar_mul(stat_sb[:, :, 0:1], stat_sb[:, :, 0:1], 1.0 / float(C)))
        seal(v.tensor_scalar_mul(stat_sb[:, :, 1:2], stat_sb[:, :, 1:2], 1.0 / float(C)))
        seal(v.tensor_tensor(stat_sb[:, :, 2:3], stat_sb[:, :, 0:1], stat_sb[:, :, 0:1], ALU.mult))
        seal(v.tensor_sub(stat_sb[:, :, 2:3], stat_sb[:, :, 1:2], stat_sb[:, :, 2:3]))
        seal(v.tensor_scalar_add(stat_sb[:, :, 2:3], stat_sb[:, :, 2:3], EPS))
        seal(v.reciprocal(stat_sb[:, :, 3:4], stat_sb[:, :, 2:3]))
        v.engine_nop().then_inc(s_rt, 1)
        v.wait_ge(s_as, 2)
        nf2 = 0
        for tt in range(NT):
            if tt >= 2:
                v.wait_ge(s_t2, 8 * (tt - 1))
            seal(v.tensor_scalar(lnt_sb[tt % 2], x_sb[:, tt, :],
                                 stat_sb[:, tt, 0:1], stat_sb[:, tt, 4:5],
                                 ALU.subtract, ALU.mult))
            v.tensor_copy(rows_sb[:, tt, :], lnt_sb[tt % 2])
            v.engine_nop().then_inc(s_n2, 1)
            for j in range(CCH):
                v.wait_ge(s_t2, nf2 + 1)
                if nf2 >= 2:
                    v.wait_ge(s_lg, nf2 - 1)
                ps = PS4[nf2 % 4]
                v.tensor_copy(fmst_ap[nf2 % 2], ps[:, 0:128]).then_inc(s_f2, 1)
                nf2 += 1
            v.wait_ge(s_lg, 8 * (tt + 1))
            v.tensor_copy(logit_sb[:, tt, :], psE[:, 0:E]).then_inc(s_le, 1)
        for tt in range(NT):
            seal(v.max(out=maxv_ap[:, tt, :], in_=logit_sb[:, tt, :]))
            seal(v.max_index(idxu_ap[:, tt, :], maxv_ap[:, tt, :], logit_sb[:, tt, :]))
        seal(v.tensor_copy(ti_sb[:], idxu_ap[:, :, 0:2]))
        v.engine_nop().then_inc(s_rt, 1)
        v.wait_ge(s_as, 3)
        seal(v.reduce_sum(zrow_ap, Eall_ap, axis=AX.X))
        seal(v.tensor_tensor(w12_ap[:, :, 2:3], Eall_ap[:, :, 0:1], Eall_ap[:, :, 1:2], ALU.add))
        seal(v.scalar_tensor_tensor(w12_ap[:, :, 2:3], zrow_ap, 1e-9,
                                    w12_ap[:, :, 2:3], ALU.mult, ALU.add))
        seal(v.reciprocal(w12_ap[:, :, 3:4], w12_ap[:, :, 2:3]))
        seal(v.tensor_tensor(tw_sb[:, :, 0:1], Eall_ap[:, :, 0:1], w12_ap[:, :, 3:4], ALU.mult))
        seal(v.tensor_tensor(tw_sb[:, :, 1:2], Eall_ap[:, :, 1:2], w12_ap[:, :, 3:4], ALU.mult))
        for e in range(E):
            seal(v.tensor_single_scalar(gd_sb[:, :, e], ti_sb[:, :, 0], float(e), ALU.is_equal))
            seal(v.tensor_tensor(gd_sb[:, :, e], gd_sb[:, :, e], tw_sb[:, :, 0], ALU.mult))
            seal(v.tensor_single_scalar(tmp16_ap, ti_sb[:, :, 1], float(e), ALU.is_equal))
            seal(v.tensor_tensor(tmp16_ap, tmp16_ap, tw_sb[:, :, 1], ALU.mult))
            seal(v.tensor_tensor(gd_sb[:, :, e], gd_sb[:, :, e], tmp16_ap, ALU.add))
        seal(v.tensor_single_scalar(indall_ap, gd_sb[:], 0.0, ALU.is_gt))
        for r in range(NT):
            v.tensor_copy(grows_sb[:, r, 0:8], gd_sb[:, r, :])
        v.memset(grows_sb[:, :, 8:128], 0.0)
        v.memset(grows_sb[:, 16, :], 0.0)
        v.engine_nop().then_inc(s_rd, 1)
        for cc in range(16):
            if cc >= 2:
                v.wait_ge(s_rp, cc - 1)
            for e in range(E):
                v.tensor_scalar(m1_ap[cc % 2][:, e * 16:(e + 1) * 16],
                                LM_sb[:, e * 16:(e + 1) * 16],
                                indall_ap[:, cc, e:e + 1], None, ALU.mult).then_inc(s_rw, 1)
        v.wait_ge(s_rp, 16)
        seal(v.tensor_copy(cum_ap[0], psC[:, 0:128]))
        seal(v.tensor_copy(bsb_ap, psC[:, 0:128]))
        par = 0
        k = 1
        while k < 128:
            nxt = 1 - par
            seal(v.tensor_copy(cum_ap[nxt][:, 0:k], cum_ap[par][:, 0:k]))
            seal(v.tensor_tensor(cum_ap[nxt][:, k:128], cum_ap[par][:, k:128],
                                 cum_ap[par][:, 0:128 - k], ALU.add))
            par = nxt
            k *= 2
        seal(v.tensor_sub(cex_ap, cum_ap[par], bsb_ap))
        seal(v.tensor_single_scalar(idxf_ap, cex_ap, float(CCAP), ALU.is_lt))
        seal(v.tensor_tensor(idxf_ap, idxf_ap, bsb_ap, ALU.mult))
        seal(v.tensor_scalar(cex_ap, cex_ap, EC_sb[:, 0:1], None, ALU.add))
        seal(v.tensor_scalar_add(cex_ap, cex_ap, 1.0))
        seal(v.tensor_tensor(idxf_ap, idxf_ap, cex_ap, ALU.mult))
        seal(v.tensor_scalar_add(idxf_ap, idxf_ap, -1.0))
        seal(v.tensor_copy(idxi_ap, idxf_ap))
        v.engine_nop().then_inc(s_rt, 1)
        v.wait_ge(s_ls, 1)
        seal(v.tensor_copy(lsf_ap, lsd_ap))
        v.engine_nop().then_inc(s_ls, 1)
        v.wait_ge(s_rp, 17)
        seal(v.tensor_copy(c16_ap, psD[0:16, 0:NCOL]))
        v.engine_nop().then_inc(s_rt, 1)
        v.wait_ge(s_rp, 18)
        seal(v.tensor_scalar_add(lsf_ap, psE[:, 0:NCOL], -1.0))
        seal(v.tensor_copy(gidx_sb[:], lsf_ap))
        seal(v.tensor_single_scalar(mski_ap, gidx_sb[:], 0, ALU.is_lt))
        seal(v.scalar_tensor_tensor(gidx_sb[:], mski_ap, 2049.0, gidx_sb[:], ALU.mult, ALU.add))
        seal(v.tensor_scalar_min(mski_ap, gidx_sb[:], 2047))
        v.engine_nop().then_inc(s_rg, 1)
        if PH < 3:
            return
        for e in range(KE):
            if KSUB < 2:
                break
            v.wait_ge(s_gth, (2 * e + 2) * 16)
            if e >= 1:
                v.wait_ge(s_gc, e)
            mask = list(range(32))
            mask[0] = e
            v.stream_shuffle(shuf_ap, ge_ap[0:32, 0, :], mask).then_inc(s_sh, 1)
            v.wait_ge(s_gc, e + 1)
            seal(v.tensor_copy(gcol_sb[:, :], psF[:, 0:NT_E]))
            v.engine_nop().then_inc(s_gv, 1)
            if not has_gateup:
                continue
            for half in range(NH):
                for ichl in range(16):
                    ich = 16 * half + ichl
                    gich = e * 32 + ich
                    v.wait_ge(s_si, gich + 1)
                    v.wait_ge(s_ub, gich + 1)
                    v.tensor_tensor(h_ap[:, ichl, :], silu_ap, psB[:, 0:REGS],
                                    ALU.mult).then_inc(s_hm, 1)
                kd2 = e * NH + half
                if not has_ye:
                    continue
                for st in range(NT_E):
                    idxg = kd2 * NT_E + st
                    v.wait_ge(s_dn, 2 * (idxg + 1))
                    if st == 0 and kd2 >= 1 and has_scat:
                        v.wait_ge(s_sct, 16 * kd2)
                    pA, pB = (psC, psD) if idxg % 2 == 0 else (psE, psF)
                    v.tensor_scalar(ye_sb[:, st, 0:512], pA[:, :],
                                    gcol_sb[:, st:st + 1], None, ALU.mult).then_inc(s_ya, 1)
                    v.tensor_scalar(ye_sb[:, st, 512:1024], pB[:, :],
                                    gcol_sb[:, st:st + 1], None, ALU.mult).then_inc(s_ya, 1)

    @block.scalar
    def _(sc):
        sc.wait_ge(s_gx, 16)
        sc.wait_ge(s_vp, NPP_PRE)
        sc.activation(sq_view, x_sb[:].rearrange("p a b -> p (a b)"), AF.Square).then_inc(s_sq, 1)
        sc.wait_ge(s_rt, 3)
        sc.activation(stat_sb[:, :, 4:5], stat_sb[:, :, 3:4], AF.Sqrt).then_inc(s_as, 1)
        for i0, (h, qb, kt, dg) in enumerate(attn_seq):
            i = i0 + 1
            sc.wait_ge(s_sc, i)
            if i >= 7:
                sc.wait_ge(s_av, i - 6)
            sc.activation(pt_ap[(i - 1) % 6], PS4[(i - 1) % 4], AF.Exp).then_inc(s_ex, 1)
        if PH < 2:
            return
        sc.wait_ge(s_ox, 32)
        sc.wait_ge(s_on, N_HQ)
        sc.activation(sq2_view, x_sb[:].rearrange("p a b -> p (a b)"), AF.Square).then_inc(s_sq, 1)
        sc.wait_ge(s_rt, 4)
        sc.activation(stat_sb[:, :, 4:5], stat_sb[:, :, 3:4], AF.Sqrt).then_inc(s_as, 1)
        sc.wait_ge(s_rt, 5)
        sc.activation(Eall_ap, maxv_ap, AF.Exp).then_inc(s_as, 1)
        if PH < 3:
            return
        for e in range(KE):
            if not has_gateup:
                break
            for ich in range(32):
                gich = e * 32 + ich
                sc.wait_ge(s_ga, gich + 1)
                if gich >= 1:
                    sc.wait_ge(s_hm, gich)
                sc.activation(silu_ap, psA[:, 0:REGS], AF.Silu).then_inc(s_si, 1)

    es.close()
    lower_extended_insts(nc)
    return nc


_NC_CACHE = [None]


def kernel(L0_attn_hidden_states, L0_attn_basis_w, L0_attn_core, L0_attn_head_residual,
           L0_attn_v_w, L0_attn_v_b, L0_attn_o_w, L0_attn_o_b,
           L0_ffn_router_gate, L0_ffn_gate_w, L0_ffn_up_w, L0_ffn_down_w):
    x = np.ascontiguousarray(np.asarray(L0_attn_hidden_states, dtype=np.float32))
    if _NC_CACHE[0] is None:
        _NC_CACHE[0] = build_nc()
    nc = _NC_CACHE[0]
    w = dict(
        basis_w=np.ascontiguousarray(np.asarray(L0_attn_basis_w, dtype=np.float32)),
        core=np.ascontiguousarray(np.asarray(L0_attn_core, dtype=np.float32)),
        head_residual=np.ascontiguousarray(np.asarray(L0_attn_head_residual, dtype=np.float32)),
        v_w=np.ascontiguousarray(np.asarray(L0_attn_v_w, dtype=np.float32)),
        o_w=np.ascontiguousarray(np.asarray(L0_attn_o_w, dtype=np.float32)),
        router_gate=np.ascontiguousarray(np.asarray(L0_ffn_router_gate, dtype=np.float32)),
        gate_w=np.ascontiguousarray(np.asarray(L0_ffn_gate_w, dtype=np.float32)
                                    .reshape(8, 8, 128, 16, 256).transpose(0, 3, 2, 1, 4)),
        up_w=np.ascontiguousarray(np.asarray(L0_ffn_up_w, dtype=np.float32)
                                  .reshape(8, 8, 128, 16, 256).transpose(0, 3, 2, 1, 4)),
        down_w=np.ascontiguousarray(np.asarray(L0_ffn_down_w, dtype=np.float32)
                                    .reshape(8, 2, 16, 128, 1024).transpose(0, 1, 3, 2, 4)),
    )
    in_maps = [dict(w, x=np.ascontiguousarray(x[b])) for b in range(8)]
    res = run_bass_kernel_spmd(nc, in_maps, core_ids=list(range(8)))
    out = np.stack([np.asarray(r["out"])[:T] for r in res.results], axis=0)
    return out.astype(np.float32)


# version 11
# speedup vs baseline: 1.0713x; 1.0608x over previous
import os as _os
import numpy as np
import ml_dtypes
import concourse.bass as bass
import concourse.mybir as mybir
from concourse.bass_utils import run_bass_kernel_spmd
from concourse import library_config as _libs
from concourse.library_overlay import lower_extended_insts

F32 = mybir.dt.float32
BF16 = mybir.dt.bfloat16
I16 = mybir.dt.int16
U32 = mybir.dt.uint32
AF = mybir.ActivationFunctionType
ALU = mybir.AluOpType
AX = mybir.AxisListType

T = 2048
NT = 16
C = 1024
CCH = 8
H = 16
R = 64
E = 8
IFF = 4096
CCAP = 56
REGS = CCAP * 16
NT_E = REGS // 128
NCOL = E * CCAP
QB = 512
NQB = T // QB
EPS = 1e-5
NG = 16
GI = 2
NH = 2


def _consts():
    p = np.arange(128)
    idn = np.eye(128, dtype=np.float32)
    MB = (np.arange(896)[None, :] >= p[:, None] + 384).astype(np.float32)
    LM = np.zeros((128, 128), np.float32)
    for e in range(E):
        for l in range(16):
            LM[:, e * 16 + l] = (p % 16 == l)
    JB = np.zeros((128, 248), np.float32)
    JB[p, (p // 16) + 120] = 1.0
    EC = (CCAP * (p // 16)).astype(np.float32).reshape(128, 1)
    DC = (16 * np.arange(128)[None, :] + (p % 16)[:, None] + 1).astype(np.int16)
    S16 = np.zeros((128, 16), np.float32)
    S16[p, p % 16] = 1.0
    S2 = np.zeros((16, 128), np.float32)
    for l in range(16):
        for g in range(8):
            S2[l, g * 16 + l] = 1.0
    return idn, MB, LM, JB, EC, DC, S16, S2


def build_nc():
    PH = int(_os.environ.get('KPH', '9'))
    KE = int(_os.environ.get('KE', '8'))
    KSUB = int(_os.environ.get('KSUB', '9'))
    has_ld = KSUB in (3, 4, 45, 5, 6, 9)
    has_gateup = KSUB in (4, 45, 5, 6, 9)
    has_down = KSUB in (45, 5, 6, 9)
    has_ye = KSUB in (5, 6, 9)
    has_scat = KSUB in (6, 9)
    from contextlib import ExitStack
    nc = bass.Bass("TRN2", target_bir_lowering=False, detect_race_conditions=False)
    dr = lambda n, s: nc.dram_tensor(n, s, F32, kind="ExternalInput").ap()
    x_h = dr("x", [T, C])
    bw_h = dr("basis_w", [R, C])
    core_h = dr("core", [R, R])
    hr_h = dr("head_residual", [H, R, R])
    vw_h = dr("v_w", [C, C])
    ow_h = dr("o_w", [C, C])
    rg_h = dr("router_gate", [C, E])
    gw_h = dr("gate_w", [E, NG, 128, CCH, 256])
    uw_h = dr("up_w", [E, NG, 128, CCH, 256])
    dw_h = dr("down_w", [E, NH, 128, 16, C])
    gwb_h = nc.dram_tensor("gwb", [E, NG, 128, CCH, 256], BF16, kind="Internal").ap()
    uwb_h = nc.dram_tensor("uwb", [E, NG, 128, CCH, 256], BF16, kind="Internal").ap()
    dwb_h = nc.dram_tensor("dwb", [E, NH, 128, 16, C], BF16, kind="Internal").ap()
    out_h = nc.dram_tensor("out", [T + 1, C], F32, kind="ExternalOutput").ap()

    idn_c, MB_c, LM_c, JB_c, EC_c, DC_c, S16_c, S2_c = _consts()
    il = lambda a, n: nc.inline_tensor(np.ascontiguousarray(a), n).ap()
    idn_h = il(idn_c, "c_idn")
    MB_h = il(MB_c, "c_mb")
    LM_h = il(LM_c, "c_lm")
    JB_h = il(JB_c, "c_jb")
    EC_h = il(EC_c, "c_ec")
    DC_h = il(DC_c, "c_dc")
    S16_h = il(S16_c, "c_s16")
    S2_h = il(S2_c, "c_s2")

    es = ExitStack()
    sbA = lambda n, s, dt=F32: es.enter_context(nc.sbuf_tensor(n, s, dt))
    idn_sb = sbA("idn", [128, 128])
    MB_sb = sbA("mbk", [128, 896], BF16)
    LM_sb = sbA("lmk", [128, 128], BF16)
    JB_sb = sbA("jbk", [128, 248], BF16)
    EC_sb = sbA("eck", [128, 1])
    DC_sb = sbA("dck", [128, 128], I16)
    S16_sb = sbA("s16k", [128, 16])
    S2_sb = sbA("s2k", [16, 128])
    bwT_sb = sbA("bwT", [128, CCH, R], BF16)
    M_sb = sbA("msbk", [R, H, R], BF16)
    rg_sb = sbA("rgk", [128, CCH, E], BF16)
    ones_sb = sbA("onesk", [1, 64])
    onesb_sb = sbA("onesbk", [128, 1], BF16)
    stat_sb = sbA("stat", [128, NT, 6])
    x_sb = sbA("xk", [128, NT, C], BF16)
    rows_sb = sbA("rows", [128, 16, 1024], BF16)
    grows_sb = sbA("grows", [128, 17, 128], BF16)
    logit_sb = sbA("logit", [128, NT, E])
    tw_sb = sbA("twk", [128, NT, 2])
    ti_sb = sbA("tik", [128, NT, 2])
    gd_sb = sbA("gdk", [128, NT, E])
    gidx_sb = sbA("gidxk", [128, NCOL], I16)
    gcol_sb = sbA("gcolk", [128, NT_E])
    wstage_sb = sbA("wstage", [128, CCH, 1024])
    V_sb = sbA("vsbk", [128, NT, H, 65], BF16)
    fm_sb = sbA("fmk", [128, CCH, T], BF16)

    fm_flat = fm_sb[:].rearrange("p a b -> p (a b)")
    x_flat = x_sb[:].rearrange("p a b -> p (a b)")
    rows_flat = rows_sb[:].rearrange("p a b -> p (a b)")
    V_flat = V_sb[:].rearrange("p a b c -> p (a b c)")
    ws_flat = wstage_sb[:].rearrange("p a b -> p (a b)")

    _off = [0]
    def fmr(n_bf16):
        o = _off[0]
        _off[0] += n_bf16
        assert _off[0] <= 16384
        return fm_flat[:, o:o + n_bf16]
    maxv_ap = fmr(2 * NT * E).bitcast(F32).rearrange("p (a b) -> p a b", a=NT)
    Eall_ap = fmr(2 * NT * E).bitcast(F32).rearrange("p (a b) -> p a b", a=NT)
    idxu_ap = fmr(2 * NT * E).bitcast(U32).rearrange("p (a b) -> p a b", a=NT)
    zrow_ap = fmr(2 * NT).bitcast(F32).rearrange("p (a b) -> p a b", b=1)
    w12_ap = fmr(2 * NT * 4).bitcast(F32).rearrange("p (a b) -> p a b", a=NT)
    indall_ap = fmr(2 * NT * E).bitcast(F32).rearrange("p (a b) -> p a b", a=NT)
    tmp16_ap = fmr(2 * NT).bitcast(F32)
    m1_ap = [fmr(128) for _ in range(2)]
    fmst_ap = [fmr(128) for _ in range(4)]
    cum_ap = [fmr(256).bitcast(F32) for _ in range(2)]
    bsb_ap = fmr(256).bitcast(F32)
    cex_ap = fmr(256).bitcast(F32)
    idxf_ap = fmr(256).bitcast(F32)
    lsf_ap = fmr(2 * NCOL).bitcast(F32)
    c16_full = fmr(2 * NCOL).bitcast(F32)
    c16_ap = c16_full[0:16, :]
    idxi_ap = fmr(128).bitcast(I16)
    lsd_ap = fmr(NCOL).bitcast(I16)
    mski_ap = fmr(NCOL).bitcast(I16)
    sq2_view = fm_flat
    silu_ap = fm_flat[:, 10240:10240 + REGS]
    gw_ap = [fm_flat[:, 11264 + 2048 * i:11264 + 2048 * (i + 1)]
             .rearrange("p (a b) -> p a b", a=CCH) for i in range(2)]
    _moe0 = (_off[0] + 1) // 2 * 2
    silu_ap = [fm_flat[:, _moe0 + REGS * i:_moe0 + REGS * (i + 1)] for i in range(2)]
    _g0 = _moe0 + 2 * REGS
    gw_ap = [fm_flat[:, _g0 + 2048 * i:_g0 + 2048 * (i + 1)]
             .rearrange("p (a b) -> p a b", a=CCH) for i in range(2)]
    _u0 = _g0 + 4096
    uw_ap = [fm_flat[:, _u0 + 2048 * i:_u0 + 2048 * (i + 1)]
             .rearrange("p (a b) -> p a b", a=CCH) for i in range(2)]
    assert _u0 + 4096 <= 16384, (_off[0], _moe0, _u0)
    xe_ap = [x_flat[:, 7168 * i:7168 * (i + 1)].rearrange("p (a b) -> p a b", a=CCH)
             for i in range(2)]
    ge_ap = x_flat[:, 14336:14336 + REGS].rearrange("p (a b) -> p a b", a=1)
    shuf_ap = x_flat[0:32, 15232:15232 + REGS]
    h_ap = V_flat[:, 0:16 * REGS].rearrange("p (a b) -> p a b", a=16)
    dw_ap = ws_flat.bitcast(BF16).rearrange("p (a b) -> p a b", a=16)
    sq_view = ws_flat.bitcast(BF16)
    ye_sb = sbA("yek", [128, NT_E, 1024])
    lnt_sb = [ye_sb[:, i, :] for i in range(2)]
    rec_ap = ye_sb[0:1, 2, 0:QB]
    hrV_ap = ye_sb[0:64, 2, :]
    coreV_ap = ye_sb[64:128, 2, 0:64]
    coreT_ap = ye_sb[64:128, 2, 64:128]
    wvo_ap = rows_flat[:, 4608:4608 + CCH * 1024].rearrange("p (a b) -> p a b", a=CCH)
    L_ap = rows_flat[0:64, 0:T]
    lt_ap = [rows_flat[0:64, T + QB * i:T + QB * (i + 1)] for i in range(2)]
    pt_ap = [rows_flat[:, T + 1024 + QB * i:T + 1024 + QB * (i + 1)] for i in range(6)]
    rb_ap = rows_flat.bitcast(F32)[0:64, 6400:6400 + QB]


    psA = es.enter_context(nc.psum_tensor("psA", [128, 1024], F32))
    psB = es.enter_context(nc.psum_tensor("psB", [128, 1024], F32))
    psC = es.enter_context(nc.psum_tensor("psC", [128, 512], F32))
    psD = es.enter_context(nc.psum_tensor("psD", [128, 512], F32))
    psE = es.enter_context(nc.psum_tensor("psE", [128, 512], F32))
    psF = es.enter_context(nc.psum_tensor("psF", [128, 512], F32))
    PS4 = [psA[:, 0:512], psA[:, 512:1024], psB[:, 0:512], psB[:, 512:1024]]
    AVB = [psE, psD]
    LTB = [psF, psC]

    sem = lambda n: es.enter_context(nc.semaphore(n))
    s_ld = sem("s_ld"); s_ow = sem("s_ow"); s_gx = sem("s_gx")
    s_mw = sem("s_mw"); s_md = sem("s_md"); s_gth = sem("s_gth")
    s_sct = sem("s_sct"); s_xo = sem("s_xo")
    s_pp = sem("s_pp"); s_vp = sem("s_vp"); s_vs = sem("s_vs"); s_as = sem("s_as")
    s_sq = sem("s_sq"); s_n1 = sem("s_n1"); s_t1 = sem("s_t1"); s_f1 = sem("s_f1")
    s_vl = sem("s_vl"); s_ve = sem("s_ve")
    s_sc = sem("s_sc"); s_ex = sem("s_ex"); s_mk = sem("s_mk"); s_av = sem("s_av")
    s_lt = sem("s_lt"); s_rc = sem("s_rc"); s_rb = sem("s_rb"); s_on = sem("s_on")
    s_op = sem("s_op"); s_ox = sem("s_ox")
    s_n2 = sem("s_n2"); s_t2 = sem("s_t2"); s_f2 = sem("s_f2")
    s_lg = sem("s_lg"); s_le = sem("s_le")
    s_rt = sem("s_rt"); s_rp = sem("s_rp"); s_rw = sem("s_rw"); s_ls = sem("s_ls")
    s_rg = sem("s_rg"); s_rd = sem("s_rd")
    s_ga = sem("s_ga"); s_ub = sem("s_ub"); s_si = sem("s_si"); s_hm = sem("s_hm")
    s_dn = sem("s_dn"); s_ya = sem("s_ya"); s_gc = sem("s_gc"); s_gv = sem("s_gv")
    s_sh = sem("s_sh"); s_a2 = sem("s_a2"); s_pc = sem("s_pc")

    attn_seq = []
    hq_list = []
    for h in range(H):
        for qb in range(NQB):
            hq_list.append((h, qb))
            for kt in range(4 * qb + 4):
                attn_seq.append((h, qb, kt, kt >= 4 * qb))
    N_KT = len(attn_seq)
    N_HQ = len(hq_list)
    SCX = {}
    MKX = {}
    nmk = 0
    for i, (h, qb, kt, dg) in enumerate(attn_seq):
        SCX[(h, qb, kt)] = i + 1
        if dg:
            nmk += 1
        MKX[(h, qb, kt)] = nmk
    LDN_CONSTS = 8
    LDN = 13
    NPP_PRE = 8 + 1 + 64
    NPP_OW = 64
    NVE_ALL = 32 + NQB

    block = es.enter_context(nc.Block())

    @block.sync
    def _(s):
        s.dma_start(out=idn_sb[:], in_=idn_h).then_inc(s_ld, 16)
        s.dma_start(out=lnt_sb[0][:, 0:896], in_=MB_h).then_inc(s_ld, 16)
        s.dma_start(out=lnt_sb[1][:, 0:128], in_=LM_h).then_inc(s_ld, 16)
        s.dma_start(out=lnt_sb[1][:, 128:376], in_=JB_h).then_inc(s_ld, 16)
        s.dma_start(out=EC_sb[:], in_=EC_h).then_inc(s_ld, 16)
        s.dma_start(out=DC_sb[:], in_=DC_h).then_inc(s_ld, 16)
        s.dma_start(out=S16_sb[:], in_=S16_h).then_inc(s_ld, 16)
        s.dma_start(out=S2_sb[:], in_=S2_h).then_inc(s_ld, 16)
        s.wait_ge(s_rt, 1)
        s.dma_start(out=lnt_sb[0][0:64, :], in_=bw_h).then_inc(s_ld, 16)
        s.dma_start(out=coreV_ap, in_=core_h).then_inc(s_ld, 16)
        s.dma_start(out=hrV_ap.rearrange("p (a b) -> p a b", a=16), in_=hr_h.rearrange("h r s -> r h s")).then_inc(s_ld, 16)
        s.dma_start(out=wstage_sb[:], in_=vw_h.rearrange("(a b) c -> b a c", b=128)).then_inc(s_ld, 16)
        s.dma_start(out=lnt_sb[1][:, 384:448].rearrange("p (a b) -> p a b", a=8),
                    in_=rg_h.rearrange("(a b) c -> b a c", b=128)).then_inc(s_ld, 16)
        s.wait_ge(s_rt, 2)
        s.dma_start(out=wstage_sb[:], in_=ow_h.rearrange("(a b) c -> b a c", b=128)).then_inc(s_ow, 16)
        if PH >= 3 and has_ld:
            s.wait_ge(s_rg, 1)
            s.wait_ge(s_rd, 1)
            s.wait_ge(s_vp, NPP_PRE + NPP_OW)
            for e in range(KE):
                for grp in range(NG):
                    k = e * NG + grp
                    if grp == 0:
                        s.wait_ge(s_pc, (3 * e + 2) * 16)
                    if k >= 2 and has_gateup:
                        s.wait_ge(s_ub, GI * (k - 1))
                    s.dma_start(out=gw_ap[k % 2], in_=gwb_h[e, grp]).then_inc(s_mw, 16)
                    s.dma_start(out=uw_ap[k % 2], in_=uwb_h[e, grp]).then_inc(s_mw, 16)
                    if grp == NG // 2 - 1 or grp == NG - 1:
                        hf = 0 if grp == NG // 2 - 1 else 1
                        kd = e * NH + hf
                        if grp == NG // 2 - 1:
                            s.wait_ge(s_pc, (3 * e + 3) * 16)
                        if kd >= 1 and has_down:
                            s.wait_ge(s_dn, 14 * kd)
                        s.dma_start(out=dw_ap, in_=dwb_h[e, hf]).then_inc(s_md, 16)

    @block.gpsimd
    def _(g):
        g.load_library(_libs.local_scatter)
        g.dma_start(out=x_sb[:], in_=x_h.rearrange("(a b) c -> b a c", b=128)).then_inc(s_gx, 16)
        if PH >= 3 and has_ld:
            g.wait_ge(s_ow, 16)
            for e in range(KE):
                g.dma_start(out=gwb_h[e], in_=gw_h[e]).then_inc(s_pc, 16)
                g.dma_start(out=uwb_h[e], in_=uw_h[e]).then_inc(s_pc, 16)
                g.dma_start(out=dwb_h[e], in_=dw_h[e]).then_inc(s_pc, 16)
        for tt in range(NT):
            g.wait_ge(s_ox, 2 * (tt + 1))
            g.dma_start(out=out_h[128 * tt:128 * (tt + 1), :], in_=x_sb[:, tt, :]).then_inc(s_xo, 16)
        if PH < 2:
            g.wait_ge(s_xo, NT * 16)
            return
        g.wait_ge(s_rt, 6)
        g.local_scatter(lsd_ap, DC_sb[:], idxi_ap, 128, NCOL, 128).then_inc(s_ls, 1)
        g.load_library(_libs.mlp)
        g.wait_ge(s_rg, 1)
        g.wait_ge(s_rd, 1)
        if PH < 3:
            return
        g.wait_ge(s_xo, NT * 16)
        def scat(e, hf):
            kd = e * NH + hf
            g.wait_ge(s_ya, 14 * (kd + 1))
            g.wait_ge(s_sct, 16 * kd)
            if kd == 0:
                g.wait_ge(s_xo, NT * 16)
            g.dma_scatter_add(out_h, ye_sb[:], gidx_sb[:, e * CCAP:(e + 1) * CCAP],
                              REGS, REGS, C, elem_step=C).then_inc(s_sct, 16)
        for e in range(KE):
            epar = e % 2
            if e >= 2 and has_gateup:
                g.wait_ge(s_ub, GI * NG * (e - 1))
            g.dma_gather(xe_ap[epar], rows_flat, mski_ap[:, e * CCAP:(e + 1) * CCAP],
                         REGS, REGS, 1024, transpose=True,
                         sbuf_tokens_per_rank=128, sbuf_free_dim_per_rank=2048,
                         sbuf_free_dim_pad_per_rank=0, sbuf_byte_offset=0).then_inc(s_gth, 16)
            if e >= 1:
                g.wait_ge(s_sh, e)
            g.dma_gather(ge_ap, grows_sb[:].rearrange("p a b -> p (a b)"),
                         gidx_sb[:, e * CCAP:(e + 1) * CCAP],
                         REGS, REGS, 128, transpose=True,
                         sbuf_tokens_per_rank=128, sbuf_free_dim_per_rank=256,
                         sbuf_free_dim_pad_per_rank=0, sbuf_byte_offset=0).then_inc(s_gth, 16)
            if has_scat and e >= 1:
                scat(e - 1, 1)
            if has_scat and has_ye:
                scat(e, 0)
        if has_scat:
            scat(KE - 1, 1)
            g.wait_ge(s_sct, KE * NH * 16)
        else:
            g.wait_ge(s_gth, KE * 32)

    @block.tensor
    def _(tn):
        tn.wait_ge(s_ld, LDN * 16)
        npp = 0
        for j in range(CCH):
            if npp >= 2:
                tn.wait_ge(s_vp, npp - 1)
            ps = psC if npp % 2 == 0 else psD
            tn.transpose(ps[:, 0:64], lnt_sb[0][0:64, 128 * j:128 * (j + 1)], idn_sb[0:64, 0:64]).then_inc(s_pp, 1)
            npp += 1
        tn.wait_ge(s_vp, npp - 1)
        ps = psC if npp % 2 == 0 else psD
        tn.transpose(ps[0:64, 0:64], coreV_ap, idn_sb[64:128, 64:128]).then_inc(s_pp, 1)
        npp += 1
        for dch in range(CCH):
            for cchi in range(CCH):
                tn.wait_ge(s_vp, npp - 1)
                ps = psC if npp % 2 == 0 else psD
                tn.transpose(ps[:, 0:128], wstage_sb[:, dch, 128 * cchi:128 * (cchi + 1)],
                             idn_sb[:, :]).then_inc(s_pp, 1)
                npp += 1
        nt1 = 0
        tn.wait_ge(s_vp, NPP_PRE)
        for tt in range(NT):
            tn.wait_ge(s_n1, tt + 1)
            for j in range(CCH):
                if nt1 >= 4:
                    tn.wait_ge(s_f1, nt1 - 3)
                ps = PS4[nt1 % 4]
                tn.transpose(ps[:, 0:128], lnt_sb[tt % 2][:, 128 * j:128 * (j + 1)],
                             idn_sb[:, :]).then_inc(s_t1, 1)
                nt1 += 1
        nvl = 0
        tn.wait_ge(s_f1, 128)
        for tt in range(NT):
            for hf in range(2):
                if nvl >= 2:
                    tn.wait_ge(s_ve, nvl - 1)
                ps = psC if nvl % 2 == 0 else psD
                for cch in range(CCH):
                    mmi = tn.matmul(ps[:, :], fm_sb[:, cch, 128 * tt:128 * (tt + 1)],
                                    wvo_ap[:, cch, 512 * hf:512 * (hf + 1)],
                                    start=(cch == 0), stop=(cch == 7))
                mmi.then_inc(s_vl, 1)
                nvl += 1
        for qb in range(NQB):
            if qb >= 1:
                tn.wait_ge(s_ve, 32 + qb)
            for cch in range(CCH):
                mmi = tn.matmul(psE[0:64, :], bwT_sb[:, cch, :],
                                fm_sb[:, cch, QB * qb:QB * (qb + 1)],
                                start=(cch == 0), stop=(cch == 7))
            mmi.then_inc(s_vl, 1)
        tn.wait_ge(s_ve, NVE_ALL)
        def emit_recb(ih2):
            tn.wait_ge(s_rc, ih2 + 1)
            tn.matmul(LTB[ih2 % 2][0:64, :], ones_sb[0:1, 0:64], rec_ap,
                      start=True, stop=True).then_inc(s_rb, 1)
        for ih, (h, qb) in enumerate(hq_list):
            if ih >= 2:
                tn.wait_ge(s_on, ih - 1)
            mmi = tn.matmul(LTB[ih % 2][0:64, :], M_sb[:, h, :], L_ap[:, QB * qb:QB * (qb + 1)],
                            start=True, stop=True)
            mmi.then_inc(s_vl, 1)
            tn.wait_ge(s_lt, ih + 1)
            nkt = 4 * qb + 4
            def emit_S(kt, h=h, qb=qb, ih=ih):
                i = SCX[(h, qb, kt)]
                if i >= 5:
                    tn.wait_ge(s_ex, i - 4)
                tn.matmul(PS4[(i - 1) % 4], L_ap[:, 128 * kt:128 * (kt + 1)],
                          lt_ap[ih % 2], start=True, stop=True).then_inc(s_sc, 1)
            def emit_AV(kt, h=h, qb=qb, ih=ih, nkt=nkt):
                i = SCX[(h, qb, kt)]
                if kt >= 4 * qb:
                    tn.wait_ge(s_mk, MKX[(h, qb, kt)])
                else:
                    tn.wait_ge(s_ex, i)
                if kt == 0 and ih >= 2:
                    tn.wait_ge(s_on, ih - 1)
                tn.matmul(AVB[ih % 2][0:65, :], V_sb[:, kt, h, :], pt_ap[(i - 1) % 6],
                          start=(kt == 0), stop=(kt == nkt - 1)).then_inc(s_av, 1)
            LA = 3
            for kt in range(min(LA, nkt)):
                emit_S(kt)
            if ih >= 1 and qb == 0:
                emit_recb(ih - 1)
            for kt in range(nkt):
                if kt + LA < nkt:
                    emit_S(kt + LA)
                emit_AV(kt)
                if kt == 0 and ih >= 1 and qb >= 1:
                    emit_recb(ih - 1)
        emit_recb(N_HQ - 1)
        tn.wait_ge(s_ow, 16)
        tn.wait_ge(s_ex, N_KT)
        tn.wait_ge(s_on, N_HQ)
        for kk in range(NPP_OW):
            if kk >= 2:
                tn.wait_ge(s_vp, NPP_PRE + kk - 1)
            ps = psC if (NPP_PRE + kk) % 2 == 0 else psD
            cchi, dch = kk // 8, kk % 8
            tn.transpose(ps[:, 0:128], wstage_sb[:, cchi, 128 * dch:128 * (dch + 1)],
                         idn_sb[:, :]).then_inc(s_pp, 1)
        tn.wait_ge(s_on, N_HQ)
        tn.wait_ge(s_vp, NPP_PRE + NPP_OW)
        nop = 0
        for tt in range(NT):
            for hf in range(2):
                if nop >= 2:
                    tn.wait_ge(s_ox, nop - 1)
                ps = psC if nop % 2 == 0 else psD
                for dch in range(CCH):
                    mmi = tn.matmul(ps[:, :], fm_sb[:, dch, 128 * tt:128 * (tt + 1)],
                                    wvo_ap[:, dch, 512 * hf:512 * (hf + 1)],
                                    start=(dch == 0), stop=(dch == 7))
                mmi.then_inc(s_op, 1)
                nop += 1
        if PH < 2:
            return
        nt2 = 0
        for tt in range(NT):
            tn.wait_ge(s_n2, tt + 1)
            if tt >= 1:
                tn.wait_ge(s_le, tt)
            for j in range(CCH):
                if nt2 >= 4:
                    tn.wait_ge(s_f2, nt2 - 3)
                ps = PS4[nt2 % 4]
                tn.transpose(ps[:, 0:128], lnt_sb[tt % 2][:, 128 * j:128 * (j + 1)],
                             idn_sb[:, :]).then_inc(s_t2, 1)
                tn.wait_ge(s_f2, nt2 + 1)
                mmi = tn.matmul(psE[:, 0:E], fmst_ap[nt2 % 4], rg_sb[:, j, :],
                                start=(j == 0), stop=(j == 7))
                mmi.then_inc(s_lg, 1)
                nt2 += 1
        tn.wait_ge(s_f2, 128)
        for cc in range(16):
            tn.wait_ge(s_rw, (cc + 1) * 8)
            tn.matmul(psC[:, 0:128], m1_ap[cc % 2], JB_sb[:, 120 - 8 * cc:248 - 8 * cc],
                      start=(cc == 0), stop=(cc == 15)).then_inc(s_rp, 1)
        tn.wait_ge(s_ls, 2)
        tn.matmul(psD[0:16, 0:NCOL], S16_sb[:, :], lsf_ap, start=True, stop=True).then_inc(s_rp, 1)
        tn.wait_ge(s_rt, 7)
        tn.matmul(psE[:, 0:NCOL], S2_sb[:, :], c16_ap, start=True, stop=True).then_inc(s_rp, 1)
        if PH < 3:
            return
        for e in range(KE):
            if KSUB < 2:
                break
            epar = e % 2
            tn.wait_ge(s_sh, e + 1)
            if e >= 1:
                tn.wait_ge(s_gv, e)
            for st in range(NT_E):
                mmg = tn.matmul(psF[:, st:st + 1], shuf_ap[0:1, 128 * st:128 * (st + 1)],
                                onesb_sb[0:1, 0:1], start=True, stop=True)
            mmg.then_inc(s_gc, 1)
            if not has_gateup:
                continue
            tn.wait_ge(s_gth, (2 * e + 1) * 16)
            for half in range(NH):
                for grp in range(NG // NH * half, NG // NH * (half + 1)):
                    k = e * NG + grp
                    tn.wait_ge(s_mw, (k + 1) * 32)
                    for il_ in range(GI):
                        ich = GI * grp + il_
                        gich = e * 32 + ich
                        if gich >= 1:
                            tn.wait_ge(s_si, gich)
                        for cch in range(CCH):
                            for sp in range(2):
                                w_ = 512 if sp == 0 else 384
                                mmi = tn.matmul(psA[:, 512 * sp:512 * sp + w_],
                                                gw_ap[k % 2][:, cch, 128 * il_:128 * (il_ + 1)],
                                                xe_ap[epar][:, cch, 512 * sp:512 * sp + w_],
                                                start=(cch == 0), stop=(cch == 7))
                        mmi.then_inc(s_ga, 1)
                        if gich >= 1:
                            tn.wait_ge(s_hm, gich)
                        for cch in range(CCH):
                            for sp in range(2):
                                w_ = 512 if sp == 0 else 384
                                mmi = tn.matmul(psB[:, 512 * sp:512 * sp + w_],
                                                uw_ap[k % 2][:, cch, 128 * il_:128 * (il_ + 1)],
                                                xe_ap[epar][:, cch, 512 * sp:512 * sp + w_],
                                                start=(cch == 0), stop=(cch == 7))
                        mmi.then_inc(s_ub, 1)
                kd = e * NH + half
                if not has_down:
                    continue
                tn.wait_ge(s_md, (kd + 1) * 16)
                tn.wait_ge(s_hm, e * 32 + 16 * (half + 1))
                if half == 0:
                    tn.wait_ge(s_gv, e + 1)
                for st in range(NT_E):
                    prev = kd * NT_E + st
                    if prev >= 2 and has_ye:
                        tn.wait_ge(s_ya, 2 * (prev - 1))
                    pA, pB = (psC, psD) if prev % 2 == 0 else (psE, psF)
                    for hf2 in range(2):
                        ps = pA if hf2 == 0 else pB
                        for ichl in range(16):
                            mmi = tn.matmul(ps[:, :], h_ap[:, ichl, 128 * st:128 * (st + 1)],
                                            dw_ap[:, ichl, 512 * hf2:512 * (hf2 + 1)],
                                            start=(ichl == 0), stop=(ichl == 15))
                        if hf2 == 0 and half == NH - 1 and st == NT_E - 1:
                            mmi.then_inc(s_a2, 1)
                    mmi.then_inc(s_dn, 2)

    @block.vector
    def _(v):
        nvs = [0]
        def seal(ins):
            nvs[0] += 1
            ins.then_inc(s_vs, 1)
            v.wait_ge(s_vs, nvs[0])
        v.memset(ones_sb[:], 1.0)
        v.memset(onesb_sb[:], 1.0)
        v.wait_ge(s_ld, LDN_CONSTS * 16)
        v.tensor_copy(MB_sb[:], lnt_sb[0][:, 0:896])
        v.tensor_copy(LM_sb[:], lnt_sb[1][:, 0:128])
        seal(v.tensor_copy(JB_sb[:], lnt_sb[1][:, 128:376]))
        v.engine_nop().then_inc(s_rt, 1)
        v.wait_ge(s_ld, LDN * 16)
        seal(v.tensor_copy(rg_sb[:], lnt_sb[1][:, 384:448].rearrange("p (a b) -> p a b", a=8)))
        nvp = 0
        for j in range(CCH):
            v.wait_ge(s_pp, nvp + 1)
            ps = psC if nvp % 2 == 0 else psD
            v.tensor_copy(bwT_sb[:, j, :], ps[:, 0:64]).then_inc(s_vp, 1)
            nvp += 1
        v.wait_ge(s_pp, nvp + 1)
        ps = psC if nvp % 2 == 0 else psD
        seal(v.tensor_copy(coreT_ap, ps[0:64, 0:64]))
        v.engine_nop().then_inc(s_vp, 1)
        nvp += 1
        hrv = hrV_ap.rearrange("p (a b) -> p a b", a=16)
        seal(v.tensor_add(lnt_sb[0][0:64, 0:64], coreV_ap, coreT_ap))
        seal(v.tensor_scalar_mul(lnt_sb[0][0:64, 0:64], lnt_sb[0][0:64, 0:64], 0.5 / (16.0 * 8.0)))
        seal(v.reduce_sum(lnt_sb[0][0:64, 64:128].rearrange("p (a b) -> p a b", b=1),
                          hrv.rearrange("p a b -> p b a"), axis=AX.X))
        seal(v.tensor_scalar_mul(lnt_sb[0][0:64, 64:128], lnt_sb[0][0:64, 64:128], 1.0 / 16.0))
        for hh in range(H):
            seal(v.tensor_sub(lnt_sb[0][0:64, 128:192], hrv[:, hh, :], lnt_sb[0][0:64, 64:128]))
            seal(v.scalar_tensor_tensor(M_sb[:, hh, :], lnt_sb[0][0:64, 128:192], 1.0 / 8.0,
                                        lnt_sb[0][0:64, 0:64], ALU.mult, ALU.add))
        for kk in range(64):
            v.wait_ge(s_pp, 9 + kk + 1)
            ps = psC if (9 + kk) % 2 == 0 else psD
            dch, cchi = kk // 8, kk % 8
            v.tensor_copy(wvo_ap[:, cchi, 128 * dch:128 * (dch + 1)], ps[:, 0:128]).then_inc(s_vp, 1)
        v.memset(V_sb[:, :, :, 64:65], 1.0)
        v.wait_ge(s_gx, 16)
        v.wait_ge(s_sq, 1)
        seal(v.reduce_sum(stat_sb[:, :, 0:1], x_sb[:], axis=AX.X))
        seal(v.reduce_sum(stat_sb[:, :, 1:2],
                          sq_view.rearrange("p (a b) -> p a b", a=NT), axis=AX.X))
        v.engine_nop().then_inc(s_rt, 1)
        seal(v.tensor_scalar_mul(stat_sb[:, :, 0:1], stat_sb[:, :, 0:1], 1.0 / float(C)))
        seal(v.tensor_scalar_mul(stat_sb[:, :, 1:2], stat_sb[:, :, 1:2], 1.0 / float(C)))
        seal(v.tensor_tensor(stat_sb[:, :, 2:3], stat_sb[:, :, 0:1], stat_sb[:, :, 0:1], ALU.mult))
        seal(v.tensor_sub(stat_sb[:, :, 2:3], stat_sb[:, :, 1:2], stat_sb[:, :, 2:3]))
        seal(v.tensor_scalar_add(stat_sb[:, :, 2:3], stat_sb[:, :, 2:3], EPS))
        seal(v.reciprocal(stat_sb[:, :, 3:4], stat_sb[:, :, 2:3]))
        v.engine_nop().then_inc(s_rt, 1)
        v.wait_ge(s_as, 1)
        nf1 = 0
        for tt in range(NT):
            if tt >= 2:
                v.wait_ge(s_t1, 8 * (tt - 1))
            seal(v.tensor_scalar(lnt_sb[tt % 2], x_sb[:, tt, :],
                                 stat_sb[:, tt, 0:1], stat_sb[:, tt, 4:5],
                                 ALU.subtract, ALU.mult))
            v.engine_nop().then_inc(s_n1, 1)
            for j in range(CCH):
                v.wait_ge(s_t1, nf1 + 1)
                ps = PS4[nf1 % 4]
                v.tensor_copy(fm_sb[:, j, 128 * tt:128 * (tt + 1)], ps[:, 0:128]).then_inc(s_f1, 1)
                nf1 += 1
        nve = 0
        for tt in range(NT):
            for hf in range(2):
                v.wait_ge(s_vl, nve + 1)
                ps = psC if nve % 2 == 0 else psD
                v.tensor_copy(V_sb[:, tt, 8 * hf:8 * (hf + 1), 0:64],
                              ps[:, :].rearrange("p (a b) -> p a b", b=64)).then_inc(s_ve, 1)
                nve += 1
        for qb in range(NQB):
            v.wait_ge(s_vl, nve + 1)
            v.tensor_copy(L_ap[:, QB * qb:QB * (qb + 1)], psE[0:64, :]).then_inc(s_ve, 1)
            nve += 1
        def vchain(j):
            h2, qb2 = hq_list[j]
            v.wait_ge(s_av, SCX[(h2, qb2, 4 * qb2 + 3)])
            seal(v.tensor_copy(rec_ap, AVB[j % 2][64:65, :]))
            v.engine_nop().then_inc(s_rc, 1)
            v.wait_ge(s_rb, j + 1)
            hcol, hpar = h2 // 2, h2 % 2
            seal(v.reciprocal(rb_ap, LTB[j % 2][0:64, :]))
            v.tensor_tensor(fm_sb[64 * hpar:64 * hpar + 64, hcol, QB * qb2:QB * (qb2 + 1)],
                            AVB[j % 2][0:64, :], rb_ap, ALU.mult).then_inc(s_on, 1)
        for ih, (h, qb) in enumerate(hq_list):
            v.wait_ge(s_vl, NVE_ALL + ih + 1)
            v.tensor_copy(lt_ap[ih % 2], LTB[ih % 2][0:64, :]).then_inc(s_lt, 1)
            if ih >= 1:
                vchain(ih - 1)
            for kt in range(4 * qb, 4 * qb + 4):
                i = SCX[(h, qb, kt)]
                d = kt - 4 * qb
                v.wait_ge(s_ex, i)
                v.tensor_tensor(pt_ap[(i - 1) % 6], pt_ap[(i - 1) % 6],
                                MB_sb[:, 384 - 128 * d:384 - 128 * d + 512],
                                ALU.mult).then_inc(s_mk, 1)
        vchain(N_HQ - 1)
        for kk in range(64):
            v.wait_ge(s_pp, NPP_PRE + kk + 1)
            ps = psC if (NPP_PRE + kk) % 2 == 0 else psD
            cchi, dch = kk // 8, kk % 8
            v.tensor_copy(wvo_ap[:, dch, 128 * cchi:128 * (cchi + 1)], ps[:, 0:128]).then_inc(s_vp, 1)
        nox = 0
        for tt in range(NT):
            for hf in range(2):
                v.wait_ge(s_op, nox + 1)
                ps = psC if nox % 2 == 0 else psD
                v.tensor_tensor(x_sb[:, tt, 512 * hf:512 * (hf + 1)], ps[:, :],
                                x_sb[:, tt, 512 * hf:512 * (hf + 1)], ALU.add).then_inc(s_ox, 1)
                nox += 1
        if PH < 2:
            return
        v.wait_ge(s_sq, 2)
        seal(v.reduce_sum(stat_sb[:, :, 0:1], x_sb[:], axis=AX.X))
        seal(v.reduce_sum(stat_sb[:, :, 1:2],
                          sq2_view.rearrange("p (a b) -> p a b", a=NT), axis=AX.X))
        seal(v.tensor_scalar_mul(stat_sb[:, :, 0:1], stat_sb[:, :, 0:1], 1.0 / float(C)))
        seal(v.tensor_scalar_mul(stat_sb[:, :, 1:2], stat_sb[:, :, 1:2], 1.0 / float(C)))
        seal(v.tensor_tensor(stat_sb[:, :, 2:3], stat_sb[:, :, 0:1], stat_sb[:, :, 0:1], ALU.mult))
        seal(v.tensor_sub(stat_sb[:, :, 2:3], stat_sb[:, :, 1:2], stat_sb[:, :, 2:3]))
        seal(v.tensor_scalar_add(stat_sb[:, :, 2:3], stat_sb[:, :, 2:3], EPS))
        seal(v.reciprocal(stat_sb[:, :, 3:4], stat_sb[:, :, 2:3]))
        v.engine_nop().then_inc(s_rt, 1)
        v.wait_ge(s_as, 2)
        nf2 = 0
        for tt in range(NT):
            if tt >= 2:
                v.wait_ge(s_t2, 8 * (tt - 1))
            seal(v.tensor_scalar(lnt_sb[tt % 2], x_sb[:, tt, :],
                                 stat_sb[:, tt, 0:1], stat_sb[:, tt, 4:5],
                                 ALU.subtract, ALU.mult))
            v.tensor_copy(rows_sb[:, tt, :], lnt_sb[tt % 2])
            v.engine_nop().then_inc(s_n2, 1)
            for j in range(CCH):
                v.wait_ge(s_t2, nf2 + 1)
                if nf2 >= 4:
                    v.wait_ge(s_lg, nf2 - 3)
                ps = PS4[nf2 % 4]
                v.tensor_copy(fmst_ap[nf2 % 4], ps[:, 0:128]).then_inc(s_f2, 1)
                nf2 += 1
            v.wait_ge(s_lg, 8 * (tt + 1))
            v.tensor_copy(logit_sb[:, tt, :], psE[:, 0:E]).then_inc(s_le, 1)
        for tt in range(NT):
            seal(v.max(out=maxv_ap[:, tt, :], in_=logit_sb[:, tt, :]))
            seal(v.max_index(idxu_ap[:, tt, :], maxv_ap[:, tt, :], logit_sb[:, tt, :]))
        seal(v.tensor_copy(ti_sb[:], idxu_ap[:, :, 0:2]))
        v.engine_nop().then_inc(s_rt, 1)
        v.wait_ge(s_as, 3)
        seal(v.reduce_sum(zrow_ap, Eall_ap, axis=AX.X))
        seal(v.tensor_tensor(w12_ap[:, :, 2:3], Eall_ap[:, :, 0:1], Eall_ap[:, :, 1:2], ALU.add))
        seal(v.scalar_tensor_tensor(w12_ap[:, :, 2:3], zrow_ap, 1e-9,
                                    w12_ap[:, :, 2:3], ALU.mult, ALU.add))
        seal(v.reciprocal(w12_ap[:, :, 3:4], w12_ap[:, :, 2:3]))
        seal(v.tensor_tensor(tw_sb[:, :, 0:1], Eall_ap[:, :, 0:1], w12_ap[:, :, 3:4], ALU.mult))
        seal(v.tensor_tensor(tw_sb[:, :, 1:2], Eall_ap[:, :, 1:2], w12_ap[:, :, 3:4], ALU.mult))
        for e in range(E):
            seal(v.tensor_single_scalar(gd_sb[:, :, e], ti_sb[:, :, 0], float(e), ALU.is_equal))
            seal(v.tensor_tensor(gd_sb[:, :, e], gd_sb[:, :, e], tw_sb[:, :, 0], ALU.mult))
            seal(v.tensor_single_scalar(tmp16_ap, ti_sb[:, :, 1], float(e), ALU.is_equal))
            seal(v.tensor_tensor(tmp16_ap, tmp16_ap, tw_sb[:, :, 1], ALU.mult))
            seal(v.tensor_tensor(gd_sb[:, :, e], gd_sb[:, :, e], tmp16_ap, ALU.add))
        seal(v.tensor_single_scalar(indall_ap, gd_sb[:], 0.0, ALU.is_gt))
        for r in range(NT):
            v.tensor_copy(grows_sb[:, r, 0:8], gd_sb[:, r, :])
        v.memset(grows_sb[:, :, 8:128], 0.0)
        v.memset(grows_sb[:, 16, :], 0.0)
        v.engine_nop().then_inc(s_rd, 1)
        for cc in range(16):
            if cc >= 2:
                v.wait_ge(s_rp, cc - 1)
            for e in range(E):
                v.tensor_scalar(m1_ap[cc % 2][:, e * 16:(e + 1) * 16],
                                LM_sb[:, e * 16:(e + 1) * 16],
                                indall_ap[:, cc, e:e + 1], None, ALU.mult).then_inc(s_rw, 1)
        v.wait_ge(s_rp, 16)
        seal(v.tensor_copy(cum_ap[0], psC[:, 0:128]))
        seal(v.tensor_copy(bsb_ap, psC[:, 0:128]))
        par = 0
        k = 1
        while k < 128:
            nxt = 1 - par
            seal(v.tensor_copy(cum_ap[nxt][:, 0:k], cum_ap[par][:, 0:k]))
            seal(v.tensor_tensor(cum_ap[nxt][:, k:128], cum_ap[par][:, k:128],
                                 cum_ap[par][:, 0:128 - k], ALU.add))
            par = nxt
            k *= 2
        seal(v.tensor_sub(cex_ap, cum_ap[par], bsb_ap))
        seal(v.tensor_single_scalar(idxf_ap, cex_ap, float(CCAP), ALU.is_lt))
        seal(v.tensor_tensor(idxf_ap, idxf_ap, bsb_ap, ALU.mult))
        seal(v.tensor_scalar(cex_ap, cex_ap, EC_sb[:, 0:1], None, ALU.add))
        seal(v.tensor_scalar_add(cex_ap, cex_ap, 1.0))
        seal(v.tensor_tensor(idxf_ap, idxf_ap, cex_ap, ALU.mult))
        seal(v.tensor_scalar_add(idxf_ap, idxf_ap, -1.0))
        seal(v.tensor_copy(idxi_ap, idxf_ap))
        v.engine_nop().then_inc(s_rt, 1)
        v.wait_ge(s_ls, 1)
        seal(v.tensor_copy(lsf_ap, lsd_ap))
        v.engine_nop().then_inc(s_ls, 1)
        v.wait_ge(s_rp, 17)
        seal(v.tensor_copy(c16_ap, psD[0:16, 0:NCOL]))
        v.engine_nop().then_inc(s_rt, 1)
        v.wait_ge(s_rp, 18)
        seal(v.tensor_scalar_add(lsf_ap, psE[:, 0:NCOL], -1.0))
        seal(v.tensor_copy(gidx_sb[:], lsf_ap))
        seal(v.tensor_single_scalar(mski_ap, gidx_sb[:], 0, ALU.is_lt))
        seal(v.scalar_tensor_tensor(gidx_sb[:], mski_ap, 2049.0, gidx_sb[:], ALU.mult, ALU.add))
        seal(v.tensor_scalar_min(mski_ap, gidx_sb[:], 2047))
        v.engine_nop().then_inc(s_rg, 1)
        if PH < 3:
            return
        def vshuf(e2):
            v.wait_ge(s_gth, (2 * e2 + 2) * 16)
            if e2 >= 1:
                v.wait_ge(s_gc, e2)
            mask = list(range(32))
            mask[0] = e2
            v.stream_shuffle(shuf_ap, ge_ap[0:32, 0, :], mask).then_inc(s_sh, 1)
        if KSUB >= 2 and KE >= 1:
            vshuf(0)
        for e in range(KE):
            if KSUB < 2:
                break
            v.wait_ge(s_gc, e + 1)
            seal(v.tensor_copy(gcol_sb[:, :], psF[:, 0:NT_E]))
            v.engine_nop().then_inc(s_gv, 1)
            if not has_gateup:
                if e + 1 < KE:
                    vshuf(e + 1)
                continue
            for half in range(NH):
                for ichl in range(16):
                    ich = 16 * half + ichl
                    gich = e * 32 + ich
                    v.wait_ge(s_si, gich + 1)
                    v.wait_ge(s_ub, gich + 1)
                    v.tensor_tensor(h_ap[:, ichl, :], silu_ap[gich % 2], psB[:, 0:REGS],
                                    ALU.mult).then_inc(s_hm, 1)
                if half == NH - 1 and e + 1 < KE:
                    vshuf(e + 1)
                kd2 = e * NH + half
                if not has_ye:
                    continue
                for st in range(NT_E):
                    idxg = kd2 * NT_E + st
                    v.wait_ge(s_dn, 2 * (idxg + 1))
                    if st == 0 and kd2 >= 1 and has_scat:
                        v.wait_ge(s_sct, 16 * kd2)
                    pA, pB = (psC, psD) if idxg % 2 == 0 else (psE, psF)
                    v.tensor_scalar(ye_sb[:, st, 0:512], pA[:, :],
                                    gcol_sb[:, st:st + 1], None, ALU.mult).then_inc(s_ya, 1)
                    v.tensor_scalar(ye_sb[:, st, 512:1024], pB[:, :],
                                    gcol_sb[:, st:st + 1], None, ALU.mult).then_inc(s_ya, 1)

    @block.scalar
    def _(sc):
        sc.wait_ge(s_gx, 16)
        sc.wait_ge(s_vp, NPP_PRE)
        sc.activation(sq_view, x_sb[:].rearrange("p a b -> p (a b)"), AF.Square).then_inc(s_sq, 1)
        sc.wait_ge(s_rt, 3)
        sc.activation(stat_sb[:, :, 4:5], stat_sb[:, :, 3:4], AF.Sqrt).then_inc(s_as, 1)
        for i0, (h, qb, kt, dg) in enumerate(attn_seq):
            i = i0 + 1
            sc.wait_ge(s_sc, i)
            if i >= 7:
                sc.wait_ge(s_av, i - 6)
            sc.activation(pt_ap[(i - 1) % 6], PS4[(i - 1) % 4], AF.Exp).then_inc(s_ex, 1)
        if PH < 2:
            return
        sc.wait_ge(s_ox, 32)
        sc.wait_ge(s_on, N_HQ)
        sc.activation(sq2_view, x_sb[:].rearrange("p a b -> p (a b)"), AF.Square).then_inc(s_sq, 1)
        sc.wait_ge(s_rt, 4)
        sc.activation(stat_sb[:, :, 4:5], stat_sb[:, :, 3:4], AF.Sqrt).then_inc(s_as, 1)
        sc.wait_ge(s_rt, 5)
        sc.activation(Eall_ap, maxv_ap, AF.Exp).then_inc(s_as, 1)
        if PH < 3:
            return
        for e in range(KE):
            if not has_gateup:
                break
            for ich in range(32):
                gich = e * 32 + ich
                sc.wait_ge(s_ga, gich + 1)
                if gich >= 2:
                    sc.wait_ge(s_hm, gich - 1)
                sc.activation(silu_ap[gich % 2], psA[:, 0:REGS], AF.Silu).then_inc(s_si, 1)

    es.close()
    lower_extended_insts(nc)
    return nc


_NC_CACHE = [None]


def kernel(L0_attn_hidden_states, L0_attn_basis_w, L0_attn_core, L0_attn_head_residual,
           L0_attn_v_w, L0_attn_v_b, L0_attn_o_w, L0_attn_o_b,
           L0_ffn_router_gate, L0_ffn_gate_w, L0_ffn_up_w, L0_ffn_down_w):
    x = np.ascontiguousarray(np.asarray(L0_attn_hidden_states, dtype=np.float32))
    if _NC_CACHE[0] is None:
        _NC_CACHE[0] = build_nc()
    nc = _NC_CACHE[0]
    w = dict(
        basis_w=np.ascontiguousarray(np.asarray(L0_attn_basis_w, dtype=np.float32)),
        core=np.ascontiguousarray(np.asarray(L0_attn_core, dtype=np.float32)),
        head_residual=np.ascontiguousarray(np.asarray(L0_attn_head_residual, dtype=np.float32)),
        v_w=np.ascontiguousarray(np.asarray(L0_attn_v_w, dtype=np.float32)),
        o_w=np.ascontiguousarray(np.asarray(L0_attn_o_w, dtype=np.float32)),
        router_gate=np.ascontiguousarray(np.asarray(L0_ffn_router_gate, dtype=np.float32)),
        gate_w=np.ascontiguousarray(np.asarray(L0_ffn_gate_w, dtype=np.float32)
                                    .reshape(8, 8, 128, 16, 256).transpose(0, 3, 2, 1, 4)),
        up_w=np.ascontiguousarray(np.asarray(L0_ffn_up_w, dtype=np.float32)
                                  .reshape(8, 8, 128, 16, 256).transpose(0, 3, 2, 1, 4)),
        down_w=np.ascontiguousarray(np.asarray(L0_ffn_down_w, dtype=np.float32)
                                    .reshape(8, 2, 16, 128, 1024).transpose(0, 1, 3, 2, 4)),
    )
    in_maps = [dict(w, x=np.ascontiguousarray(x[b])) for b in range(8)]
    res = run_bass_kernel_spmd(nc, in_maps, core_ids=list(range(8)))
    out = np.stack([np.asarray(r["out"])[:T] for r in res.results], axis=0)
    return out.astype(np.float32)


# version 12
# speedup vs baseline: 1.0748x; 1.0033x over previous
import os as _os
import numpy as np
import ml_dtypes
import concourse.bass as bass
import concourse.mybir as mybir
from concourse.bass_utils import run_bass_kernel_spmd
from concourse import library_config as _libs
from concourse.library_overlay import lower_extended_insts

F32 = mybir.dt.float32
BF16 = mybir.dt.bfloat16
I16 = mybir.dt.int16
U32 = mybir.dt.uint32
AF = mybir.ActivationFunctionType
ALU = mybir.AluOpType
AX = mybir.AxisListType

T = 2048
NT = 16
C = 1024
CCH = 8
H = 16
R = 64
E = 8
IFF = 4096
CCAP = 56
REGS = CCAP * 16
NT_E = REGS // 128
NCOL = E * CCAP
QB = 512
NQB = T // QB
EPS = 1e-5
NG = 16
GI = 2
NH = 2


def _consts():
    p = np.arange(128)
    idn = np.eye(128, dtype=np.float32)
    MB = (np.arange(896)[None, :] >= p[:, None] + 384).astype(np.float32)
    LM = np.zeros((128, 128), np.float32)
    for e in range(E):
        for l in range(16):
            LM[:, e * 16 + l] = (p % 16 == l)
    JB = np.zeros((128, 248), np.float32)
    JB[p, (p // 16) + 120] = 1.0
    EC = (CCAP * (p // 16)).astype(np.float32).reshape(128, 1)
    DC = (16 * np.arange(128)[None, :] + (p % 16)[:, None] + 1).astype(np.int16)
    S16 = np.zeros((128, 16), np.float32)
    S16[p, p % 16] = 1.0
    S2 = np.zeros((16, 128), np.float32)
    for l in range(16):
        for g in range(8):
            S2[l, g * 16 + l] = 1.0
    return idn, MB, LM, JB, EC, DC, S16, S2


def build_nc():
    PH = int(_os.environ.get('KPH', '9'))
    KE = int(_os.environ.get('KE', '8'))
    KSUB = int(_os.environ.get('KSUB', '9'))
    has_ld = KSUB in (3, 4, 45, 5, 6, 9)
    has_gateup = KSUB in (4, 45, 5, 6, 9)
    has_down = KSUB in (45, 5, 6, 9)
    has_ye = KSUB in (5, 6, 9)
    has_scat = KSUB in (6, 9)
    from contextlib import ExitStack
    nc = bass.Bass("TRN2", target_bir_lowering=False, detect_race_conditions=False)
    dr = lambda n, s: nc.dram_tensor(n, s, F32, kind="ExternalInput").ap()
    x_h = dr("x", [T, C])
    bw_h = dr("basis_w", [R, C])
    core_h = dr("core", [R, R])
    hr_h = dr("head_residual", [H, R, R])
    vw_h = dr("v_w", [C, C])
    ow_h = dr("o_w", [C, C])
    rg_h = dr("router_gate", [C, E])
    gw_h = dr("gate_w", [E, NG, 128, CCH, 256])
    uw_h = dr("up_w", [E, NG, 128, CCH, 256])
    dw_h = dr("down_w", [E, NH, 128, 16, C])
    gwb_h = nc.dram_tensor("gwb", [E, NG, 128, CCH, 256], BF16, kind="Internal").ap()
    uwb_h = nc.dram_tensor("uwb", [E, NG, 128, CCH, 256], BF16, kind="Internal").ap()
    dwb_h = nc.dram_tensor("dwb", [E, NH, 128, 16, C], BF16, kind="Internal").ap()
    out_h = nc.dram_tensor("out", [T + 1, C], F32, kind="ExternalOutput").ap()

    idn_c, MB_c, LM_c, JB_c, EC_c, DC_c, S16_c, S2_c = _consts()
    il = lambda a, n: nc.inline_tensor(np.ascontiguousarray(a), n).ap()
    idn_h = il(idn_c, "c_idn")
    MB_h = il(MB_c, "c_mb")
    LM_h = il(LM_c, "c_lm")
    JB_h = il(JB_c, "c_jb")
    EC_h = il(EC_c, "c_ec")
    DC_h = il(DC_c, "c_dc")
    S16_h = il(S16_c, "c_s16")
    S2_h = il(S2_c, "c_s2")

    es = ExitStack()
    sbA = lambda n, s, dt=F32: es.enter_context(nc.sbuf_tensor(n, s, dt))
    idn_sb = sbA("idn", [128, 128])
    MB_sb = sbA("mbk", [128, 896], BF16)
    LM_sb = sbA("lmk", [128, 128], BF16)
    JB_sb = sbA("jbk", [128, 248], BF16)
    EC_sb = sbA("eck", [128, 1])
    DC_sb = sbA("dck", [128, 128], I16)
    S16_sb = sbA("s16k", [128, 16])
    S2_sb = sbA("s2k", [16, 128])
    bwT_sb = sbA("bwT", [128, CCH, R], BF16)
    M_sb = sbA("msbk", [R, H, R], BF16)
    rg_sb = sbA("rgk", [128, CCH, E], BF16)
    ones_sb = sbA("onesk", [1, 64])
    onesb_sb = sbA("onesbk", [128, 1], BF16)
    stat_sb = sbA("stat", [128, NT, 6])
    x_sb = sbA("xk", [128, NT, C], BF16)
    rows_sb = sbA("rows", [128, 16, 1024], BF16)
    grows_sb = sbA("grows", [128, 17, 128], BF16)
    logit_sb = sbA("logit", [128, NT, E])
    tw_sb = sbA("twk", [128, NT, 2])
    ti_sb = sbA("tik", [128, NT, 2])
    gd_sb = sbA("gdk", [128, NT, E])
    gidx_sb = sbA("gidxk", [128, NCOL], I16)
    gcol_sb = sbA("gcolk", [128, NT_E])
    wstage_sb = sbA("wstage", [128, CCH, 1024])
    V_sb = sbA("vsbk", [128, NT, H, 65], BF16)
    fm_sb = sbA("fmk", [128, CCH, T], BF16)

    fm_flat = fm_sb[:].rearrange("p a b -> p (a b)")
    x_flat = x_sb[:].rearrange("p a b -> p (a b)")
    rows_flat = rows_sb[:].rearrange("p a b -> p (a b)")
    V_flat = V_sb[:].rearrange("p a b c -> p (a b c)")
    ws_flat = wstage_sb[:].rearrange("p a b -> p (a b)")

    _off = [0]
    def fmr(n_bf16):
        o = _off[0]
        _off[0] += n_bf16
        assert _off[0] <= 16384
        return fm_flat[:, o:o + n_bf16]
    maxv_ap = fmr(2 * NT * E).bitcast(F32).rearrange("p (a b) -> p a b", a=NT)
    Eall_ap = fmr(2 * NT * E).bitcast(F32).rearrange("p (a b) -> p a b", a=NT)
    idxu_ap = fmr(2 * NT * E).bitcast(U32).rearrange("p (a b) -> p a b", a=NT)
    zrow_ap = fmr(2 * NT).bitcast(F32).rearrange("p (a b) -> p a b", b=1)
    w12_ap = fmr(2 * NT * 4).bitcast(F32).rearrange("p (a b) -> p a b", a=NT)
    indall_ap = fmr(2 * NT * E).bitcast(F32).rearrange("p (a b) -> p a b", a=NT)
    tmp16_ap = fmr(2 * NT).bitcast(F32)
    m1_ap = [fmr(128) for _ in range(2)]
    fmst_ap = [fmr(128) for _ in range(4)]
    cum_ap = [fmr(256).bitcast(F32) for _ in range(2)]
    bsb_ap = fmr(256).bitcast(F32)
    cex_ap = fmr(256).bitcast(F32)
    idxf_ap = fmr(256).bitcast(F32)
    lsf_ap = fmr(2 * NCOL).bitcast(F32)
    c16_full = fmr(2 * NCOL).bitcast(F32)
    c16_ap = c16_full[0:16, :]
    idxi_ap = fmr(128).bitcast(I16)
    lsd_ap = fmr(NCOL).bitcast(I16)
    mski_ap = fmr(NCOL).bitcast(I16)
    sq2_view = fm_flat
    silu_ap = fm_flat[:, 10240:10240 + REGS]
    gw_ap = [fm_flat[:, 11264 + 2048 * i:11264 + 2048 * (i + 1)]
             .rearrange("p (a b) -> p a b", a=CCH) for i in range(2)]
    _moe0 = (_off[0] + 1) // 2 * 2
    silu_ap = fm_flat[:, _moe0:_moe0 + REGS]
    _g0 = _moe0 + REGS
    gw_ap = [fm_flat[:, _g0 + 2048 * i:_g0 + 2048 * (i + 1)]
             .rearrange("p (a b) -> p a b", a=CCH) for i in range(2)]
    _u0 = _g0 + 4096
    uw_ap = [fm_flat[:, _u0 + 2048 * i:_u0 + 2048 * (i + 1)]
             .rearrange("p (a b) -> p a b", a=CCH) for i in range(2)]
    assert _u0 + 4096 <= 16384, (_off[0], _moe0, _u0)
    xe_ap = [x_flat[:, 7168 * i:7168 * (i + 1)].rearrange("p (a b) -> p a b", a=CCH)
             for i in range(2)]
    ge_ap = x_flat[:, 14336:14336 + REGS].rearrange("p (a b) -> p a b", a=1)
    shuf_ap = x_flat[0:32, 15232:15232 + REGS]
    h_ap = V_flat[:, 0:16 * REGS].rearrange("p (a b) -> p a b", a=16)
    dw_ap = ws_flat.bitcast(BF16).rearrange("p (a b) -> p a b", a=16)
    sq_view = ws_flat.bitcast(BF16)
    ye_sb = sbA("yek", [128, NT_E, 1024])
    lnt_sb = [ye_sb[:, i, :] for i in range(2)]
    rec_ap = ye_sb[0:1, 2, 0:QB]
    hrV_ap = ye_sb[0:64, 2, :]
    coreV_ap = ye_sb[64:128, 2, 0:64]
    coreT_ap = ye_sb[64:128, 2, 64:128]
    wvo_ap = rows_flat[:, 4608:4608 + CCH * 1024].rearrange("p (a b) -> p a b", a=CCH)
    L_ap = rows_flat[0:64, 0:T]
    lt_ap = [rows_flat[0:64, T + QB * i:T + QB * (i + 1)] for i in range(2)]
    pt_ap = [rows_flat[:, T + 1024 + QB * i:T + 1024 + QB * (i + 1)] for i in range(6)]
    rb_ap = rows_flat.bitcast(F32)[0:64, 6400:6400 + QB]


    psA = es.enter_context(nc.psum_tensor("psA", [128, 1024], F32))
    psB = es.enter_context(nc.psum_tensor("psB", [128, 1024], F32))
    psC = es.enter_context(nc.psum_tensor("psC", [128, 512], F32))
    psD = es.enter_context(nc.psum_tensor("psD", [128, 512], F32))
    psE = es.enter_context(nc.psum_tensor("psE", [128, 512], F32))
    psF = es.enter_context(nc.psum_tensor("psF", [128, 512], F32))
    PS4 = [psA[:, 0:512], psA[:, 512:1024], psB[:, 0:512], psB[:, 512:1024]]
    AVB = [psE, psD]
    LTB = [psF, psC]

    sem = lambda n: es.enter_context(nc.semaphore(n))
    s_ld = sem("s_ld"); s_ow = sem("s_ow"); s_gx = sem("s_gx")
    s_mw = sem("s_mw"); s_md = sem("s_md"); s_gth = sem("s_gth")
    s_sct = sem("s_sct"); s_xo = sem("s_xo")
    s_pp = sem("s_pp"); s_vp = sem("s_vp"); s_vs = sem("s_vs"); s_as = sem("s_as")
    s_sq = sem("s_sq"); s_n1 = sem("s_n1"); s_t1 = sem("s_t1"); s_f1 = sem("s_f1")
    s_vl = sem("s_vl"); s_ve = sem("s_ve")
    s_sc = sem("s_sc"); s_ex = sem("s_ex"); s_mk = sem("s_mk"); s_av = sem("s_av")
    s_lt = sem("s_lt"); s_rc = sem("s_rc"); s_rb = sem("s_rb"); s_on = sem("s_on")
    s_op = sem("s_op"); s_ox = sem("s_ox")
    s_n2 = sem("s_n2"); s_t2 = sem("s_t2"); s_f2 = sem("s_f2")
    s_lg = sem("s_lg"); s_le = sem("s_le")
    s_rt = sem("s_rt"); s_rp = sem("s_rp"); s_rw = sem("s_rw"); s_ls = sem("s_ls")
    s_rg = sem("s_rg"); s_rd = sem("s_rd")
    s_ga = sem("s_ga"); s_ub = sem("s_ub"); s_si = sem("s_si"); s_hm = sem("s_hm")
    s_dn = sem("s_dn"); s_ya = sem("s_ya"); s_gc = sem("s_gc"); s_gv = sem("s_gv")
    s_sh = sem("s_sh"); s_a2 = sem("s_a2"); s_pc = sem("s_pc")

    attn_seq = []
    hq_list = []
    for h in range(H):
        for qb in range(NQB):
            hq_list.append((h, qb))
            for kt in range(4 * qb + 4):
                attn_seq.append((h, qb, kt, kt >= 4 * qb))
    N_KT = len(attn_seq)
    N_HQ = len(hq_list)
    SCX = {}
    MKX = {}
    nmk = 0
    for i, (h, qb, kt, dg) in enumerate(attn_seq):
        SCX[(h, qb, kt)] = i + 1
        if dg:
            nmk += 1
        MKX[(h, qb, kt)] = nmk
    LDN_CONSTS = 8
    LDN = 13
    NPP_PRE = 8 + 1 + 64
    NPP_OW = 64
    NVE_ALL = 32 + NQB

    block = es.enter_context(nc.Block())

    @block.sync
    def _(s):
        s.dma_start(out=idn_sb[:], in_=idn_h).then_inc(s_ld, 16)
        s.dma_start(out=lnt_sb[0][:, 0:896], in_=MB_h).then_inc(s_ld, 16)
        s.dma_start(out=lnt_sb[1][:, 0:128], in_=LM_h).then_inc(s_ld, 16)
        s.dma_start(out=lnt_sb[1][:, 128:376], in_=JB_h).then_inc(s_ld, 16)
        s.dma_start(out=EC_sb[:], in_=EC_h).then_inc(s_ld, 16)
        s.dma_start(out=DC_sb[:], in_=DC_h).then_inc(s_ld, 16)
        s.dma_start(out=S16_sb[:], in_=S16_h).then_inc(s_ld, 16)
        s.dma_start(out=S2_sb[:], in_=S2_h).then_inc(s_ld, 16)
        s.wait_ge(s_rt, 1)
        s.dma_start(out=lnt_sb[0][0:64, :], in_=bw_h).then_inc(s_ld, 16)
        s.dma_start(out=coreV_ap, in_=core_h).then_inc(s_ld, 16)
        s.dma_start(out=hrV_ap.rearrange("p (a b) -> p a b", a=16), in_=hr_h.rearrange("h r s -> r h s")).then_inc(s_ld, 16)
        s.dma_start(out=wstage_sb[:], in_=vw_h.rearrange("(a b) c -> b a c", b=128)).then_inc(s_ld, 16)
        s.dma_start(out=lnt_sb[1][:, 384:448].rearrange("p (a b) -> p a b", a=8),
                    in_=rg_h.rearrange("(a b) c -> b a c", b=128)).then_inc(s_ld, 16)
        s.wait_ge(s_rt, 2)
        s.dma_start(out=wstage_sb[:], in_=ow_h.rearrange("(a b) c -> b a c", b=128)).then_inc(s_ow, 16)
        if PH >= 3 and has_ld:
            s.wait_ge(s_rg, 1)
            s.wait_ge(s_rd, 1)
            s.wait_ge(s_vp, NPP_PRE + NPP_OW)
            for e in range(KE):
                for grp in range(NG):
                    k = e * NG + grp
                    if grp == 0:
                        s.wait_ge(s_pc, (3 * e + 2) * 16)
                    if k >= 2 and has_gateup:
                        s.wait_ge(s_ub, GI * (k - 1))
                    s.dma_start(out=gw_ap[k % 2], in_=gwb_h[e, grp]).then_inc(s_mw, 16)
                    s.dma_start(out=uw_ap[k % 2], in_=uwb_h[e, grp]).then_inc(s_mw, 16)
                    if grp == NG // 2 - 1 or grp == NG - 1:
                        hf = 0 if grp == NG // 2 - 1 else 1
                        kd = e * NH + hf
                        if grp == NG // 2 - 1:
                            s.wait_ge(s_pc, (3 * e + 3) * 16)
                        if kd >= 1 and has_down:
                            s.wait_ge(s_dn, 14 * kd)
                        s.dma_start(out=dw_ap, in_=dwb_h[e, hf]).then_inc(s_md, 16)

    @block.gpsimd
    def _(g):
        g.load_library(_libs.local_scatter)
        g.dma_start(out=x_sb[:], in_=x_h.rearrange("(a b) c -> b a c", b=128)).then_inc(s_gx, 16)
        if PH >= 3 and has_ld:
            g.wait_ge(s_ow, 16)
            for e in range(KE):
                g.dma_start(out=gwb_h[e], in_=gw_h[e]).then_inc(s_pc, 16)
                g.dma_start(out=uwb_h[e], in_=uw_h[e]).then_inc(s_pc, 16)
                g.dma_start(out=dwb_h[e], in_=dw_h[e]).then_inc(s_pc, 16)
        for tt in range(NT):
            g.wait_ge(s_ox, 2 * (tt + 1))
            g.dma_start(out=out_h[128 * tt:128 * (tt + 1), :], in_=x_sb[:, tt, :]).then_inc(s_xo, 16)
        if PH < 2:
            g.wait_ge(s_xo, NT * 16)
            return
        g.wait_ge(s_rt, 6)
        g.local_scatter(lsd_ap, DC_sb[:], idxi_ap, 128, NCOL, 128).then_inc(s_ls, 1)
        g.load_library(_libs.mlp)
        g.wait_ge(s_rg, 1)
        g.wait_ge(s_rd, 1)
        if PH < 3:
            return
        g.wait_ge(s_xo, NT * 16)
        def scat(e, hf):
            kd = e * NH + hf
            g.wait_ge(s_ya, 14 * (kd + 1))
            g.wait_ge(s_sct, 16 * kd)
            if kd == 0:
                g.wait_ge(s_xo, NT * 16)
            g.dma_scatter_add(out_h, ye_sb[:], gidx_sb[:, e * CCAP:(e + 1) * CCAP],
                              REGS, REGS, C, elem_step=C).then_inc(s_sct, 16)
        for e in range(KE):
            epar = e % 2
            if e >= 2 and has_gateup:
                g.wait_ge(s_ub, GI * NG * (e - 1))
            g.dma_gather(xe_ap[epar], rows_flat, mski_ap[:, e * CCAP:(e + 1) * CCAP],
                         REGS, REGS, 1024, transpose=True,
                         sbuf_tokens_per_rank=128, sbuf_free_dim_per_rank=2048,
                         sbuf_free_dim_pad_per_rank=0, sbuf_byte_offset=0).then_inc(s_gth, 16)
            if e >= 1:
                g.wait_ge(s_sh, e)
            g.dma_gather(ge_ap, grows_sb[:].rearrange("p a b -> p (a b)"),
                         gidx_sb[:, e * CCAP:(e + 1) * CCAP],
                         REGS, REGS, 128, transpose=True,
                         sbuf_tokens_per_rank=128, sbuf_free_dim_per_rank=256,
                         sbuf_free_dim_pad_per_rank=0, sbuf_byte_offset=0).then_inc(s_gth, 16)
            if has_scat and e >= 1:
                scat(e - 1, 1)
            if has_scat and has_ye:
                scat(e, 0)
        if has_scat:
            scat(KE - 1, 1)
            g.wait_ge(s_sct, KE * NH * 16)
        else:
            g.wait_ge(s_gth, KE * 32)

    @block.tensor
    def _(tn):
        tn.wait_ge(s_ld, LDN * 16)
        npp = 0
        for j in range(CCH):
            if npp >= 2:
                tn.wait_ge(s_vp, npp - 1)
            ps = psC if npp % 2 == 0 else psD
            tn.transpose(ps[:, 0:64], lnt_sb[0][0:64, 128 * j:128 * (j + 1)], idn_sb[0:64, 0:64]).then_inc(s_pp, 1)
            npp += 1
        tn.wait_ge(s_vp, npp - 1)
        ps = psC if npp % 2 == 0 else psD
        tn.transpose(ps[0:64, 0:64], coreV_ap, idn_sb[64:128, 64:128]).then_inc(s_pp, 1)
        npp += 1
        for dch in range(CCH):
            for cchi in range(CCH):
                tn.wait_ge(s_vp, npp - 1)
                ps = psC if npp % 2 == 0 else psD
                tn.transpose(ps[:, 0:128], wstage_sb[:, dch, 128 * cchi:128 * (cchi + 1)],
                             idn_sb[:, :]).then_inc(s_pp, 1)
                npp += 1
        nt1 = 0
        tn.wait_ge(s_vp, NPP_PRE)
        for tt in range(NT):
            tn.wait_ge(s_n1, tt + 1)
            for j in range(CCH):
                if nt1 >= 4:
                    tn.wait_ge(s_f1, nt1 - 3)
                ps = PS4[nt1 % 4]
                tn.transpose(ps[:, 0:128], lnt_sb[tt % 2][:, 128 * j:128 * (j + 1)],
                             idn_sb[:, :]).then_inc(s_t1, 1)
                nt1 += 1
        nvl = 0
        tn.wait_ge(s_f1, 128)
        for tt in range(NT):
            for hf in range(2):
                if nvl >= 2:
                    tn.wait_ge(s_ve, nvl - 1)
                ps = psC if nvl % 2 == 0 else psD
                for cch in range(CCH):
                    mmi = tn.matmul(ps[:, :], fm_sb[:, cch, 128 * tt:128 * (tt + 1)],
                                    wvo_ap[:, cch, 512 * hf:512 * (hf + 1)],
                                    start=(cch == 0), stop=(cch == 7))
                mmi.then_inc(s_vl, 1)
                nvl += 1
        for qb in range(NQB):
            if qb >= 1:
                tn.wait_ge(s_ve, 32 + qb)
            for cch in range(CCH):
                mmi = tn.matmul(psE[0:64, :], bwT_sb[:, cch, :],
                                fm_sb[:, cch, QB * qb:QB * (qb + 1)],
                                start=(cch == 0), stop=(cch == 7))
            mmi.then_inc(s_vl, 1)
        tn.wait_ge(s_ve, NVE_ALL)
        def emit_recb(ih2):
            tn.wait_ge(s_rc, ih2 + 1)
            tn.matmul(LTB[ih2 % 2][0:64, :], ones_sb[0:1, 0:64], rec_ap,
                      start=True, stop=True).then_inc(s_rb, 1)
        for ih, (h, qb) in enumerate(hq_list):
            if ih >= 2:
                tn.wait_ge(s_on, ih - 1)
            mmi = tn.matmul(LTB[ih % 2][0:64, :], M_sb[:, h, :], L_ap[:, QB * qb:QB * (qb + 1)],
                            start=True, stop=True)
            mmi.then_inc(s_vl, 1)
            tn.wait_ge(s_lt, ih + 1)
            nkt = 4 * qb + 4
            def emit_S(kt, h=h, qb=qb, ih=ih):
                i = SCX[(h, qb, kt)]
                if i >= 5:
                    tn.wait_ge(s_ex, i - 4)
                tn.matmul(PS4[(i - 1) % 4], L_ap[:, 128 * kt:128 * (kt + 1)],
                          lt_ap[ih % 2], start=True, stop=True).then_inc(s_sc, 1)
            def emit_AV(kt, h=h, qb=qb, ih=ih, nkt=nkt):
                i = SCX[(h, qb, kt)]
                if kt >= 4 * qb:
                    tn.wait_ge(s_mk, MKX[(h, qb, kt)])
                else:
                    tn.wait_ge(s_ex, i)
                if kt == 0 and ih >= 2:
                    tn.wait_ge(s_on, ih - 1)
                tn.matmul(AVB[ih % 2][0:65, :], V_sb[:, kt, h, :], pt_ap[(i - 1) % 6],
                          start=(kt == 0), stop=(kt == nkt - 1)).then_inc(s_av, 1)
            LA = 3
            for kt in range(min(LA, nkt)):
                emit_S(kt)
            if ih >= 1 and qb == 0:
                emit_recb(ih - 1)
            for kt in range(nkt):
                if kt + LA < nkt:
                    emit_S(kt + LA)
                emit_AV(kt)
                if kt == 0 and ih >= 1 and qb >= 1:
                    emit_recb(ih - 1)
        emit_recb(N_HQ - 1)
        tn.wait_ge(s_ow, 16)
        tn.wait_ge(s_ex, N_KT)
        tn.wait_ge(s_on, N_HQ)
        for kk in range(NPP_OW):
            if kk >= 2:
                tn.wait_ge(s_vp, NPP_PRE + kk - 1)
            ps = psC if (NPP_PRE + kk) % 2 == 0 else psD
            cchi, dch = kk // 8, kk % 8
            tn.transpose(ps[:, 0:128], wstage_sb[:, cchi, 128 * dch:128 * (dch + 1)],
                         idn_sb[:, :]).then_inc(s_pp, 1)
        tn.wait_ge(s_on, N_HQ)
        tn.wait_ge(s_vp, NPP_PRE + NPP_OW)
        nop = 0
        for tt in range(NT):
            for hf in range(2):
                if nop >= 2:
                    tn.wait_ge(s_ox, nop - 1)
                ps = psC if nop % 2 == 0 else psD
                for dch in range(CCH):
                    mmi = tn.matmul(ps[:, :], fm_sb[:, dch, 128 * tt:128 * (tt + 1)],
                                    wvo_ap[:, dch, 512 * hf:512 * (hf + 1)],
                                    start=(dch == 0), stop=(dch == 7))
                mmi.then_inc(s_op, 1)
                nop += 1
        if PH < 2:
            return
        nt2 = 0
        for tt in range(NT):
            tn.wait_ge(s_n2, tt + 1)
            if tt >= 1:
                tn.wait_ge(s_le, tt)
            for j in range(CCH):
                if nt2 >= 4:
                    tn.wait_ge(s_f2, nt2 - 3)
                ps = PS4[nt2 % 4]
                tn.transpose(ps[:, 0:128], lnt_sb[tt % 2][:, 128 * j:128 * (j + 1)],
                             idn_sb[:, :]).then_inc(s_t2, 1)
                tn.wait_ge(s_f2, nt2 + 1)
                mmi = tn.matmul(psE[:, 0:E], fmst_ap[nt2 % 4], rg_sb[:, j, :],
                                start=(j == 0), stop=(j == 7))
                mmi.then_inc(s_lg, 1)
                nt2 += 1
        tn.wait_ge(s_f2, 128)
        for cc in range(16):
            tn.wait_ge(s_rw, (cc + 1) * 8)
            tn.matmul(psC[:, 0:128], m1_ap[cc % 2], JB_sb[:, 120 - 8 * cc:248 - 8 * cc],
                      start=(cc == 0), stop=(cc == 15)).then_inc(s_rp, 1)
        tn.wait_ge(s_ls, 2)
        tn.matmul(psD[0:16, 0:NCOL], S16_sb[:, :], lsf_ap, start=True, stop=True).then_inc(s_rp, 1)
        tn.wait_ge(s_rt, 7)
        tn.matmul(psE[:, 0:NCOL], S2_sb[:, :], c16_ap, start=True, stop=True).then_inc(s_rp, 1)
        if PH < 3:
            return
        for e in range(KE):
            if KSUB < 2:
                break
            epar = e % 2
            tn.wait_ge(s_sh, e + 1)
            if e >= 1:
                tn.wait_ge(s_gv, e)
            for st in range(NT_E):
                mmg = tn.matmul(psF[:, st:st + 1], shuf_ap[0:1, 128 * st:128 * (st + 1)],
                                onesb_sb[0:1, 0:1], start=True, stop=True)
            mmg.then_inc(s_gc, 1)
            if not has_gateup:
                continue
            tn.wait_ge(s_gth, (2 * e + 1) * 16)
            for half in range(NH):
                for grp in range(NG // NH * half, NG // NH * (half + 1)):
                    k = e * NG + grp
                    tn.wait_ge(s_mw, (k + 1) * 32)
                    for il_ in range(GI):
                        ich = GI * grp + il_
                        gich = e * 32 + ich
                        if gich >= 1:
                            tn.wait_ge(s_si, gich)
                        for cch in range(CCH):
                            for sp in range(2):
                                w_ = 512 if sp == 0 else 384
                                mmi = tn.matmul(psA[:, 512 * sp:512 * sp + w_],
                                                gw_ap[k % 2][:, cch, 128 * il_:128 * (il_ + 1)],
                                                xe_ap[epar][:, cch, 512 * sp:512 * sp + w_],
                                                start=(cch == 0), stop=(cch == 7))
                        mmi.then_inc(s_ga, 1)
                        if gich >= 1:
                            tn.wait_ge(s_hm, gich)
                        for cch in range(CCH):
                            for sp in range(2):
                                w_ = 512 if sp == 0 else 384
                                mmi = tn.matmul(psB[:, 512 * sp:512 * sp + w_],
                                                uw_ap[k % 2][:, cch, 128 * il_:128 * (il_ + 1)],
                                                xe_ap[epar][:, cch, 512 * sp:512 * sp + w_],
                                                start=(cch == 0), stop=(cch == 7))
                        mmi.then_inc(s_ub, 1)
                kd = e * NH + half
                if not has_down:
                    continue
                tn.wait_ge(s_md, (kd + 1) * 16)
                tn.wait_ge(s_hm, e * 32 + 16 * (half + 1))
                if half == 0:
                    tn.wait_ge(s_gv, e + 1)
                for st in range(NT_E):
                    prev = kd * NT_E + st
                    if prev >= 2 and has_ye:
                        tn.wait_ge(s_ya, 2 * (prev - 1))
                    pA, pB = (psC, psD) if prev % 2 == 0 else (psE, psF)
                    for hf2 in range(2):
                        ps = pA if hf2 == 0 else pB
                        for ichl in range(16):
                            mmi = tn.matmul(ps[:, :], h_ap[:, ichl, 128 * st:128 * (st + 1)],
                                            dw_ap[:, ichl, 512 * hf2:512 * (hf2 + 1)],
                                            start=(ichl == 0), stop=(ichl == 15))
                        if hf2 == 0 and half == NH - 1 and st == NT_E - 1:
                            mmi.then_inc(s_a2, 1)
                    mmi.then_inc(s_dn, 2)

    @block.vector
    def _(v):
        nvs = [0]
        def seal(ins):
            nvs[0] += 1
            ins.then_inc(s_vs, 1)
            v.wait_ge(s_vs, nvs[0])
        v.memset(ones_sb[:], 1.0)
        v.memset(onesb_sb[:], 1.0)
        v.wait_ge(s_ld, LDN_CONSTS * 16)
        v.tensor_copy(MB_sb[:], lnt_sb[0][:, 0:896])
        v.tensor_copy(LM_sb[:], lnt_sb[1][:, 0:128])
        seal(v.tensor_copy(JB_sb[:], lnt_sb[1][:, 128:376]))
        v.engine_nop().then_inc(s_rt, 1)
        v.wait_ge(s_ld, LDN * 16)
        seal(v.tensor_copy(rg_sb[:], lnt_sb[1][:, 384:448].rearrange("p (a b) -> p a b", a=8)))
        nvp = 0
        for j in range(CCH):
            v.wait_ge(s_pp, nvp + 1)
            ps = psC if nvp % 2 == 0 else psD
            v.tensor_copy(bwT_sb[:, j, :], ps[:, 0:64]).then_inc(s_vp, 1)
            nvp += 1
        v.wait_ge(s_pp, nvp + 1)
        ps = psC if nvp % 2 == 0 else psD
        seal(v.tensor_copy(coreT_ap, ps[0:64, 0:64]))
        v.engine_nop().then_inc(s_vp, 1)
        nvp += 1
        hrv = hrV_ap.rearrange("p (a b) -> p a b", a=16)
        seal(v.tensor_add(lnt_sb[0][0:64, 0:64], coreV_ap, coreT_ap))
        seal(v.tensor_scalar_mul(lnt_sb[0][0:64, 0:64], lnt_sb[0][0:64, 0:64], 0.5 / (16.0 * 8.0)))
        seal(v.reduce_sum(lnt_sb[0][0:64, 64:128].rearrange("p (a b) -> p a b", b=1),
                          hrv.rearrange("p a b -> p b a"), axis=AX.X))
        seal(v.tensor_scalar_mul(lnt_sb[0][0:64, 64:128], lnt_sb[0][0:64, 64:128], 1.0 / 16.0))
        for hh in range(H):
            seal(v.tensor_sub(lnt_sb[0][0:64, 128:192], hrv[:, hh, :], lnt_sb[0][0:64, 64:128]))
            seal(v.scalar_tensor_tensor(M_sb[:, hh, :], lnt_sb[0][0:64, 128:192], 1.0 / 8.0,
                                        lnt_sb[0][0:64, 0:64], ALU.mult, ALU.add))
        for kk in range(64):
            v.wait_ge(s_pp, 9 + kk + 1)
            ps = psC if (9 + kk) % 2 == 0 else psD
            dch, cchi = kk // 8, kk % 8
            v.tensor_copy(wvo_ap[:, cchi, 128 * dch:128 * (dch + 1)], ps[:, 0:128]).then_inc(s_vp, 1)
        v.memset(V_sb[:, :, :, 64:65], 1.0)
        v.wait_ge(s_gx, 16)
        v.wait_ge(s_sq, 1)
        seal(v.reduce_sum(stat_sb[:, :, 0:1], x_sb[:], axis=AX.X))
        seal(v.reduce_sum(stat_sb[:, :, 1:2],
                          sq_view.rearrange("p (a b) -> p a b", a=NT), axis=AX.X))
        v.engine_nop().then_inc(s_rt, 1)
        seal(v.tensor_scalar_mul(stat_sb[:, :, 0:1], stat_sb[:, :, 0:1], 1.0 / float(C)))
        seal(v.tensor_scalar_mul(stat_sb[:, :, 1:2], stat_sb[:, :, 1:2], 1.0 / float(C)))
        seal(v.tensor_tensor(stat_sb[:, :, 2:3], stat_sb[:, :, 0:1], stat_sb[:, :, 0:1], ALU.mult))
        seal(v.tensor_sub(stat_sb[:, :, 2:3], stat_sb[:, :, 1:2], stat_sb[:, :, 2:3]))
        seal(v.tensor_scalar_add(stat_sb[:, :, 2:3], stat_sb[:, :, 2:3], EPS))
        seal(v.reciprocal(stat_sb[:, :, 3:4], stat_sb[:, :, 2:3]))
        v.engine_nop().then_inc(s_rt, 1)
        v.wait_ge(s_as, 1)
        nf1 = 0
        for tt in range(NT):
            if tt >= 2:
                v.wait_ge(s_t1, 8 * (tt - 1))
            seal(v.tensor_scalar(lnt_sb[tt % 2], x_sb[:, tt, :],
                                 stat_sb[:, tt, 0:1], stat_sb[:, tt, 4:5],
                                 ALU.subtract, ALU.mult))
            v.engine_nop().then_inc(s_n1, 1)
            for j in range(CCH):
                v.wait_ge(s_t1, nf1 + 1)
                ps = PS4[nf1 % 4]
                v.tensor_copy(fm_sb[:, j, 128 * tt:128 * (tt + 1)], ps[:, 0:128]).then_inc(s_f1, 1)
                nf1 += 1
        nve = 0
        for tt in range(NT):
            for hf in range(2):
                v.wait_ge(s_vl, nve + 1)
                ps = psC if nve % 2 == 0 else psD
                v.tensor_copy(V_sb[:, tt, 8 * hf:8 * (hf + 1), 0:64],
                              ps[:, :].rearrange("p (a b) -> p a b", b=64)).then_inc(s_ve, 1)
                nve += 1
        for qb in range(NQB):
            v.wait_ge(s_vl, nve + 1)
            v.tensor_copy(L_ap[:, QB * qb:QB * (qb + 1)], psE[0:64, :]).then_inc(s_ve, 1)
            nve += 1
        def vchain(j):
            h2, qb2 = hq_list[j]
            v.wait_ge(s_av, SCX[(h2, qb2, 4 * qb2 + 3)])
            seal(v.tensor_copy(rec_ap, AVB[j % 2][64:65, :]))
            v.engine_nop().then_inc(s_rc, 1)
            v.wait_ge(s_rb, j + 1)
            hcol, hpar = h2 // 2, h2 % 2
            seal(v.reciprocal(rb_ap, LTB[j % 2][0:64, :]))
            v.tensor_tensor(fm_sb[64 * hpar:64 * hpar + 64, hcol, QB * qb2:QB * (qb2 + 1)],
                            AVB[j % 2][0:64, :], rb_ap, ALU.mult).then_inc(s_on, 1)
        for ih, (h, qb) in enumerate(hq_list):
            v.wait_ge(s_vl, NVE_ALL + ih + 1)
            v.tensor_copy(lt_ap[ih % 2], LTB[ih % 2][0:64, :]).then_inc(s_lt, 1)
            if ih >= 1:
                vchain(ih - 1)
            for kt in range(4 * qb, 4 * qb + 4):
                i = SCX[(h, qb, kt)]
                d = kt - 4 * qb
                v.wait_ge(s_ex, i)
                v.tensor_tensor(pt_ap[(i - 1) % 6], pt_ap[(i - 1) % 6],
                                MB_sb[:, 384 - 128 * d:384 - 128 * d + 512],
                                ALU.mult).then_inc(s_mk, 1)
        vchain(N_HQ - 1)
        for kk in range(64):
            v.wait_ge(s_pp, NPP_PRE + kk + 1)
            ps = psC if (NPP_PRE + kk) % 2 == 0 else psD
            cchi, dch = kk // 8, kk % 8
            v.tensor_copy(wvo_ap[:, dch, 128 * cchi:128 * (cchi + 1)], ps[:, 0:128]).then_inc(s_vp, 1)
        nox = 0
        for tt in range(NT):
            for hf in range(2):
                v.wait_ge(s_op, nox + 1)
                ps = psC if nox % 2 == 0 else psD
                v.tensor_tensor(x_sb[:, tt, 512 * hf:512 * (hf + 1)], ps[:, :],
                                x_sb[:, tt, 512 * hf:512 * (hf + 1)], ALU.add).then_inc(s_ox, 1)
                nox += 1
        if PH < 2:
            return
        v.wait_ge(s_sq, 2)
        seal(v.reduce_sum(stat_sb[:, :, 0:1], x_sb[:], axis=AX.X))
        seal(v.reduce_sum(stat_sb[:, :, 1:2],
                          sq2_view.rearrange("p (a b) -> p a b", a=NT), axis=AX.X))
        seal(v.tensor_scalar_mul(stat_sb[:, :, 0:1], stat_sb[:, :, 0:1], 1.0 / float(C)))
        seal(v.tensor_scalar_mul(stat_sb[:, :, 1:2], stat_sb[:, :, 1:2], 1.0 / float(C)))
        seal(v.tensor_tensor(stat_sb[:, :, 2:3], stat_sb[:, :, 0:1], stat_sb[:, :, 0:1], ALU.mult))
        seal(v.tensor_sub(stat_sb[:, :, 2:3], stat_sb[:, :, 1:2], stat_sb[:, :, 2:3]))
        seal(v.tensor_scalar_add(stat_sb[:, :, 2:3], stat_sb[:, :, 2:3], EPS))
        seal(v.reciprocal(stat_sb[:, :, 3:4], stat_sb[:, :, 2:3]))
        v.engine_nop().then_inc(s_rt, 1)
        v.wait_ge(s_as, 2)
        nf2 = 0
        for tt in range(NT):
            if tt >= 2:
                v.wait_ge(s_t2, 8 * (tt - 1))
            seal(v.tensor_scalar(lnt_sb[tt % 2], x_sb[:, tt, :],
                                 stat_sb[:, tt, 0:1], stat_sb[:, tt, 4:5],
                                 ALU.subtract, ALU.mult))
            v.tensor_copy(rows_sb[:, tt, :], lnt_sb[tt % 2])
            v.engine_nop().then_inc(s_n2, 1)
            for j in range(CCH):
                v.wait_ge(s_t2, nf2 + 1)
                if nf2 >= 4:
                    v.wait_ge(s_lg, nf2 - 3)
                ps = PS4[nf2 % 4]
                v.tensor_copy(fmst_ap[nf2 % 4], ps[:, 0:128]).then_inc(s_f2, 1)
                nf2 += 1
            v.wait_ge(s_lg, 8 * (tt + 1))
            v.tensor_copy(logit_sb[:, tt, :], psE[:, 0:E]).then_inc(s_le, 1)
        for tt in range(NT):
            seal(v.max(out=maxv_ap[:, tt, :], in_=logit_sb[:, tt, :]))
            seal(v.max_index(idxu_ap[:, tt, :], maxv_ap[:, tt, :], logit_sb[:, tt, :]))
        seal(v.tensor_copy(ti_sb[:], idxu_ap[:, :, 0:2]))
        v.engine_nop().then_inc(s_rt, 1)
        v.wait_ge(s_as, 3)
        seal(v.reduce_sum(zrow_ap, Eall_ap, axis=AX.X))
        seal(v.tensor_tensor(w12_ap[:, :, 2:3], Eall_ap[:, :, 0:1], Eall_ap[:, :, 1:2], ALU.add))
        seal(v.scalar_tensor_tensor(w12_ap[:, :, 2:3], zrow_ap, 1e-9,
                                    w12_ap[:, :, 2:3], ALU.mult, ALU.add))
        seal(v.reciprocal(w12_ap[:, :, 3:4], w12_ap[:, :, 2:3]))
        seal(v.tensor_tensor(tw_sb[:, :, 0:1], Eall_ap[:, :, 0:1], w12_ap[:, :, 3:4], ALU.mult))
        seal(v.tensor_tensor(tw_sb[:, :, 1:2], Eall_ap[:, :, 1:2], w12_ap[:, :, 3:4], ALU.mult))
        for e in range(E):
            seal(v.tensor_single_scalar(gd_sb[:, :, e], ti_sb[:, :, 0], float(e), ALU.is_equal))
            seal(v.tensor_tensor(gd_sb[:, :, e], gd_sb[:, :, e], tw_sb[:, :, 0], ALU.mult))
            seal(v.tensor_single_scalar(tmp16_ap, ti_sb[:, :, 1], float(e), ALU.is_equal))
            seal(v.tensor_tensor(tmp16_ap, tmp16_ap, tw_sb[:, :, 1], ALU.mult))
            seal(v.tensor_tensor(gd_sb[:, :, e], gd_sb[:, :, e], tmp16_ap, ALU.add))
        seal(v.tensor_single_scalar(indall_ap, gd_sb[:], 0.0, ALU.is_gt))
        for r in range(NT):
            v.tensor_copy(grows_sb[:, r, 0:8], gd_sb[:, r, :])
        v.memset(grows_sb[:, :, 8:128], 0.0)
        v.memset(grows_sb[:, 16, :], 0.0)
        v.engine_nop().then_inc(s_rd, 1)
        for cc in range(16):
            if cc >= 2:
                v.wait_ge(s_rp, cc - 1)
            for e in range(E):
                v.tensor_scalar(m1_ap[cc % 2][:, e * 16:(e + 1) * 16],
                                LM_sb[:, e * 16:(e + 1) * 16],
                                indall_ap[:, cc, e:e + 1], None, ALU.mult).then_inc(s_rw, 1)
        v.wait_ge(s_rp, 16)
        seal(v.tensor_copy(cum_ap[0], psC[:, 0:128]))
        seal(v.tensor_copy(bsb_ap, psC[:, 0:128]))
        par = 0
        k = 1
        while k < 128:
            nxt = 1 - par
            seal(v.tensor_copy(cum_ap[nxt][:, 0:k], cum_ap[par][:, 0:k]))
            seal(v.tensor_tensor(cum_ap[nxt][:, k:128], cum_ap[par][:, k:128],
                                 cum_ap[par][:, 0:128 - k], ALU.add))
            par = nxt
            k *= 2
        seal(v.tensor_sub(cex_ap, cum_ap[par], bsb_ap))
        seal(v.tensor_single_scalar(idxf_ap, cex_ap, float(CCAP), ALU.is_lt))
        seal(v.tensor_tensor(idxf_ap, idxf_ap, bsb_ap, ALU.mult))
        seal(v.tensor_scalar(cex_ap, cex_ap, EC_sb[:, 0:1], None, ALU.add))
        seal(v.tensor_scalar_add(cex_ap, cex_ap, 1.0))
        seal(v.tensor_tensor(idxf_ap, idxf_ap, cex_ap, ALU.mult))
        seal(v.tensor_scalar_add(idxf_ap, idxf_ap, -1.0))
        seal(v.tensor_copy(idxi_ap, idxf_ap))
        v.engine_nop().then_inc(s_rt, 1)
        v.wait_ge(s_ls, 1)
        seal(v.tensor_copy(lsf_ap, lsd_ap))
        v.engine_nop().then_inc(s_ls, 1)
        v.wait_ge(s_rp, 17)
        seal(v.tensor_copy(c16_ap, psD[0:16, 0:NCOL]))
        v.engine_nop().then_inc(s_rt, 1)
        v.wait_ge(s_rp, 18)
        seal(v.tensor_scalar_add(lsf_ap, psE[:, 0:NCOL], -1.0))
        seal(v.tensor_copy(gidx_sb[:], lsf_ap))
        seal(v.tensor_single_scalar(mski_ap, gidx_sb[:], 0, ALU.is_lt))
        seal(v.scalar_tensor_tensor(gidx_sb[:], mski_ap, 2049.0, gidx_sb[:], ALU.mult, ALU.add))
        seal(v.tensor_scalar_min(mski_ap, gidx_sb[:], 2047))
        v.engine_nop().then_inc(s_rg, 1)
        if PH < 3:
            return
        def vshuf(e2):
            v.wait_ge(s_gth, (2 * e2 + 2) * 16)
            if e2 >= 1:
                v.wait_ge(s_gc, e2)
            mask = list(range(32))
            mask[0] = e2
            v.stream_shuffle(shuf_ap, ge_ap[0:32, 0, :], mask).then_inc(s_sh, 1)
        if KSUB >= 2 and KE >= 1:
            vshuf(0)
        for e in range(KE):
            if KSUB < 2:
                break
            v.wait_ge(s_gc, e + 1)
            seal(v.tensor_copy(gcol_sb[:, :], psF[:, 0:NT_E]))
            v.engine_nop().then_inc(s_gv, 1)
            if not has_gateup:
                if e + 1 < KE:
                    vshuf(e + 1)
                continue
            for half in range(NH):
                for ichl in range(16):
                    ich = 16 * half + ichl
                    gich = e * 32 + ich
                    v.wait_ge(s_si, gich + 1)
                    v.wait_ge(s_ub, gich + 1)
                    v.tensor_tensor(h_ap[:, ichl, :], silu_ap, psB[:, 0:REGS],
                                    ALU.mult).then_inc(s_hm, 1)
                if half == NH - 1 and e + 1 < KE:
                    vshuf(e + 1)
                kd2 = e * NH + half
                if not has_ye:
                    continue
                for st in range(NT_E):
                    idxg = kd2 * NT_E + st
                    v.wait_ge(s_dn, 2 * (idxg + 1))
                    if st == 0 and kd2 >= 1 and has_scat:
                        v.wait_ge(s_sct, 16 * kd2)
                    pA, pB = (psC, psD) if idxg % 2 == 0 else (psE, psF)
                    v.tensor_scalar(ye_sb[:, st, 0:512], pA[:, :],
                                    gcol_sb[:, st:st + 1], None, ALU.mult).then_inc(s_ya, 1)
                    v.tensor_scalar(ye_sb[:, st, 512:1024], pB[:, :],
                                    gcol_sb[:, st:st + 1], None, ALU.mult).then_inc(s_ya, 1)

    @block.scalar
    def _(sc):
        sc.wait_ge(s_gx, 16)
        sc.wait_ge(s_vp, NPP_PRE)
        sc.activation(sq_view, x_sb[:].rearrange("p a b -> p (a b)"), AF.Square).then_inc(s_sq, 1)
        sc.wait_ge(s_rt, 3)
        sc.activation(stat_sb[:, :, 4:5], stat_sb[:, :, 3:4], AF.Sqrt).then_inc(s_as, 1)
        for i0, (h, qb, kt, dg) in enumerate(attn_seq):
            i = i0 + 1
            sc.wait_ge(s_sc, i)
            if i >= 7:
                sc.wait_ge(s_av, i - 6)
            sc.activation(pt_ap[(i - 1) % 6], PS4[(i - 1) % 4], AF.Exp).then_inc(s_ex, 1)
        if PH < 2:
            return
        sc.wait_ge(s_ox, 32)
        sc.wait_ge(s_on, N_HQ)
        sc.activation(sq2_view, x_sb[:].rearrange("p a b -> p (a b)"), AF.Square).then_inc(s_sq, 1)
        sc.wait_ge(s_rt, 4)
        sc.activation(stat_sb[:, :, 4:5], stat_sb[:, :, 3:4], AF.Sqrt).then_inc(s_as, 1)
        sc.wait_ge(s_rt, 5)
        sc.activation(Eall_ap, maxv_ap, AF.Exp).then_inc(s_as, 1)
        if PH < 3:
            return
        for e in range(KE):
            if not has_gateup:
                break
            for ich in range(32):
                gich = e * 32 + ich
                sc.wait_ge(s_ga, gich + 1)
                if gich >= 1:
                    sc.wait_ge(s_hm, gich)
                sc.activation(silu_ap, psA[:, 0:REGS], AF.Silu).then_inc(s_si, 1)

    es.close()
    lower_extended_insts(nc)
    return nc


_NC_CACHE = [None]


def kernel(L0_attn_hidden_states, L0_attn_basis_w, L0_attn_core, L0_attn_head_residual,
           L0_attn_v_w, L0_attn_v_b, L0_attn_o_w, L0_attn_o_b,
           L0_ffn_router_gate, L0_ffn_gate_w, L0_ffn_up_w, L0_ffn_down_w):
    x = np.ascontiguousarray(np.asarray(L0_attn_hidden_states, dtype=np.float32))
    if _NC_CACHE[0] is None:
        _NC_CACHE[0] = build_nc()
    nc = _NC_CACHE[0]
    w = dict(
        basis_w=np.ascontiguousarray(np.asarray(L0_attn_basis_w, dtype=np.float32)),
        core=np.ascontiguousarray(np.asarray(L0_attn_core, dtype=np.float32)),
        head_residual=np.ascontiguousarray(np.asarray(L0_attn_head_residual, dtype=np.float32)),
        v_w=np.ascontiguousarray(np.asarray(L0_attn_v_w, dtype=np.float32)),
        o_w=np.ascontiguousarray(np.asarray(L0_attn_o_w, dtype=np.float32)),
        router_gate=np.ascontiguousarray(np.asarray(L0_ffn_router_gate, dtype=np.float32)),
        gate_w=np.ascontiguousarray(np.asarray(L0_ffn_gate_w, dtype=np.float32)
                                    .reshape(8, 8, 128, 16, 256).transpose(0, 3, 2, 1, 4)),
        up_w=np.ascontiguousarray(np.asarray(L0_ffn_up_w, dtype=np.float32)
                                  .reshape(8, 8, 128, 16, 256).transpose(0, 3, 2, 1, 4)),
        down_w=np.ascontiguousarray(np.asarray(L0_ffn_down_w, dtype=np.float32)
                                    .reshape(8, 2, 16, 128, 1024).transpose(0, 1, 3, 2, 4)),
    )
    in_maps = [dict(w, x=np.ascontiguousarray(x[b])) for b in range(8)]
    res = run_bass_kernel_spmd(nc, in_maps, core_ids=list(range(8)))
    out = np.stack([np.asarray(r["out"])[:T] for r in res.results], axis=0)
    return out.astype(np.float32)


# version 13
# speedup vs baseline: 1.0896x; 1.0138x over previous
import os as _os
import numpy as np
import ml_dtypes
import concourse.bass as bass
import concourse.mybir as mybir
from concourse.bass_utils import run_bass_kernel_spmd
from concourse import library_config as _libs
from concourse.library_overlay import lower_extended_insts

F32 = mybir.dt.float32
BF16 = mybir.dt.bfloat16
I16 = mybir.dt.int16
U32 = mybir.dt.uint32
AF = mybir.ActivationFunctionType
ALU = mybir.AluOpType
AX = mybir.AxisListType

T = 2048
NT = 16
C = 1024
CCH = 8
H = 16
R = 64
E = 8
IFF = 4096
CCAP = 56
REGS = CCAP * 16
NT_E = REGS // 128
NCOL = E * CCAP
QB = 512
NQB = T // QB
EPS = 1e-5
NG = 16
GI = 2
NH = 2


def _consts():
    p = np.arange(128)
    idn = np.eye(128, dtype=np.float32)
    MB = (np.arange(896)[None, :] >= p[:, None] + 384).astype(np.float32)
    LM = np.zeros((128, 128), np.float32)
    for e in range(E):
        for l in range(16):
            LM[:, e * 16 + l] = (p % 16 == l)
    JB = np.zeros((128, 248), np.float32)
    JB[p, (p // 16) + 120] = 1.0
    EC = (CCAP * (p // 16)).astype(np.float32).reshape(128, 1)
    DC = (16 * np.arange(128)[None, :] + (p % 16)[:, None] + 1).astype(np.int16)
    S16 = np.zeros((128, 16), np.float32)
    S16[p, p % 16] = 1.0
    S2 = np.zeros((16, 128), np.float32)
    for l in range(16):
        for g in range(8):
            S2[l, g * 16 + l] = 1.0
    return idn, MB, LM, JB, EC, DC, S16, S2


def build_nc():
    PH = int(_os.environ.get('KPH', '9'))
    KE = int(_os.environ.get('KE', '8'))
    KSUB = int(_os.environ.get('KSUB', '9'))
    has_ld = KSUB in (3, 4, 45, 5, 6, 9)
    has_gateup = KSUB in (4, 45, 5, 6, 9)
    has_down = KSUB in (45, 5, 6, 9)
    has_ye = KSUB in (5, 6, 9)
    has_scat = KSUB in (6, 9)
    from contextlib import ExitStack
    nc = bass.Bass("TRN2", target_bir_lowering=False, detect_race_conditions=False)
    dr = lambda n, s: nc.dram_tensor(n, s, F32, kind="ExternalInput").ap()
    x_h = dr("x", [T, C])
    bw_h = dr("basis_w", [R, C])
    core_h = dr("core", [R, R])
    hr_h = dr("head_residual", [H, R, R])
    vw_h = dr("v_w", [C, C])
    ow_h = dr("o_w", [C, C])
    rg_h = dr("router_gate", [C, E])
    gw_h = dr("gate_w", [E, NG, 128, CCH, 256])
    uw_h = dr("up_w", [E, NG, 128, CCH, 256])
    dw_h = dr("down_w", [E, NH, 128, 16, C])
    gwb_h = nc.dram_tensor("gwb", [E, NG, 128, CCH, 256], BF16, kind="Internal").ap()
    uwb_h = nc.dram_tensor("uwb", [E, NG, 128, CCH, 256], BF16, kind="Internal").ap()
    dwb_h = nc.dram_tensor("dwb", [E, NH, 128, 16, C], BF16, kind="Internal").ap()
    out_h = nc.dram_tensor("out", [T + 1, C], F32, kind="ExternalOutput").ap()

    idn_c, MB_c, LM_c, JB_c, EC_c, DC_c, S16_c, S2_c = _consts()
    il = lambda a, n: nc.inline_tensor(np.ascontiguousarray(a), n).ap()
    idn_h = il(idn_c, "c_idn")
    MB_h = il(MB_c, "c_mb")
    LM_h = il(LM_c, "c_lm")
    JB_h = il(JB_c, "c_jb")
    EC_h = il(EC_c, "c_ec")
    DC_h = il(DC_c, "c_dc")
    S16_h = il(S16_c, "c_s16")
    S2_h = il(S2_c, "c_s2")

    es = ExitStack()
    sbA = lambda n, s, dt=F32: es.enter_context(nc.sbuf_tensor(n, s, dt))
    idn_sb = sbA("idn", [128, 128])
    MB_sb = sbA("mbk", [128, 896], BF16)
    LM_sb = sbA("lmk", [128, 128], BF16)
    JB_sb = sbA("jbk", [128, 248], BF16)
    EC_sb = sbA("eck", [128, 1])
    DC_sb = sbA("dck", [128, 128], I16)
    S16_sb = sbA("s16k", [128, 16])
    S2_sb = sbA("s2k", [16, 128])
    bwT_sb = sbA("bwT", [128, CCH, R], BF16)
    M_sb = sbA("msbk", [R, H, R], BF16)
    rg_sb = sbA("rgk", [128, CCH, E], BF16)
    ones_sb = sbA("onesk", [1, 64])
    onesb_sb = sbA("onesbk", [128, 1], BF16)
    stat_sb = sbA("stat", [128, NT, 6])
    x_sb = sbA("xk", [128, NT, C], BF16)
    rows_sb = sbA("rows", [128, 16, 1024], BF16)
    grows_sb = sbA("grows", [128, 17, 128], BF16)
    logit_sb = sbA("logit", [128, NT, E])
    tw_sb = sbA("twk", [128, NT, 2])
    ti_sb = sbA("tik", [128, NT, 2])
    gd_sb = sbA("gdk", [128, NT, E])
    gidx_sb = sbA("gidxk", [128, NCOL], I16)
    gcol_sb = sbA("gcolk", [128, NT_E])
    wstage_sb = sbA("wstage", [128, CCH, 1024])
    V_sb = sbA("vsbk", [128, NT, H, 65], BF16)
    fm_sb = sbA("fmk", [128, CCH, T], BF16)

    fm_flat = fm_sb[:].rearrange("p a b -> p (a b)")
    x_flat = x_sb[:].rearrange("p a b -> p (a b)")
    rows_flat = rows_sb[:].rearrange("p a b -> p (a b)")
    V_flat = V_sb[:].rearrange("p a b c -> p (a b c)")
    ws_flat = wstage_sb[:].rearrange("p a b -> p (a b)")

    _off = [0]
    def fmr(n_bf16):
        o = _off[0]
        _off[0] += n_bf16
        assert _off[0] <= 16384
        return fm_flat[:, o:o + n_bf16]
    maxv_ap = fmr(2 * NT * E).bitcast(F32).rearrange("p (a b) -> p a b", a=NT)
    Eall_ap = fmr(2 * NT * E).bitcast(F32).rearrange("p (a b) -> p a b", a=NT)
    idxu_ap = fmr(2 * NT * E).bitcast(U32).rearrange("p (a b) -> p a b", a=NT)
    zrow_ap = fmr(2 * NT).bitcast(F32).rearrange("p (a b) -> p a b", b=1)
    w12_ap = fmr(2 * NT * 4).bitcast(F32).rearrange("p (a b) -> p a b", a=NT)
    indall_ap = fmr(2 * NT * E).bitcast(F32).rearrange("p (a b) -> p a b", a=NT)
    tmp16_ap = fmr(2 * NT).bitcast(F32)
    m1_ap = [fmr(128) for _ in range(2)]
    fmst_ap = [fmr(128) for _ in range(2)]
    cum_ap = [fmr(256).bitcast(F32) for _ in range(2)]
    bsb_ap = fmr(256).bitcast(F32)
    cex_ap = fmr(256).bitcast(F32)
    idxf_ap = fmr(256).bitcast(F32)
    lsf_ap = fmr(2 * NCOL).bitcast(F32)
    c16_full = fmr(2 * NCOL).bitcast(F32)
    c16_ap = c16_full[0:16, :]
    idxi_ap = fmr(128).bitcast(I16)
    lsd_ap = fmr(NCOL).bitcast(I16)
    mski_ap = fmr(NCOL).bitcast(I16)
    sq2_view = fm_flat
    silu_ap = fm_flat[:, 10240:10240 + REGS]
    gw_ap = [fm_flat[:, 11264 + 2048 * i:11264 + 2048 * (i + 1)]
             .rearrange("p (a b) -> p a b", a=CCH) for i in range(2)]
    _moe0 = (_off[0] + 1) // 2 * 2
    silu_ap = fm_flat[:, _moe0:_moe0 + REGS]
    _g0 = _moe0 + REGS
    gw_ap = [fm_flat[:, _g0 + 2048 * i:_g0 + 2048 * (i + 1)]
             .rearrange("p (a b) -> p a b", a=CCH) for i in range(2)]
    _u0 = _g0 + 4096
    uw_ap = [fm_flat[:, _u0 + 2048 * i:_u0 + 2048 * (i + 1)]
             .rearrange("p (a b) -> p a b", a=CCH) for i in range(2)]
    assert _u0 + 4096 <= 16384, (_off[0], _moe0, _u0)
    xe_ap = [x_flat[:, 7168 * i:7168 * (i + 1)].rearrange("p (a b) -> p a b", a=CCH)
             for i in range(2)]
    ge_ap = x_flat[:, 14336:14336 + REGS].rearrange("p (a b) -> p a b", a=1)
    shuf_ap = x_flat[0:32, 15232:15232 + REGS]
    h_ap = V_flat[:, 0:16 * REGS].rearrange("p (a b) -> p a b", a=16)
    dw_ap = ws_flat.bitcast(BF16).rearrange("p (a b) -> p a b", a=16)
    sq_view = ws_flat.bitcast(BF16)
    ye_sb = sbA("yek", [128, NT_E, 1024])
    lnt_sb = [ye_sb[:, i, :] for i in range(2)]
    rec_ap = ye_sb[0:1, 2, 0:QB]
    hrV_ap = ye_sb[0:64, 2, :]
    coreV_ap = ye_sb[64:128, 2, 0:64]
    coreT_ap = ye_sb[64:128, 2, 64:128]
    wvo_ap = rows_flat[:, 4608:4608 + CCH * 1024].rearrange("p (a b) -> p a b", a=CCH)
    L_ap = rows_flat[0:64, 0:T]
    lt_ap = [rows_flat[0:64, T + QB * i:T + QB * (i + 1)] for i in range(2)]
    pt_ap = [rows_flat[:, T + 1024 + QB * i:T + 1024 + QB * (i + 1)] for i in range(6)]
    rb_ap = rows_flat.bitcast(F32)[0:64, 6400:6400 + QB]


    psA = es.enter_context(nc.psum_tensor("psA", [128, 1024], F32))
    psB = es.enter_context(nc.psum_tensor("psB", [128, 1024], F32))
    psC = es.enter_context(nc.psum_tensor("psC", [128, 512], F32))
    psD = es.enter_context(nc.psum_tensor("psD", [128, 512], F32))
    psE = es.enter_context(nc.psum_tensor("psE", [128, 512], F32))
    psF = es.enter_context(nc.psum_tensor("psF", [128, 512], F32))
    PS4 = [psA[:, 0:512], psA[:, 512:1024], psB[:, 0:512], psB[:, 512:1024]]
    AVB = [psE, psD]
    LTB = [psF, psC]

    sem = lambda n: es.enter_context(nc.semaphore(n))
    s_ld = sem("s_ld"); s_ow = sem("s_ow"); s_gx = sem("s_gx")
    s_mw = sem("s_mw"); s_md = sem("s_md"); s_gth = sem("s_gth")
    s_sct = sem("s_sct"); s_xo = sem("s_xo")
    s_pp = sem("s_pp"); s_vp = sem("s_vp"); s_vs = sem("s_vs"); s_as = sem("s_as")
    s_sq = sem("s_sq"); s_n1 = sem("s_n1"); s_t1 = sem("s_t1"); s_f1 = sem("s_f1")
    s_vl = sem("s_vl"); s_ve = sem("s_ve")
    s_sc = sem("s_sc"); s_ex = sem("s_ex"); s_mk = sem("s_mk"); s_av = sem("s_av")
    s_lt = sem("s_lt"); s_rc = sem("s_rc"); s_rb = sem("s_rb"); s_on = sem("s_on")
    s_op = sem("s_op"); s_ox = sem("s_ox")
    s_n2 = sem("s_n2"); s_t2 = sem("s_t2"); s_f2 = sem("s_f2")
    s_lg = sem("s_lg"); s_le = sem("s_le")
    s_rt = sem("s_rt"); s_rp = sem("s_rp"); s_rw = sem("s_rw"); s_ls = sem("s_ls")
    s_rg = sem("s_rg"); s_rd = sem("s_rd")
    s_ga = sem("s_ga"); s_ub = sem("s_ub"); s_si = sem("s_si"); s_hm = sem("s_hm")
    s_dn = sem("s_dn"); s_ya = sem("s_ya"); s_gc = sem("s_gc"); s_gv = sem("s_gv")
    s_sh = sem("s_sh"); s_a2 = sem("s_a2"); s_pc = sem("s_pc")

    attn_seq = []
    hq_list = []
    for h in range(H):
        for qb in range(NQB):
            hq_list.append((h, qb))
            for kt in range(4 * qb + 4):
                attn_seq.append((h, qb, kt, kt >= 4 * qb))
    N_KT = len(attn_seq)
    N_HQ = len(hq_list)
    SCX = {}
    MKX = {}
    nmk = 0
    for i, (h, qb, kt, dg) in enumerate(attn_seq):
        SCX[(h, qb, kt)] = i + 1
        if dg:
            nmk += 1
        MKX[(h, qb, kt)] = nmk
    LDN_CONSTS = 8
    LDN = 13
    NPP_PRE = 8 + 1 + 64
    NPP_OW = 64
    NVE_ALL = 32 + NQB

    block = es.enter_context(nc.Block())

    @block.sync
    def _(s):
        s.dma_start(out=idn_sb[:], in_=idn_h).then_inc(s_ld, 16)
        s.dma_start(out=lnt_sb[0][:, 0:896], in_=MB_h).then_inc(s_ld, 16)
        s.dma_start(out=lnt_sb[1][:, 0:128], in_=LM_h).then_inc(s_ld, 16)
        s.dma_start(out=lnt_sb[1][:, 128:376], in_=JB_h).then_inc(s_ld, 16)
        s.dma_start(out=EC_sb[:], in_=EC_h).then_inc(s_ld, 16)
        s.dma_start(out=DC_sb[:], in_=DC_h).then_inc(s_ld, 16)
        s.dma_start(out=S16_sb[:], in_=S16_h).then_inc(s_ld, 16)
        s.dma_start(out=S2_sb[:], in_=S2_h).then_inc(s_ld, 16)
        s.wait_ge(s_rt, 1)
        s.dma_start(out=lnt_sb[0][0:64, :], in_=bw_h).then_inc(s_ld, 16)
        s.dma_start(out=coreV_ap, in_=core_h).then_inc(s_ld, 16)
        s.dma_start(out=hrV_ap.rearrange("p (a b) -> p a b", a=16), in_=hr_h.rearrange("h r s -> r h s")).then_inc(s_ld, 16)
        s.dma_start(out=wstage_sb[:], in_=vw_h.rearrange("(a b) c -> b a c", b=128)).then_inc(s_ld, 16)
        s.dma_start(out=lnt_sb[1][:, 384:448].rearrange("p (a b) -> p a b", a=8),
                    in_=rg_h.rearrange("(a b) c -> b a c", b=128)).then_inc(s_ld, 16)
        s.wait_ge(s_rt, 2)
        s.dma_start(out=wstage_sb[:], in_=ow_h.rearrange("(a b) c -> b a c", b=128)).then_inc(s_ow, 16)
        if PH >= 3 and has_ld:
            s.wait_ge(s_rt, 4)
            s.wait_ge(s_vp, NPP_PRE + NPP_OW)
            for e in range(KE):
                for grp in range(NG):
                    k = e * NG + grp
                    if grp == 0:
                        s.wait_ge(s_pc, (3 * e + 2) * 16)
                    if k >= 2 and has_gateup:
                        s.wait_ge(s_ub, GI * (k - 1))
                    s.dma_start(out=gw_ap[k % 2], in_=gwb_h[e, grp]).then_inc(s_mw, 16)
                    s.dma_start(out=uw_ap[k % 2], in_=uwb_h[e, grp]).then_inc(s_mw, 16)
                    if grp == NG // 2 - 1 or grp == NG - 1:
                        hf = 0 if grp == NG // 2 - 1 else 1
                        kd = e * NH + hf
                        if grp == NG // 2 - 1:
                            s.wait_ge(s_pc, (3 * e + 3) * 16)
                        if kd >= 1 and has_down:
                            s.wait_ge(s_dn, 14 * kd)
                        s.dma_start(out=dw_ap, in_=dwb_h[e, hf]).then_inc(s_md, 16)

    @block.gpsimd
    def _(g):
        g.load_library(_libs.local_scatter)
        g.dma_start(out=x_sb[:], in_=x_h.rearrange("(a b) c -> b a c", b=128)).then_inc(s_gx, 16)
        if PH >= 3 and has_ld:
            g.wait_ge(s_ow, 16)
            for e in range(KE):
                g.dma_start(out=gwb_h[e], in_=gw_h[e]).then_inc(s_pc, 16)
                g.dma_start(out=uwb_h[e], in_=uw_h[e]).then_inc(s_pc, 16)
                g.dma_start(out=dwb_h[e], in_=dw_h[e]).then_inc(s_pc, 16)
        for tt in range(NT):
            g.wait_ge(s_ox, 2 * (tt + 1))
            g.dma_start(out=out_h[128 * tt:128 * (tt + 1), :], in_=x_sb[:, tt, :]).then_inc(s_xo, 16)
        if PH < 2:
            g.wait_ge(s_xo, NT * 16)
            return
        g.wait_ge(s_rt, 6)
        g.local_scatter(lsd_ap, DC_sb[:], idxi_ap, 128, NCOL, 128).then_inc(s_ls, 1)
        g.load_library(_libs.mlp)
        g.wait_ge(s_rg, 1)
        g.wait_ge(s_rd, 1)
        if PH < 3:
            return
        g.wait_ge(s_xo, NT * 16)
        def scat(e, hf):
            kd = e * NH + hf
            g.wait_ge(s_ya, 14 * (kd + 1))
            g.wait_ge(s_sct, 16 * kd)
            if kd == 0:
                g.wait_ge(s_xo, NT * 16)
            g.dma_scatter_add(out_h, ye_sb[:], gidx_sb[:, e * CCAP:(e + 1) * CCAP],
                              REGS, REGS, C, elem_step=C).then_inc(s_sct, 16)
        for e in range(KE):
            epar = e % 2
            if e >= 2 and has_gateup:
                g.wait_ge(s_ub, GI * NG * (e - 1))
            g.dma_gather(xe_ap[epar], rows_flat, mski_ap[:, e * CCAP:(e + 1) * CCAP],
                         REGS, REGS, 1024, transpose=True,
                         sbuf_tokens_per_rank=128, sbuf_free_dim_per_rank=2048,
                         sbuf_free_dim_pad_per_rank=0, sbuf_byte_offset=0).then_inc(s_gth, 16)
            if e >= 1:
                g.wait_ge(s_sh, e)
            g.dma_gather(ge_ap, grows_sb[:].rearrange("p a b -> p (a b)"),
                         gidx_sb[:, e * CCAP:(e + 1) * CCAP],
                         REGS, REGS, 128, transpose=True,
                         sbuf_tokens_per_rank=128, sbuf_free_dim_per_rank=256,
                         sbuf_free_dim_pad_per_rank=0, sbuf_byte_offset=0).then_inc(s_gth, 16)
            if has_scat and e >= 1:
                scat(e - 1, 1)
            if has_scat and has_ye:
                scat(e, 0)
        if has_scat:
            scat(KE - 1, 1)
            g.wait_ge(s_sct, KE * NH * 16)
        else:
            g.wait_ge(s_gth, KE * 32)

    @block.tensor
    def _(tn):
        tn.wait_ge(s_ld, LDN * 16)
        npp = 0
        for j in range(CCH):
            if npp >= 2:
                tn.wait_ge(s_vp, npp - 1)
            ps = psC if npp % 2 == 0 else psD
            tn.transpose(ps[:, 0:64], lnt_sb[0][0:64, 128 * j:128 * (j + 1)], idn_sb[0:64, 0:64]).then_inc(s_pp, 1)
            npp += 1
        tn.wait_ge(s_vp, npp - 1)
        ps = psC if npp % 2 == 0 else psD
        tn.transpose(ps[0:64, 0:64], coreV_ap, idn_sb[64:128, 64:128]).then_inc(s_pp, 1)
        npp += 1
        for dch in range(CCH):
            for cchi in range(CCH):
                tn.wait_ge(s_vp, npp - 1)
                ps = psC if npp % 2 == 0 else psD
                tn.transpose(ps[:, 0:128], wstage_sb[:, dch, 128 * cchi:128 * (cchi + 1)],
                             idn_sb[:, :]).then_inc(s_pp, 1)
                npp += 1
        nt1 = 0
        tn.wait_ge(s_vp, NPP_PRE)
        for tt in range(NT):
            tn.wait_ge(s_n1, tt + 1)
            for j in range(CCH):
                if nt1 >= 4:
                    tn.wait_ge(s_f1, nt1 - 3)
                ps = PS4[nt1 % 4]
                tn.transpose(ps[:, 0:128], lnt_sb[tt % 2][:, 128 * j:128 * (j + 1)],
                             idn_sb[:, :]).then_inc(s_t1, 1)
                nt1 += 1
        nvl = 0
        tn.wait_ge(s_f1, 128)
        for tt in range(NT):
            for hf in range(2):
                if nvl >= 2:
                    tn.wait_ge(s_ve, nvl - 1)
                ps = psC if nvl % 2 == 0 else psD
                for cch in range(CCH):
                    mmi = tn.matmul(ps[:, :], fm_sb[:, cch, 128 * tt:128 * (tt + 1)],
                                    wvo_ap[:, cch, 512 * hf:512 * (hf + 1)],
                                    start=(cch == 0), stop=(cch == 7))
                mmi.then_inc(s_vl, 1)
                nvl += 1
        for qb in range(NQB):
            if qb >= 1:
                tn.wait_ge(s_ve, 32 + qb)
            for cch in range(CCH):
                mmi = tn.matmul(psE[0:64, :], bwT_sb[:, cch, :],
                                fm_sb[:, cch, QB * qb:QB * (qb + 1)],
                                start=(cch == 0), stop=(cch == 7))
            mmi.then_inc(s_vl, 1)
        tn.wait_ge(s_ve, NVE_ALL)
        def emit_recb(ih2):
            tn.wait_ge(s_rc, ih2 + 1)
            tn.matmul(LTB[ih2 % 2][0:64, :], ones_sb[0:1, 0:64], rec_ap,
                      start=True, stop=True).then_inc(s_rb, 1)
        for ih, (h, qb) in enumerate(hq_list):
            if ih >= 2:
                tn.wait_ge(s_on, ih - 1)
            mmi = tn.matmul(LTB[ih % 2][0:64, :], M_sb[:, h, :], L_ap[:, QB * qb:QB * (qb + 1)],
                            start=True, stop=True)
            mmi.then_inc(s_vl, 1)
            tn.wait_ge(s_lt, ih + 1)
            nkt = 4 * qb + 4
            def emit_S(kt, h=h, qb=qb, ih=ih):
                i = SCX[(h, qb, kt)]
                if i >= 5:
                    tn.wait_ge(s_ex, i - 4)
                tn.matmul(PS4[(i - 1) % 4], L_ap[:, 128 * kt:128 * (kt + 1)],
                          lt_ap[ih % 2], start=True, stop=True).then_inc(s_sc, 1)
            def emit_AV(kt, h=h, qb=qb, ih=ih, nkt=nkt):
                i = SCX[(h, qb, kt)]
                if kt >= 4 * qb:
                    tn.wait_ge(s_mk, MKX[(h, qb, kt)])
                else:
                    tn.wait_ge(s_ex, i)
                if kt == 0 and ih >= 2:
                    tn.wait_ge(s_on, ih - 1)
                tn.matmul(AVB[ih % 2][0:65, :], V_sb[:, kt, h, :], pt_ap[(i - 1) % 6],
                          start=(kt == 0), stop=(kt == nkt - 1)).then_inc(s_av, 1)
            LA = 3
            for kt in range(min(LA, nkt)):
                emit_S(kt)
            if ih >= 1 and qb == 0:
                emit_recb(ih - 1)
            for kt in range(nkt):
                if kt + LA < nkt:
                    emit_S(kt + LA)
                emit_AV(kt)
                if kt == 0 and ih >= 1 and qb >= 1:
                    emit_recb(ih - 1)
        emit_recb(N_HQ - 1)
        tn.wait_ge(s_ow, 16)
        tn.wait_ge(s_ex, N_KT)
        tn.wait_ge(s_on, N_HQ)
        for kk in range(NPP_OW):
            if kk >= 2:
                tn.wait_ge(s_vp, NPP_PRE + kk - 1)
            ps = psC if (NPP_PRE + kk) % 2 == 0 else psD
            cchi, dch = kk // 8, kk % 8
            tn.transpose(ps[:, 0:128], wstage_sb[:, cchi, 128 * dch:128 * (dch + 1)],
                         idn_sb[:, :]).then_inc(s_pp, 1)
        tn.wait_ge(s_on, N_HQ)
        tn.wait_ge(s_vp, NPP_PRE + NPP_OW)
        nop = 0
        for tt in range(NT):
            for hf in range(2):
                if nop >= 2:
                    tn.wait_ge(s_ox, nop - 1)
                ps = psC if nop % 2 == 0 else psD
                for dch in range(CCH):
                    mmi = tn.matmul(ps[:, :], fm_sb[:, dch, 128 * tt:128 * (tt + 1)],
                                    wvo_ap[:, dch, 512 * hf:512 * (hf + 1)],
                                    start=(dch == 0), stop=(dch == 7))
                mmi.then_inc(s_op, 1)
                nop += 1
        if PH < 2:
            return
        nt2 = 0
        for tt in range(NT):
            tn.wait_ge(s_n2, tt + 1)
            if tt >= 1:
                tn.wait_ge(s_le, tt)
            for j in range(CCH):
                if nt2 >= 4:
                    tn.wait_ge(s_f2, nt2 - 3)
                ps = PS4[nt2 % 4]
                tn.transpose(ps[:, 0:128], lnt_sb[tt % 2][:, 128 * j:128 * (j + 1)],
                             idn_sb[:, :]).then_inc(s_t2, 1)
                tn.wait_ge(s_f2, nt2 + 1)
                mmi = tn.matmul(psE[:, 0:E], fmst_ap[nt2 % 2], rg_sb[:, j, :],
                                start=(j == 0), stop=(j == 7))
                mmi.then_inc(s_lg, 1)
                nt2 += 1
        tn.wait_ge(s_f2, 128)
        for cc in range(16):
            tn.wait_ge(s_rw, (cc + 1) * 8)
            tn.matmul(psC[:, 0:128], m1_ap[cc % 2], JB_sb[:, 120 - 8 * cc:248 - 8 * cc],
                      start=(cc == 0), stop=(cc == 15)).then_inc(s_rp, 1)
        tn.wait_ge(s_ls, 2)
        tn.matmul(psD[0:16, 0:NCOL], S16_sb[:, :], lsf_ap, start=True, stop=True).then_inc(s_rp, 1)
        tn.wait_ge(s_rt, 7)
        tn.matmul(psE[:, 0:NCOL], S2_sb[:, :], c16_ap, start=True, stop=True).then_inc(s_rp, 1)
        if PH < 3:
            return
        for e in range(KE):
            if KSUB < 2:
                break
            epar = e % 2
            tn.wait_ge(s_sh, e + 1)
            if e >= 1:
                tn.wait_ge(s_gv, e)
            for st in range(NT_E):
                mmg = tn.matmul(psF[:, st:st + 1], shuf_ap[0:1, 128 * st:128 * (st + 1)],
                                onesb_sb[0:1, 0:1], start=True, stop=True)
            mmg.then_inc(s_gc, 1)
            if not has_gateup:
                continue
            tn.wait_ge(s_gth, (2 * e + 1) * 16)
            for half in range(NH):
                for grp in range(NG // NH * half, NG // NH * (half + 1)):
                    k = e * NG + grp
                    tn.wait_ge(s_mw, (k + 1) * 32)
                    for il_ in range(GI):
                        ich = GI * grp + il_
                        gich = e * 32 + ich
                        if gich >= 1:
                            tn.wait_ge(s_si, gich)
                        for cch in range(CCH):
                            for sp in range(2):
                                w_ = 512 if sp == 0 else 384
                                mmi = tn.matmul(psA[:, 512 * sp:512 * sp + w_],
                                                gw_ap[k % 2][:, cch, 128 * il_:128 * (il_ + 1)],
                                                xe_ap[epar][:, cch, 512 * sp:512 * sp + w_],
                                                start=(cch == 0), stop=(cch == 7))
                        mmi.then_inc(s_ga, 1)
                        if gich >= 1:
                            tn.wait_ge(s_hm, gich)
                        for cch in range(CCH):
                            for sp in range(2):
                                w_ = 512 if sp == 0 else 384
                                mmi = tn.matmul(psB[:, 512 * sp:512 * sp + w_],
                                                uw_ap[k % 2][:, cch, 128 * il_:128 * (il_ + 1)],
                                                xe_ap[epar][:, cch, 512 * sp:512 * sp + w_],
                                                start=(cch == 0), stop=(cch == 7))
                        mmi.then_inc(s_ub, 1)
                kd = e * NH + half
                if not has_down:
                    continue
                tn.wait_ge(s_md, (kd + 1) * 16)
                tn.wait_ge(s_hm, e * 32 + 16 * (half + 1))
                if half == 0:
                    tn.wait_ge(s_gv, e + 1)
                for st in range(NT_E):
                    prev = kd * NT_E + st
                    if prev >= 2 and has_ye:
                        tn.wait_ge(s_ya, 2 * (prev - 1))
                    pA, pB = (psC, psD) if prev % 2 == 0 else (psE, psF)
                    for hf2 in range(2):
                        ps = pA if hf2 == 0 else pB
                        for ichl in range(16):
                            mmi = tn.matmul(ps[:, :], h_ap[:, ichl, 128 * st:128 * (st + 1)],
                                            dw_ap[:, ichl, 512 * hf2:512 * (hf2 + 1)],
                                            start=(ichl == 0), stop=(ichl == 15))
                        if hf2 == 0 and half == NH - 1 and st == NT_E - 1:
                            mmi.then_inc(s_a2, 1)
                    mmi.then_inc(s_dn, 2)

    @block.vector
    def _(v):
        nvs = [0]
        def seal(ins):
            nvs[0] += 1
            ins.then_inc(s_vs, 1)
            v.wait_ge(s_vs, nvs[0])
        v.memset(ones_sb[:], 1.0)
        v.memset(onesb_sb[:], 1.0)
        v.wait_ge(s_ld, LDN_CONSTS * 16)
        v.tensor_copy(MB_sb[:], lnt_sb[0][:, 0:896])
        v.tensor_copy(LM_sb[:], lnt_sb[1][:, 0:128])
        seal(v.tensor_copy(JB_sb[:], lnt_sb[1][:, 128:376]))
        v.engine_nop().then_inc(s_rt, 1)
        v.wait_ge(s_ld, LDN * 16)
        seal(v.tensor_copy(rg_sb[:], lnt_sb[1][:, 384:448].rearrange("p (a b) -> p a b", a=8)))
        nvp = 0
        for j in range(CCH):
            v.wait_ge(s_pp, nvp + 1)
            ps = psC if nvp % 2 == 0 else psD
            v.tensor_copy(bwT_sb[:, j, :], ps[:, 0:64]).then_inc(s_vp, 1)
            nvp += 1
        v.wait_ge(s_pp, nvp + 1)
        ps = psC if nvp % 2 == 0 else psD
        seal(v.tensor_copy(coreT_ap, ps[0:64, 0:64]))
        v.engine_nop().then_inc(s_vp, 1)
        nvp += 1
        hrv = hrV_ap.rearrange("p (a b) -> p a b", a=16)
        seal(v.tensor_add(lnt_sb[0][0:64, 0:64], coreV_ap, coreT_ap))
        seal(v.tensor_scalar_mul(lnt_sb[0][0:64, 0:64], lnt_sb[0][0:64, 0:64], 0.5 / (16.0 * 8.0)))
        seal(v.reduce_sum(lnt_sb[0][0:64, 64:128].rearrange("p (a b) -> p a b", b=1),
                          hrv.rearrange("p a b -> p b a"), axis=AX.X))
        seal(v.tensor_scalar_mul(lnt_sb[0][0:64, 64:128], lnt_sb[0][0:64, 64:128], 1.0 / 16.0))
        for hh in range(H):
            seal(v.tensor_sub(lnt_sb[0][0:64, 128:192], hrv[:, hh, :], lnt_sb[0][0:64, 64:128]))
            seal(v.scalar_tensor_tensor(M_sb[:, hh, :], lnt_sb[0][0:64, 128:192], 1.0 / 8.0,
                                        lnt_sb[0][0:64, 0:64], ALU.mult, ALU.add))
        for kk in range(64):
            v.wait_ge(s_pp, 9 + kk + 1)
            ps = psC if (9 + kk) % 2 == 0 else psD
            dch, cchi = kk // 8, kk % 8
            v.tensor_copy(wvo_ap[:, cchi, 128 * dch:128 * (dch + 1)], ps[:, 0:128]).then_inc(s_vp, 1)
        v.memset(V_sb[:, :, :, 64:65], 1.0)
        v.wait_ge(s_gx, 16)
        v.wait_ge(s_sq, 1)
        seal(v.reduce_sum(stat_sb[:, :, 0:1], x_sb[:], axis=AX.X))
        seal(v.reduce_sum(stat_sb[:, :, 1:2],
                          sq_view.rearrange("p (a b) -> p a b", a=NT), axis=AX.X))
        v.engine_nop().then_inc(s_rt, 1)
        seal(v.tensor_scalar_mul(stat_sb[:, :, 0:1], stat_sb[:, :, 0:1], 1.0 / float(C)))
        seal(v.tensor_scalar_mul(stat_sb[:, :, 1:2], stat_sb[:, :, 1:2], 1.0 / float(C)))
        seal(v.tensor_tensor(stat_sb[:, :, 2:3], stat_sb[:, :, 0:1], stat_sb[:, :, 0:1], ALU.mult))
        seal(v.tensor_sub(stat_sb[:, :, 2:3], stat_sb[:, :, 1:2], stat_sb[:, :, 2:3]))
        seal(v.tensor_scalar_add(stat_sb[:, :, 2:3], stat_sb[:, :, 2:3], EPS))
        seal(v.reciprocal(stat_sb[:, :, 3:4], stat_sb[:, :, 2:3]))
        v.engine_nop().then_inc(s_rt, 1)
        v.wait_ge(s_as, 1)
        nf1 = 0
        for tt in range(NT):
            if tt >= 2:
                v.wait_ge(s_t1, 8 * (tt - 1))
            seal(v.tensor_scalar(lnt_sb[tt % 2], x_sb[:, tt, :],
                                 stat_sb[:, tt, 0:1], stat_sb[:, tt, 4:5],
                                 ALU.subtract, ALU.mult))
            v.engine_nop().then_inc(s_n1, 1)
            for j in range(CCH):
                v.wait_ge(s_t1, nf1 + 1)
                ps = PS4[nf1 % 4]
                v.tensor_copy(fm_sb[:, j, 128 * tt:128 * (tt + 1)], ps[:, 0:128]).then_inc(s_f1, 1)
                nf1 += 1
        nve = 0
        for tt in range(NT):
            for hf in range(2):
                v.wait_ge(s_vl, nve + 1)
                ps = psC if nve % 2 == 0 else psD
                v.tensor_copy(V_sb[:, tt, 8 * hf:8 * (hf + 1), 0:64],
                              ps[:, :].rearrange("p (a b) -> p a b", b=64)).then_inc(s_ve, 1)
                nve += 1
        for qb in range(NQB):
            v.wait_ge(s_vl, nve + 1)
            v.tensor_copy(L_ap[:, QB * qb:QB * (qb + 1)], psE[0:64, :]).then_inc(s_ve, 1)
            nve += 1
        def vchain(j):
            h2, qb2 = hq_list[j]
            v.wait_ge(s_av, SCX[(h2, qb2, 4 * qb2 + 3)])
            seal(v.tensor_copy(rec_ap, AVB[j % 2][64:65, :]))
            v.engine_nop().then_inc(s_rc, 1)
            v.wait_ge(s_rb, j + 1)
            hcol, hpar = h2 // 2, h2 % 2
            seal(v.reciprocal(rb_ap, LTB[j % 2][0:64, :]))
            v.tensor_tensor(fm_sb[64 * hpar:64 * hpar + 64, hcol, QB * qb2:QB * (qb2 + 1)],
                            AVB[j % 2][0:64, :], rb_ap, ALU.mult).then_inc(s_on, 1)
        for ih, (h, qb) in enumerate(hq_list):
            v.wait_ge(s_vl, NVE_ALL + ih + 1)
            v.tensor_copy(lt_ap[ih % 2], LTB[ih % 2][0:64, :]).then_inc(s_lt, 1)
            if ih >= 1:
                vchain(ih - 1)
            for kt in range(4 * qb, 4 * qb + 4):
                i = SCX[(h, qb, kt)]
                d = kt - 4 * qb
                v.wait_ge(s_ex, i)
                v.tensor_tensor(pt_ap[(i - 1) % 6], pt_ap[(i - 1) % 6],
                                MB_sb[:, 384 - 128 * d:384 - 128 * d + 512],
                                ALU.mult).then_inc(s_mk, 1)
        vchain(N_HQ - 1)
        for kk in range(64):
            v.wait_ge(s_pp, NPP_PRE + kk + 1)
            ps = psC if (NPP_PRE + kk) % 2 == 0 else psD
            cchi, dch = kk // 8, kk % 8
            v.tensor_copy(wvo_ap[:, dch, 128 * cchi:128 * (cchi + 1)], ps[:, 0:128]).then_inc(s_vp, 1)
        nox = 0
        for tt in range(NT):
            for hf in range(2):
                v.wait_ge(s_op, nox + 1)
                ps = psC if nox % 2 == 0 else psD
                v.tensor_tensor(x_sb[:, tt, 512 * hf:512 * (hf + 1)], ps[:, :],
                                x_sb[:, tt, 512 * hf:512 * (hf + 1)], ALU.add).then_inc(s_ox, 1)
                nox += 1
        if PH < 2:
            return
        v.wait_ge(s_sq, 2)
        seal(v.reduce_sum(stat_sb[:, :, 0:1], x_sb[:], axis=AX.X))
        seal(v.reduce_sum(stat_sb[:, :, 1:2],
                          sq2_view.rearrange("p (a b) -> p a b", a=NT), axis=AX.X))
        seal(v.tensor_scalar_mul(stat_sb[:, :, 0:1], stat_sb[:, :, 0:1], 1.0 / float(C)))
        seal(v.tensor_scalar_mul(stat_sb[:, :, 1:2], stat_sb[:, :, 1:2], 1.0 / float(C)))
        seal(v.tensor_tensor(stat_sb[:, :, 2:3], stat_sb[:, :, 0:1], stat_sb[:, :, 0:1], ALU.mult))
        seal(v.tensor_sub(stat_sb[:, :, 2:3], stat_sb[:, :, 1:2], stat_sb[:, :, 2:3]))
        seal(v.tensor_scalar_add(stat_sb[:, :, 2:3], stat_sb[:, :, 2:3], EPS))
        seal(v.reciprocal(stat_sb[:, :, 3:4], stat_sb[:, :, 2:3]))
        v.engine_nop().then_inc(s_rt, 1)
        v.wait_ge(s_as, 2)
        nf2 = 0
        for tt in range(NT):
            if tt >= 2:
                v.wait_ge(s_t2, 8 * (tt - 1))
            seal(v.tensor_scalar(lnt_sb[tt % 2], x_sb[:, tt, :],
                                 stat_sb[:, tt, 0:1], stat_sb[:, tt, 4:5],
                                 ALU.subtract, ALU.mult))
            v.tensor_copy(rows_sb[:, tt, :], lnt_sb[tt % 2])
            v.engine_nop().then_inc(s_n2, 1)
            for j in range(CCH):
                v.wait_ge(s_t2, nf2 + 1)
                if nf2 >= 2:
                    v.wait_ge(s_lg, nf2 - 1)
                ps = PS4[nf2 % 4]
                v.tensor_copy(fmst_ap[nf2 % 2], ps[:, 0:128]).then_inc(s_f2, 1)
                nf2 += 1
            v.wait_ge(s_lg, 8 * (tt + 1))
            v.tensor_copy(logit_sb[:, tt, :], psE[:, 0:E]).then_inc(s_le, 1)
        for tt in range(NT):
            seal(v.max(out=maxv_ap[:, tt, :], in_=logit_sb[:, tt, :]))
            seal(v.max_index(idxu_ap[:, tt, :], maxv_ap[:, tt, :], logit_sb[:, tt, :]))
        seal(v.tensor_copy(ti_sb[:], idxu_ap[:, :, 0:2]))
        v.engine_nop().then_inc(s_rt, 1)
        v.wait_ge(s_as, 3)
        seal(v.reduce_sum(zrow_ap, Eall_ap, axis=AX.X))
        seal(v.tensor_tensor(w12_ap[:, :, 2:3], Eall_ap[:, :, 0:1], Eall_ap[:, :, 1:2], ALU.add))
        seal(v.scalar_tensor_tensor(w12_ap[:, :, 2:3], zrow_ap, 1e-9,
                                    w12_ap[:, :, 2:3], ALU.mult, ALU.add))
        seal(v.reciprocal(w12_ap[:, :, 3:4], w12_ap[:, :, 2:3]))
        seal(v.tensor_tensor(tw_sb[:, :, 0:1], Eall_ap[:, :, 0:1], w12_ap[:, :, 3:4], ALU.mult))
        seal(v.tensor_tensor(tw_sb[:, :, 1:2], Eall_ap[:, :, 1:2], w12_ap[:, :, 3:4], ALU.mult))
        for e in range(E):
            seal(v.tensor_single_scalar(gd_sb[:, :, e], ti_sb[:, :, 0], float(e), ALU.is_equal))
            seal(v.tensor_tensor(gd_sb[:, :, e], gd_sb[:, :, e], tw_sb[:, :, 0], ALU.mult))
            seal(v.tensor_single_scalar(tmp16_ap, ti_sb[:, :, 1], float(e), ALU.is_equal))
            seal(v.tensor_tensor(tmp16_ap, tmp16_ap, tw_sb[:, :, 1], ALU.mult))
            seal(v.tensor_tensor(gd_sb[:, :, e], gd_sb[:, :, e], tmp16_ap, ALU.add))
        seal(v.tensor_single_scalar(indall_ap, gd_sb[:], 0.0, ALU.is_gt))
        for r in range(NT):
            v.tensor_copy(grows_sb[:, r, 0:8], gd_sb[:, r, :])
        v.memset(grows_sb[:, :, 8:128], 0.0)
        v.memset(grows_sb[:, 16, :], 0.0)
        v.engine_nop().then_inc(s_rd, 1)
        for cc in range(16):
            if cc >= 2:
                v.wait_ge(s_rp, cc - 1)
            for e in range(E):
                v.tensor_scalar(m1_ap[cc % 2][:, e * 16:(e + 1) * 16],
                                LM_sb[:, e * 16:(e + 1) * 16],
                                indall_ap[:, cc, e:e + 1], None, ALU.mult).then_inc(s_rw, 1)
        v.wait_ge(s_rp, 16)
        seal(v.tensor_copy(cum_ap[0], psC[:, 0:128]))
        seal(v.tensor_copy(bsb_ap, psC[:, 0:128]))
        par = 0
        k = 1
        while k < 128:
            nxt = 1 - par
            seal(v.tensor_copy(cum_ap[nxt][:, 0:k], cum_ap[par][:, 0:k]))
            seal(v.tensor_tensor(cum_ap[nxt][:, k:128], cum_ap[par][:, k:128],
                                 cum_ap[par][:, 0:128 - k], ALU.add))
            par = nxt
            k *= 2
        seal(v.tensor_sub(cex_ap, cum_ap[par], bsb_ap))
        seal(v.tensor_single_scalar(idxf_ap, cex_ap, float(CCAP), ALU.is_lt))
        seal(v.tensor_tensor(idxf_ap, idxf_ap, bsb_ap, ALU.mult))
        seal(v.tensor_scalar(cex_ap, cex_ap, EC_sb[:, 0:1], None, ALU.add))
        seal(v.tensor_scalar_add(cex_ap, cex_ap, 1.0))
        seal(v.tensor_tensor(idxf_ap, idxf_ap, cex_ap, ALU.mult))
        seal(v.tensor_scalar_add(idxf_ap, idxf_ap, -1.0))
        seal(v.tensor_copy(idxi_ap, idxf_ap))
        v.engine_nop().then_inc(s_rt, 1)
        v.wait_ge(s_ls, 1)
        seal(v.tensor_copy(lsf_ap, lsd_ap))
        v.engine_nop().then_inc(s_ls, 1)
        v.wait_ge(s_rp, 17)
        seal(v.tensor_copy(c16_ap, psD[0:16, 0:NCOL]))
        v.engine_nop().then_inc(s_rt, 1)
        v.wait_ge(s_rp, 18)
        seal(v.tensor_scalar_add(lsf_ap, psE[:, 0:NCOL], -1.0))
        seal(v.tensor_copy(gidx_sb[:], lsf_ap))
        seal(v.tensor_single_scalar(mski_ap, gidx_sb[:], 0, ALU.is_lt))
        seal(v.scalar_tensor_tensor(gidx_sb[:], mski_ap, 2049.0, gidx_sb[:], ALU.mult, ALU.add))
        seal(v.tensor_scalar_min(mski_ap, gidx_sb[:], 2047))
        v.engine_nop().then_inc(s_rg, 1)
        if PH < 3:
            return
        for e in range(KE):
            if KSUB < 2:
                break
            v.wait_ge(s_gth, (2 * e + 2) * 16)
            if e >= 1:
                v.wait_ge(s_gc, e)
            mask = list(range(32))
            mask[0] = e
            v.stream_shuffle(shuf_ap, ge_ap[0:32, 0, :], mask).then_inc(s_sh, 1)
            v.wait_ge(s_gc, e + 1)
            seal(v.tensor_copy(gcol_sb[:, :], psF[:, 0:NT_E]))
            v.engine_nop().then_inc(s_gv, 1)
            if not has_gateup:
                continue
            for half in range(NH):
                for ichl in range(16):
                    ich = 16 * half + ichl
                    gich = e * 32 + ich
                    v.wait_ge(s_si, gich + 1)
                    v.wait_ge(s_ub, gich + 1)
                    v.tensor_tensor(h_ap[:, ichl, :], silu_ap, psB[:, 0:REGS],
                                    ALU.mult).then_inc(s_hm, 1)
                kd2 = e * NH + half
                if not has_ye:
                    continue
                for st in range(NT_E):
                    idxg = kd2 * NT_E + st
                    v.wait_ge(s_dn, 2 * (idxg + 1))
                    if st == 0 and kd2 >= 1 and has_scat:
                        v.wait_ge(s_sct, 16 * kd2)
                    pA, pB = (psC, psD) if idxg % 2 == 0 else (psE, psF)
                    v.tensor_scalar(ye_sb[:, st, 0:512], pA[:, :],
                                    gcol_sb[:, st:st + 1], None, ALU.mult).then_inc(s_ya, 1)
                    v.tensor_scalar(ye_sb[:, st, 512:1024], pB[:, :],
                                    gcol_sb[:, st:st + 1], None, ALU.mult).then_inc(s_ya, 1)

    @block.scalar
    def _(sc):
        sc.wait_ge(s_gx, 16)
        sc.wait_ge(s_vp, NPP_PRE)
        sc.activation(sq_view, x_sb[:].rearrange("p a b -> p (a b)"), AF.Square).then_inc(s_sq, 1)
        sc.wait_ge(s_rt, 3)
        sc.activation(stat_sb[:, :, 4:5], stat_sb[:, :, 3:4], AF.Sqrt).then_inc(s_as, 1)
        for i0, (h, qb, kt, dg) in enumerate(attn_seq):
            i = i0 + 1
            sc.wait_ge(s_sc, i)
            if i >= 7:
                sc.wait_ge(s_av, i - 6)
            sc.activation(pt_ap[(i - 1) % 6], PS4[(i - 1) % 4], AF.Exp).then_inc(s_ex, 1)
        if PH < 2:
            return
        sc.wait_ge(s_ox, 32)
        sc.wait_ge(s_on, N_HQ)
        sc.activation(sq2_view, x_sb[:].rearrange("p a b -> p (a b)"), AF.Square).then_inc(s_sq, 1)
        sc.wait_ge(s_rt, 4)
        sc.activation(stat_sb[:, :, 4:5], stat_sb[:, :, 3:4], AF.Sqrt).then_inc(s_as, 1)
        sc.wait_ge(s_rt, 5)
        sc.activation(Eall_ap, maxv_ap, AF.Exp).then_inc(s_as, 1)
        if PH < 3:
            return
        for e in range(KE):
            if not has_gateup:
                break
            for ich in range(32):
                gich = e * 32 + ich
                sc.wait_ge(s_ga, gich + 1)
                if gich >= 1:
                    sc.wait_ge(s_hm, gich)
                sc.activation(silu_ap, psA[:, 0:REGS], AF.Silu).then_inc(s_si, 1)

    es.close()
    lower_extended_insts(nc)
    return nc


_NC_CACHE = [None]


def kernel(L0_attn_hidden_states, L0_attn_basis_w, L0_attn_core, L0_attn_head_residual,
           L0_attn_v_w, L0_attn_v_b, L0_attn_o_w, L0_attn_o_b,
           L0_ffn_router_gate, L0_ffn_gate_w, L0_ffn_up_w, L0_ffn_down_w):
    x = np.ascontiguousarray(np.asarray(L0_attn_hidden_states, dtype=np.float32))
    if _NC_CACHE[0] is None:
        _NC_CACHE[0] = build_nc()
    nc = _NC_CACHE[0]
    w = dict(
        basis_w=np.ascontiguousarray(np.asarray(L0_attn_basis_w, dtype=np.float32)),
        core=np.ascontiguousarray(np.asarray(L0_attn_core, dtype=np.float32)),
        head_residual=np.ascontiguousarray(np.asarray(L0_attn_head_residual, dtype=np.float32)),
        v_w=np.ascontiguousarray(np.asarray(L0_attn_v_w, dtype=np.float32)),
        o_w=np.ascontiguousarray(np.asarray(L0_attn_o_w, dtype=np.float32)),
        router_gate=np.ascontiguousarray(np.asarray(L0_ffn_router_gate, dtype=np.float32)),
        gate_w=np.ascontiguousarray(np.asarray(L0_ffn_gate_w, dtype=np.float32)
                                    .reshape(8, 8, 128, 16, 256).transpose(0, 3, 2, 1, 4)),
        up_w=np.ascontiguousarray(np.asarray(L0_ffn_up_w, dtype=np.float32)
                                  .reshape(8, 8, 128, 16, 256).transpose(0, 3, 2, 1, 4)),
        down_w=np.ascontiguousarray(np.asarray(L0_ffn_down_w, dtype=np.float32)
                                    .reshape(8, 2, 16, 128, 1024).transpose(0, 1, 3, 2, 4)),
    )
    in_maps = [dict(w, x=np.ascontiguousarray(x[b])) for b in range(8)]
    res = run_bass_kernel_spmd(nc, in_maps, core_ids=list(range(8)))
    out = np.stack([np.asarray(r["out"])[:T] for r in res.results], axis=0)
    return out.astype(np.float32)
